# version 31
# baseline (speedup 1.0000x reference)
import contextlib
import numpy as np
import ml_dtypes
import concourse.bass as bass
import concourse.mybir as mybir
from concourse.bass_core_rust import IndirectOffsetOnAxis
from concourse.bass_utils import run_bass_kernel_spmd

F32 = mybir.dt.float32
I32 = mybir.dt.int32
BF16 = mybir.dt.bfloat16
AX = mybir.AxisListType
ALU = mybir.AluOpType
ACT = mybir.ActivationFunctionType

D = 1024
T = 2048
NT = T // 128
ND = D // 128
FP = 640
NFT = FP // 128
FR = 2 * FP
C = 10
NU = NFT * C
II = 2816
NI = II // 128
NCH = 4
TC = T // NCH
QMAX = 127.0
EPS = 1e-5
RG = [[0, 1, 2, 3, 4, 5, 6, 7]]
RG4 = [[0, 1, 2, 3], [4, 5, 6, 7]]
ENGS = ("sync", "vector", "scalar", "tensor", "gpsimd")

_CACHE = {}


class Op:
    __slots__ = ("eng", "fn", "deps", "kind", "sem_inc", "tok", "qtag")

    def __init__(self, eng, fn, deps, kind, qtag=""):
        self.eng, self.fn, self.deps, self.kind = eng, fn, deps, kind
        self.sem_inc = 16 if kind == "d" else 1
        self.tok = None
        self.qtag = qtag


class Buf:
    __slots__ = ("w", "r")

    def __init__(self):
        self.w = None
        self.r = []


class Prog:
    def __init__(self):
        self.ops = {e: [] for e in ENGS}

    def add(self, eng, fn, reads=(), writes=(), kind="c", qtag=""):
        deps = []
        for b in reads:
            if b.w is not None:
                deps.append(b.w)
        for b in writes:
            if b.w is not None:
                deps.append(b.w)
            deps.extend(b.r)
        op = Op(eng, fn, deps, kind, qtag)
        self.ops[eng].append(op)
        for b in reads:
            b.r.append(op)
        for b in writes:
            b.w = op
            b.r = []
        return op

    def emit(self, nc, es):
        signaled = set()
        for e in ENGS:
            for op in self.ops[e]:
                for dp in op.deps:
                    if dp.eng != op.eng or dp.kind != "c":
                        signaled.add(id(dp))
        streams = {}

        def stream(key):
            if key not in streams:
                nm = f"q_{key[0]}_{key[1]}_{key[2]}"[:28]
                streams[key] = [es.enter_context(nc.semaphore(nm)), 0]
            return streams[key]

        for e in ENGS:
            for op in self.ops[e]:
                if id(op) in signaled or op.kind != "c":
                    st = stream((op.eng, op.kind, op.qtag))
                    st[1] += op.sem_inc
                    op.tok = (st[0], st[1])

        with nc.Block() as b:
            def make(e):
                def body(eng_obj):
                    for op in self.ops[e]:
                        seen = {}
                        for dp in op.deps:
                            if dp.eng == op.eng and dp.kind == "c":
                                continue
                            if dp.tok is None:
                                continue
                            sem, val = dp.tok
                            if id(sem) not in seen or seen[id(sem)][1] < val:
                                seen[id(sem)] = (sem, val)
                        for sem, val in seen.values():
                            eng_obj.wait_ge(sem, val)
                        ins = op.fn(eng_obj)
                        if op.tok is not None:
                            assert ins is not None, f"{e} op returned None but needs signal"
                            ins.then_inc(op.tok[0], op.sem_inc)
                return body
            for e in ENGS:
                if self.ops[e]:
                    getattr(b, e)(make(e))


def _host_consts():
    dd = np.arange(D)[:, None].astype(np.float64)
    ff = np.arange(FP)[None, :].astype(np.float64)
    ang = 2.0 * np.pi * (dd * ff) / D
    valid = (ff < 513).astype(np.float64)
    cosm = np.cos(ang) * valid
    sinm = np.sin(ang) * valid
    dft = np.concatenate([cosm, -sinm], axis=1).astype(np.float32)
    w = np.where((ff == 0) | (ff == 512), 1.0, 2.0) * valid
    ctm = np.concatenate([(w * cosm) / (D * C), -(w * sinm) / (D * C)],
                         axis=1).astype(np.float32)
    idn = np.eye(128, dtype=np.float32)
    return dft, ctm, idn


def build_program(debug=False):
    ik = "ExternalOutput" if debug else "Internal"
    nc = bass.Bass("TRN2", target_bir_lowering=False, debug=False)

    x_in = nc.dram_tensor("x_in", [T, D], F32, kind="ExternalInput").ap()
    w_s = [nc.dram_tensor(n, [D, 128], F32, kind="ExternalInput").ap()
           for n in ("wq_s", "wk_s", "wv_s")]
    wo_s = nc.dram_tensor("wo_s", [128, D], F32, kind="ExternalInput").ap()
    wgu_s = [nc.dram_tensor(n, [384, D], F32, kind="ExternalInput").ap()
             for n in ("wg_s", "wu_s")]
    wd_s = nc.dram_tensor("wd_s", [128, II], F32, kind="ExternalInput").ap()
    clip2 = nc.dram_tensor("clip2", [128, 2], F32, kind="ExternalInput").ap()
    dft_in = nc.dram_tensor("dft", [D, FR], F32, kind="ExternalInput").ap()
    ct_in = nc.dram_tensor("ct", [D, FR], F32, kind="ExternalInput").ap()
    idn_in = nc.dram_tensor("idn", [128, 128], F32, kind="ExternalInput").ap()
    permspad = nc.dram_tensor("permspad", [C, FP], I32, kind="ExternalInput").ap()
    mask16_in = nc.dram_tensor("mask16", [128, 8], F32, kind="ExternalInput").ap()
    out_t = nc.dram_tensor("out_t", [T, D], F32, kind="ExternalOutput").ap()

    d_fqk = nc.dram_tensor("d_fqk", [FP, 4, T], F32, kind=ik).ap()
    d_fv = nc.dram_tensor("d_fv", [2 * NFT, 128, T], F32, kind=ik).ap()
    d_xnt = nc.dram_tensor("d_xnt", [ND, 128, T], F32, kind=ik).ap()
    d_r = nc.dram_tensor("d_r", [2 * NFT, 128, T], F32, kind=ik).ap()
    d_h = nc.dram_tensor("d_h", [NI, 128, T], F32, kind=ik).ap()
    cc_w1ai = nc.dram_tensor("cc_w1ai", [128, 3840], F32, kind="Internal").ap()
    cc_w1oa = nc.dram_tensor("cc_w1oa", [8, 128, 3840], F32, kind="Internal").ap()
    cc_w1bi = nc.dram_tensor("cc_w1bi", [128, 1280], F32, kind="Internal").ap()
    cc_w1ob = nc.dram_tensor("cc_w1ob", [8, 128, 1280], F32, kind="Internal").ap()
    cc_w2i = nc.dram_tensor("cc_w2i", [128, 4488], F32, kind="Internal").ap()
    cc_w2o = nc.dram_tensor("cc_w2o", [8, 128, 4488], F32, kind="Internal").ap()
    cc_m1i = nc.dram_tensor("cc_m1i", [128, 1], F32, kind="Internal").ap()
    cc_m1o = nc.dram_tensor("cc_m1o", [128, 1], F32, kind="Internal").ap()
    cc_m2i = nc.dram_tensor("cc_m2i", [128, 1], F32, kind="Internal").ap()
    cc_m2o = nc.dram_tensor("cc_m2o", [128, 1], F32, kind="Internal").ap()
    cc_si = [nc.dram_tensor(f"cc_si{p}", [128, 2], F32, kind="Internal").ap()
             for p in range(3)]
    cc_so = [nc.dram_tensor(f"cc_so{p}", [4, 128, 2], F32, kind="Internal").ap()
             for p in range(3)]
    d_off = nc.dram_tensor("d_off", [NU, 128, 2], F32, kind=ik).ap()
    d_slot = nc.dram_tensor("d_slot", [NU, 128, 8], F32, kind=ik).ap()
    d_mpr = nc.dram_tensor("d_mpr", [NU, 128, 8], F32, kind=ik).ap()
    d_w2dbg = nc.dram_tensor("d_w2dbg", [128, 4488], F32, kind=ik).ap()
    d_mxdbg = nc.dram_tensor("d_mxdbg", [2, 128, 1], F32, kind=ik).ap()
    d_w2odbg = nc.dram_tensor("d_w2odbg", [8, 128, 561], F32, kind=ik).ap()

    es = contextlib.ExitStack()
    with es:
        def sbt(name, shape, dt=F32):
            return es.enter_context(nc.sbuf_tensor(name, shape, dt))

        A = sbt("A", [128, 16384])
        Bb = sbt("Bb", [128, 16384])
        G = sbt("G", [128, 8192])
        W0 = sbt("W0", [128, 2048])
        W1 = sbt("W1", [128, 2048])
        W2 = sbt("W2", [128, 2048])
        EV = sbt("EV", [128, 2048])
        s_id = sbt("s_id", [128, 128])
        s_clip = sbt("s_clip", [128, 2])
        s_mask = sbt("s_mask", [128, 8])
        s_idx = [sbt(f"s_idx{p}", [128, 1], I32) for p in range(3)]
        s_slot = [sbt(f"s_slot{p}", [128, 8]) for p in range(2)]
        s_mpr = sbt("s_mpr", [128, 8])
        s_stg = [sbt(f"s_stg{p}", [128, 2]) for p in range(3)]
        s_off = sbt("s_off", [128, 2])
        s_tmp = sbt("s_tmp", [128, 8])
        s_rmax = sbt("s_rmax", [128, 1])
        s_gmax = sbt("s_gmax", [128, 1])
        s_hmax = sbt("s_hmax", [128, 1])
        s_kx = sbt("s_kx", [128, 1])
        s_kh = sbt("s_kh", [128, 1])
        s_sxq = sbt("s_sxq", [128, 1])
        s_shq = sbt("s_shq", [128, 1])
        s_mysc = sbt("s_mysc", [128, 8])
        s_scf = sbt("s_scf", [128, 56])
        s_sxgu = sbt("s_sxgu", [128, 56])
        s_shdall = sbt("s_shdall", [128, 56])
        s_bc = sbt("s_bc", [128, 128])
        s_sgs = sbt("s_sgs", [128, 2 * TC])
        s_wt = [sbt(f"s_wt{p}", [128, 128], BF16) for p in range(4)]
        s_qb = [sbt(f"s_qb{p}", [128, 1024], BF16) for p in range(2)]
        s_wgu = [sbt(f"s_wgu{p}", [128, 1024], BF16) for p in range(2)]

        ps = [es.enter_context(nc.psum_tensor(f"ps{i}", [128, TC], F32))
              for i in range(8)]

        Abf = A.bitcast(BF16)
        Gbf = G.bitcast(BF16)
        Bbf = Bb.bitcast(BF16)
        Bi32 = Bb.bitcast(I32)
        W0i = W0.bitcast(I32)
        W0bf = W0.bitcast(BF16)
        W1bf = W1.bitcast(BF16)

        P = Prog()
        BUFS = {}

        def bf_(t):
            if id(t) not in BUFS:
                BUFS[id(t)] = Buf()
            return BUFS[id(t)]

        bA, bB, bG, bW0, bW1, bW2, bEV = (bf_(x) for x in (A, Bb, G, W0, W1, W2, EV))
        bps = [bf_(p_) for p_ in ps]
        bid = bf_(s_id)
        btmp = bf_(s_tmp)

        s_jnk = sbt("s_jnk", [128, 16])

        def add(eng, fn, reads=(), writes=(), kind="c", qtag=""):
            return P.add(eng, fn, reads, writes, kind, qtag)

        def vspace():
            P.add("vector", lambda e: e.tensor_copy(out=s_jnk[:, 0:1],
                                                    in_=s_id[:, 0:1]), (), (), "c")

        def split_hilo(dst_h, dst_l, src, tmp, rbs, wbs, tmpb):
            add("vector", lambda e, a=dst_h, b=src: e.tensor_copy(out=a, in_=b),
                reads=rbs, writes=wbs)
            add("vector", lambda e, a=tmp, b=dst_h: e.tensor_copy(out=a, in_=b),
                reads=wbs, writes=[tmpb])
            add("vector", lambda e, a=tmp, b=src, c_=tmp: e.tensor_sub(out=a, in0=b, in1=c_),
                reads=rbs + [tmpb], writes=[tmpb])
            add("vector", lambda e, a=dst_l, b=tmp: e.tensor_copy(out=a, in_=b),
                reads=[tmpb], writes=wbs)

        add("sync", lambda e: e.dma_start(out=s_id[:], in_=idn_in[:, :]),
            writes=[bid], kind="d", qtag="i0")
        add("sync", lambda e: e.dma_start(out=s_clip[:], in_=clip2[:, :]),
            writes=[bf_(s_clip)], kind="d", qtag="i1")
        add("sync", lambda e: e.dma_start(out=s_mask[:], in_=mask16_in[:, :]),
            writes=[bf_(s_mask)], kind="d", qtag="i2")

        for tt in range(NT):
            buf = W0 if tt % 2 == 0 else W1
            bb = bW0 if tt % 2 == 0 else bW1
            add("sync", lambda e, buf=buf, tt=tt: e.dma_start(
                out=buf[:, 0:D], in_=x_in[tt * 128:(tt + 1) * 128, :]),
                writes=[bb], kind="d", qtag=f"x{tt % 2}")
            add("vector", lambda e, buf=buf: e.tensor_reduce(
                out=s_tmp[:, 0:1], in_=buf[:, 0:D], axis=AX.X, op=ALU.add),
                reads=[bb], writes=[btmp])
            vspace()
            add("vector", lambda e: e.tensor_scalar(
                s_tmp[:, 0:1], s_tmp[:, 0:1], 1.0 / D, None, ALU.mult),
                reads=[btmp], writes=[btmp])
            vspace()
            add("vector", lambda e, buf=buf: e.tensor_scalar(
                buf[:, 0:D], buf[:, 0:D], s_tmp[:, 0:1], None, ALU.subtract),
                reads=[bb, btmp], writes=[bb])
            add("vector", lambda e, buf=buf: e.tensor_mul(
                out=buf[:, D:2 * D], in0=buf[:, 0:D], in1=buf[:, 0:D]),
                reads=[bb], writes=[bb])
            add("vector", lambda e, buf=buf: e.tensor_reduce(
                out=s_tmp[:, 1:2], in_=buf[:, D:2 * D], axis=AX.X, op=ALU.add),
                reads=[bb], writes=[btmp])
            vspace()
            add("vector", lambda e: e.tensor_scalar(
                s_tmp[:, 1:2], s_tmp[:, 1:2], 1.0 / D, EPS, ALU.mult, ALU.add),
                reads=[btmp], writes=[btmp])
            add("scalar", lambda e: e.activation(
                out=s_tmp[:, 4:5], in_=s_tmp[:, 1:2], func=ACT.Sqrt),
                reads=[btmp], writes=[btmp])
            add("vector", lambda e: e.reciprocal(
                out=s_tmp[:, 2:3], in_=s_tmp[:, 4:5]),
                reads=[btmp], writes=[btmp])
            vspace()
            add("scalar", lambda e, buf=buf: e.activation(
                out=buf[:, D:2 * D], in_=buf[:, 0:D], func=ACT.Copy,
                scale=s_tmp[:, 2:3]),
                reads=[bb, btmp], writes=[bb])
            if tt == 0:
                add("vector", lambda e, buf=buf: e.reduce_max(
                    out=s_rmax[:], in_=buf[:, D:2 * D], axis=AX.X,
                    apply_absolute_value=True),
                    reads=[bb], writes=[bf_(s_rmax)])
                vspace()
            else:
                add("vector", lambda e, buf=buf: e.reduce_max(
                    out=s_tmp[:, 3:4], in_=buf[:, D:2 * D], axis=AX.X,
                    apply_absolute_value=True),
                    reads=[bb], writes=[btmp])
                vspace()
                add("vector", lambda e: e.tensor_max(
                    out=s_rmax[:], in0=s_rmax[:], in1=s_tmp[:, 3:4]),
                    reads=[btmp, bf_(s_rmax)], writes=[bf_(s_rmax)])
            for k in range(ND):
                bnk = (tt * ND + k) % 2
                add("tensor", lambda e, buf=buf, k=k, bnk=bnk: e.transpose(
                    ps[bnk][:, 0:128], buf[:, D + 128 * k:D + 128 * (k + 1)], s_id[:]),
                    reads=[bb, bid], writes=[bps[bnk]])
                add("scalar", lambda e, tt=tt, k=k, bnk=bnk: e.copy(
                    out=A[:, k * T + 128 * tt:k * T + 128 * (tt + 1)],
                    in_=ps[bnk][:, 0:128]),
                    reads=[bps[bnk]], writes=[bA])

        bdxnt = bf_(d_xnt)
        for k in range(ND):
            add("sync", lambda e, k=k: e.dma_start(
                out=d_xnt[k, :, :], in_=A[:, k * T:(k + 1) * T]),
                reads=[bA], writes=[bdxnt], kind="d", qtag="xn")

        add("vector", lambda e: e.tensor_scalar(
            s_bc[:], s_id[:], 0.0, s_rmax[:, 0:1], ALU.mult, ALU.add),
            reads=[bid, bf_(s_rmax)], writes=[bf_(s_bc)])
        add("tensor", lambda e: e.transpose(ps[0][:, 0:128], s_bc[:], s_id[:]),
            reads=[bf_(s_bc), bid], writes=[bps[0]])
        add("scalar", lambda e: e.copy(out=s_bc[:], in_=ps[0][:, 0:128]),
            reads=[bps[0]], writes=[bf_(s_bc)])
        add("vector", lambda e: e.reduce_max(out=s_rmax[:], in_=s_bc[:], axis=AX.X),
            reads=[bf_(s_bc)], writes=[bf_(s_rmax)])
        add("sync", lambda e: e.dma_start(out=cc_m1i[:, :], in_=s_rmax[:]),
            reads=[bf_(s_rmax)], writes=[bf_(cc_m1i)], kind="d", qtag="upm")
        add("gpsimd", lambda e: e.collective_compute(
            "AllReduce", ALU.max, replica_groups=RG,
            ins=[cc_m1i[:, :]], outs=[cc_m1o[:, :]]),
            reads=[bf_(cc_m1i)], writes=[bf_(cc_m1o)], kind="cc")

        def dfth(k):
            return Bbf[:, k * FR:(k + 1) * FR]

        def dftl(k):
            return Bbf[:, 10240 + k * FR:10240 + (k + 1) * FR]

        def wsh(k):
            return Bbf[:, 20480 + k * 128:20480 + (k + 1) * 128]

        def wsl(k):
            return Bbf[:, 21504 + k * 128:21504 + (k + 1) * 128]

        bw1as = [Buf(), Buf()]
        bw1oa = bf_(cc_w1oa)
        bw1bs = [Buf(), Buf()]
        bw1ob = bf_(cc_w1ob)
        bw2i = bf_(cc_w2i)
        bw2s = [Buf() for _ in range(5)]
        bw2o = bf_(cc_w2o)
        s_qbf = [q.bitcast(F32) for q in s_qb]
        s_wtf = [w.bitcast(F32) for w in s_wt]

        for k in range(ND):
            stgb = W2 if k % 2 == 0 else EV
            bstgb = bW2 if k % 2 == 0 else bEV
            add("sync", lambda e, k=k, stgb=stgb: e.dma_start(
                out=stgb[:, 0:FR], in_=dft_in[k * 128:(k + 1) * 128, :]),
                writes=[bstgb], kind="d", qtag=f"pw{k % 2}")
            split_hilo(dfth(k), dftl(k), stgb[:, 0:FR], W1[:, 0:FR],
                       [bstgb], [bB], bW1)

        NCS = [(0, 512), (512, 512), (1024, 256)]

        for ti in range(3):
            for k in range(ND):
                add("sync", lambda e, k=k, ti=ti: e.dma_start(
                    out=W2[:, 0:128], in_=w_s[ti][k * 128:(k + 1) * 128, :]),
                    writes=[bW2], kind="d")
                split_hilo(wsh(k), wsl(k), W2[:, 0:128], W1[:, 0:128],
                           [bW2], [bB], bW1)
            for j, (noff, nsz) in enumerate(NCS):
                bnk = j % 2

                def mmq(e, noff=noff, nsz=nsz, bnk=bnk):
                    ins = None
                    for k in range(ND):
                        lh = wsh(k)
                        ll = wsl(k)
                        rh = dfth(k)[:, noff:noff + nsz]
                        rl = dftl(k)[:, noff:noff + nsz]
                        ins = e.matmul(ps[bnk][:, 0:nsz], lh, rh,
                                       start=(k == 0), stop=False)
                        ins = e.matmul(ps[bnk][:, 0:nsz], lh, rl,
                                       start=False, stop=False)
                        ins = e.matmul(ps[bnk][:, 0:nsz], ll, rh,
                                       start=False, stop=(k == ND - 1))
                    return ins
                add("tensor", mmq, reads=[bB], writes=[bps[bnk]])
                add("scalar", lambda e, bnk=bnk, nsz=nsz: e.copy(
                    out=EV[:, bnk * 1024:bnk * 1024 + nsz],
                    in_=ps[bnk][:, 0:nsz]),
                    reads=[bps[bnk]], writes=[bEV])
                qb = s_qb[bnk]
                qbf = s_qbf[bnk]
                bqb = bf_(qb)
                split_hilo(qb[:, 0:nsz], qb[:, 512:512 + nsz],
                           EV[:, bnk * 1024:bnk * 1024 + nsz],
                           W1[:, 0:nsz], [bEV], [bqb], bW1)
                add("sync", lambda e, ti=ti, noff=noff, nsz=nsz, qbf=qbf:
                    e.dma_start(out=cc_w1ai[:, ti * 1280 + noff // 2:
                                            ti * 1280 + (noff + nsz) // 2],
                                in_=qbf[:, 0:nsz // 2]),
                    reads=[bqb], writes=[bw1as[bnk]], kind="d", qtag=f"w1{bnk}")
                add("sync", lambda e, ti=ti, noff=noff, nsz=nsz, qbf=qbf:
                    e.dma_start(out=cc_w1ai[:, ti * 1280 + 640 + noff // 2:
                                            ti * 1280 + 640 + (noff + nsz) // 2],
                                in_=qbf[:, 256:256 + nsz // 2]),
                    reads=[bqb], writes=[bw1as[bnk]], kind="d", qtag=f"w1{bnk}")

        add("gpsimd", lambda e: e.collective_compute(
            "AllGather", ALU.bypass, replica_groups=RG,
            ins=[cc_w1ai[:, :]], outs=[cc_w1oa[:, :, :]]),
            reads=bw1as, writes=[bw1oa], kind="cc")

        def xh(k):
            return Abf[:, k * T:(k + 1) * T]

        def xl(k):
            return Abf[:, 16384 + k * T:16384 + (k + 1) * T]

        for k in range(ND):
            stgb = W2 if k % 2 == 0 else EV
            bstgb = bW2 if k % 2 == 0 else bEV
            add("sync", lambda e, k=k, stgb=stgb: e.dma_start(
                out=stgb[:, 0:T], in_=d_xnt[k, :, :]),
                reads=[bdxnt], writes=[bstgb], kind="d", qtag=f"xr{k % 2}")
            split_hilo(xh(k), xl(k), stgb[:, 0:T], W1[:, 0:T], [bstgb], [bA], bW1)


        add("sync", lambda e: e.dma_start(out=W2[:, 0:D], in_=wo_s[:, :]),
            writes=[bW2], kind="d")
        for k in range(ND):
            bnk = k % 2
            add("tensor", lambda e, k=k, bnk=bnk: e.transpose(
                ps[bnk][:, 0:128], W2[:, 128 * k:128 * (k + 1)], s_id[:]),
                reads=[bW2, bid], writes=[bps[bnk]])
            add("scalar", lambda e, k=k, bnk=bnk: e.copy(
                out=G[:, k * 128:(k + 1) * 128], in_=ps[bnk][:, 0:128]),
                reads=[bps[bnk]], writes=[bG])

        def woth(k):
            return Gbf[:, 2048 + k * 128:2048 + (k + 1) * 128]

        def wotl(k):
            return Gbf[:, 3072 + k * 128:3072 + (k + 1) * 128]

        for k in range(ND):
            split_hilo(woth(k), wotl(k), G[:, k * 128:(k + 1) * 128],
                       W1[:, 0:128], [bG], [bG], bW1)

        for k in range(ND):
            stgb = W2 if k % 2 == 0 else EV
            bstgb = bW2 if k % 2 == 0 else bEV
            add("sync", lambda e, k=k, stgb=stgb: e.dma_start(
                out=stgb[:, 0:FR], in_=ct_in[k * 128:(k + 1) * 128, :]),
                writes=[bstgb], kind="d", qtag=f"pw{k % 2}")
            split_hilo(dfth(k), dftl(k), stgb[:, 0:FR], W1[:, 0:FR],
                       [bstgb], [bB], bW1)

        for mt in range(2 * NFT):
            bnk = mt % 2

            def mmo(e, mt=mt, bnk=bnk):
                ins = None
                for k in range(ND):
                    lh = dfth(k)[:, mt * 128:(mt + 1) * 128]
                    ll = dftl(k)[:, mt * 128:(mt + 1) * 128]
                    rh = woth(k)
                    rl = wotl(k)
                    ins = e.matmul(ps[bnk][:, 0:128], lh, rh,
                                   start=(k == 0), stop=False)
                    ins = e.matmul(ps[bnk][:, 0:128], lh, rl,
                                   start=False, stop=False)
                    ins = e.matmul(ps[bnk][:, 0:128], ll, rh,
                                   start=False, stop=(k == ND - 1))
                return ins
            add("tensor", mmo, reads=[bB, bG], writes=[bps[bnk]])
            add("scalar", lambda e, bnk=bnk: e.copy(
                out=EV[:, bnk * 1024:bnk * 1024 + 128],
                in_=ps[bnk][:, 0:128]),
                reads=[bps[bnk]], writes=[bEV])
            qb = s_qb[bnk]
            qbf = s_qbf[bnk]
            bqb = bf_(qb)
            split_hilo(qb[:, 0:128], qb[:, 512:640],
                       EV[:, bnk * 1024:bnk * 1024 + 128],
                       W1[:, 0:128], [bEV], [bqb], bW1)
            add("sync", lambda e, mt=mt, qbf=qbf: e.dma_start(
                out=cc_w1bi[:, mt * 64:(mt + 1) * 64],
                in_=qbf[:, 0:64]),
                reads=[bqb], writes=[bw1bs[bnk]], kind="d", qtag=f"w1b{bnk}")
            add("sync", lambda e, mt=mt, qbf=qbf: e.dma_start(
                out=cc_w1bi[:, 640 + mt * 64:640 + (mt + 1) * 64],
                in_=qbf[:, 256:256 + 64]),
                reads=[bqb], writes=[bw1bs[bnk]], kind="d", qtag=f"w1b{bnk}")

        add("gpsimd", lambda e: e.collective_compute(
            "AllGather", ALU.bypass, replica_groups=RG,
            ins=[cc_w1bi[:, :]], outs=[cc_w1ob[:, :, :]]),
            reads=bw1bs, writes=[bw1ob], kind="cc")

        W2i = W2.bitcast(I32)
        bmys = bf_(s_mysc)

        def qsrc(i):
            return G[:, (i % 2) * 2816:(i % 2) * 2816 + 2816]

        qcbuf = [W0, W1]
        bqcb = [bW0, bW1]
        TILES = [(wi, il) for wi in range(2) for il in range(3)] + [(2, 0)]

        def load_qsrc(i):
            if i < 6:
                wi, il = TILES[i]
                add("scalar", lambda e, i=i, wi=wi, il=il: e.dma_start(
                    out=qsrc(i)[:, 0:D],
                    in_=wgu_s[wi][il * 128:(il + 1) * 128, :]),
                    writes=[bG], kind="d", qtag=f"qs{i % 2}")
            else:
                add("scalar", lambda e, i=i: e.dma_start(
                    out=qsrc(i)[:, 0:II], in_=wd_s[:, :]),
                    writes=[bG], kind="d", qtag=f"qs{i % 2}")

        def quant_chunk(i, jc, co, cw):
            qc = qcbuf[jc % 2][:, 0:cw]
            bqc = bqcb[jc % 2]
            add("vector", lambda e, i=i, co=co, cw=cw: e.tensor_scalar(
                EV[:, 0:cw], qsrc(i)[:, co:co + cw], s_tmp[:, 5:6], None,
                ALU.mult),
                reads=[bG, btmp], writes=[bEV])
            add("vector", lambda e, cw=cw: e.tensor_copy(
                out=W2i[:, 0:cw], in_=EV[:, 0:cw]), reads=[bEV], writes=[bW2])
            add("vector", lambda e, cw=cw, qc=qc: e.tensor_copy(
                out=qc, in_=W2i[:, 0:cw]), reads=[bW2], writes=[bqc])
            add("vector", lambda e, cw=cw, qc=qc: e.scalar_tensor_tensor(
                out=W2[:, 0:cw], in0=EV[:, 0:cw], scalar=0.5,
                in1=qc, op0=ALU.subtract, op1=ALU.is_gt),
                reads=[bEV, bqc], writes=[bW2])
            add("vector", lambda e, cw=cw, qc=qc: e.tensor_add(
                out=qc, in0=qc, in1=W2[:, 0:cw]),
                reads=[bW2, bqc], writes=[bqc])
            add("vector", lambda e, cw=cw, qc=qc: e.scalar_tensor_tensor(
                out=W2[:, 0:cw], in0=EV[:, 0:cw], scalar=0.5,
                in1=qc, op0=ALU.add, op1=ALU.is_lt),
                reads=[bEV, bqc], writes=[bW2])
            add("vector", lambda e, cw=cw, qc=qc: e.tensor_sub(
                out=qc, in0=qc, in1=W2[:, 0:cw]),
                reads=[bW2, bqc], writes=[bqc])
            add("vector", lambda e, cw=cw, qc=qc: e.tensor_scalar(
                qc, qc, -(QMAX + 1.0), QMAX, ALU.max, ALU.min),
                reads=[bqc], writes=[bqc])

        def tq_chunk(jc, cw, colbase):
            for kt in range(cw // 128):
                bnk = kt % 2
                add("tensor", lambda e, jc=jc, kt=kt, bnk=bnk: e.transpose(
                    ps[bnk][:, 0:128],
                    qcbuf[jc % 2][:, 128 * kt:128 * (kt + 1)], s_id[:]),
                    reads=[bqcb[jc % 2], bid], writes=[bps[bnk]])
                wtb = bf_(s_wt[kt % 4])
                add("scalar", lambda e, kt=kt, bnk=bnk: e.copy(
                    out=s_wt[kt % 4][:], in_=ps[bnk][:, 0:128]),
                    reads=[bps[bnk]], writes=[wtb])
                add("gpsimd", lambda e, kt=kt, colbase=colbase: e.dma_start(
                    out=cc_w2i[:, colbase + kt * 64:colbase + (kt + 1) * 64],
                    in_=s_wtf[kt % 4][:, :]),
                    reads=[wtb], writes=[bw2s[kt % 4]], kind="d",
                    qtag=f"w2{kt % 4}")

        load_qsrc(0)
        jc = 0
        for i in range(7):
            if i + 1 < 7:
                load_qsrc(i + 1)
            wdt = D if i < 6 else II
            add("vector", lambda e, i=i, wdt=wdt: e.reduce_max(
                out=s_tmp[:, 4:5], in_=qsrc(i)[:, 0:wdt], axis=AX.X,
                apply_absolute_value=True),
                reads=[bG], writes=[btmp])
            vspace()
            sc = s_mysc[:, i:i + 1]
            add("vector", lambda e, sc=sc: e.tensor_scalar(
                sc, s_tmp[:, 4:5], 1.0 / QMAX, None, ALU.mult),
                reads=[btmp], writes=[bmys])
            add("vector", lambda e: e.reciprocal(out=s_tmp[:, 5:6], in_=s_tmp[:, 4:5]),
                reads=[btmp], writes=[btmp])
            vspace()
            add("vector", lambda e: e.tensor_scalar(
                s_tmp[:, 5:6], s_tmp[:, 5:6], QMAX, None, ALU.mult),
                reads=[btmp], writes=[btmp])
            vspace()
            if i < 6:
                wi, il = TILES[i]
                quant_chunk(i, jc, 0, D)
                tq_chunk(jc, D, wi * 1536 + il * 512)
                jc += 1
            else:
                for half in range(2):
                    quant_chunk(i, jc, half * 1408, 1408)
                    tq_chunk(jc, 1408, 3072 + half * 11 * 64)
                    jc += 1
        add("gpsimd", lambda e: e.dma_start(
            out=cc_w2i[:, 4480:4487], in_=s_mysc[:, 0:7]),
            reads=[bmys], writes=[bw2s[4]], kind="d", qtag="w2s")
        add("gpsimd", lambda e: e.collective_compute(
            "AllGather", ALU.bypass, replica_groups=RG,
            ins=[cc_w2i[:, :]], outs=[cc_w2o[:, :, :]]),
            reads=bw2s, writes=[bw2o], kind="cc")

        def wfh(k):
            return Bbf[:, k * FR:(k + 1) * FR]

        def wfl(k):
            return Bbf[:, 10240 + k * FR:10240 + (k + 1) * FR]

        stg_f32 = [Bb[:, 11264:13312], Bb[:, 13312:15360]]
        bsp = [Buf(), Buf()]
        bwf = [Buf() for _ in range(ND)]

        for ti in range(3):
            for k in range(ND):
                add("sync", lambda e, ti=ti, k=k: e.dma_start(
                    out=Bb[:, k * 640:(k + 1) * 640],
                    in_=cc_w1oa[k, :, ti * 1280:ti * 1280 + 640]),
                    reads=[bw1oa], writes=[bwf[k]], kind="d", qtag=f"wf{k}")
                add("sync", lambda e, ti=ti, k=k: e.dma_start(
                    out=Bb[:, 5120 + k * 640:5120 + (k + 1) * 640],
                    in_=cc_w1oa[k, :, ti * 1280 + 640:ti * 1280 + 1280]),
                    reads=[bw1oa], writes=[bwf[k]], kind="d", qtag=f"wf{k}")
            for mt in range(2 * NFT):
                bset = (mt % 2) * 4
                for k in range(ND):
                    def mmx(e, mt=mt, bset=bset, k=k):
                        ins = None
                        lh = wfh(k)[:, mt * 128:(mt + 1) * 128]
                        ll = wfl(k)[:, mt * 128:(mt + 1) * 128]
                        for pp, (lw, rsel) in enumerate(((lh, xh), (lh, xl),
                                                         (ll, xh))):
                            for ch in range(NCH):
                                ins = e.matmul(
                                    ps[bset + ch][:, 0:TC], lw,
                                    rsel(k)[:, ch * TC:(ch + 1) * TC],
                                    start=(k == 0 and pp == 0),
                                    stop=(k == ND - 1 and pp == 2))
                        return ins
                    add("tensor", mmx, reads=[bA, bwf[k]],
                        writes=[bps[bset + ch] for ch in range(NCH)])
                stg = stg_f32[mt % 2]
                for ch in range(NCH):
                    add("scalar", lambda e, stg=stg, bset=bset, ch=ch: e.copy(
                        out=stg[:, ch * TC:(ch + 1) * TC], in_=ps[bset + ch][:, 0:TC]),
                        reads=[bps[bset + ch]], writes=[bB])
                if ti < 2:
                    comp = (0 if ti == 0 else 2) + (0 if mt < NFT else 1)
                    fmt = mt % NFT
                    add("sync", lambda e, comp=comp, fmt=fmt, stg=stg: e.dma_start(
                        out=d_fqk.rearrange("a b c2 -> a (b c2)")[
                            fmt * 128:(fmt + 1) * 128, comp * T:(comp + 1) * T],
                        in_=stg[:, 0:T]),
                        reads=[bB], writes=[bsp[mt % 2]], kind="d",
                        qtag=f"sp{mt % 2}")
                else:
                    add("sync", lambda e, mt=mt, stg=stg: e.dma_start(
                        out=d_fv[mt, :, :], in_=stg[:, 0:T]),
                        reads=[bB], writes=[bsp[mt % 2]], kind="d",
                        qtag=f"sp{mt % 2}")

        fv_re = [Bb[:, 0:T], W2[:, 0:T]]
        fv_im = [Bb[:, T:2 * T], EV[:, 0:T]]
        kv_re = [Bb[:, 2 * T:3 * T], Bb[:, 3 * T:4 * T], Bb[:, 4 * T:5 * T]]
        kv_im = [Bb[:, 5 * T:6 * T], Bb[:, 6 * T:7 * T], Bb[:, 7 * T:8 * T]]
        r_re = A[:, 0:T]
        r_im = A[:, T:2 * T]
        p1 = A[:, 2 * T:3 * T]
        p2 = A[:, 3 * T:4 * T]
        gk_re = [A[:, 4 * T:5 * T], A[:, 6 * T:7 * T], W0[:, 0:T]]
        gk_im = [A[:, 5 * T:6 * T], A[:, 7 * T:8 * T], W1[:, 0:T]]
        gq = [G[:, 0:2 * T], G[:, 2 * T:4 * T]]

        bfv_r = [[bB], [bW2]]
        bfv_i = [[bB], [bEV]]
        bkv = [bB, bB, bB]
        brr = Buf()
        bp12 = Buf()
        bg01 = [Buf(), Buf()]
        bgq = [Buf(), Buf()]
        bgk_r = [[bg01[0]], [bg01[1]], [bW0]]
        bgk_i = [[bg01[0]], [bg01[1]], [bW1]]
        bslot = [bf_(s_slot[p]) for p in range(2)]
        bstg = [bf_(s_stg[p]) for p in range(3)]
        boff = bf_(s_off)
        bmpr = bf_(s_mpr)
        bidx = [bf_(s_idx[p]) for p in range(3)]
        bcsi = [bf_(cc_si[p]) for p in range(3)]
        bcso = [bf_(cc_so[p]) for p in range(3)]
        bdr = bf_(d_r)

        def load_fv(ft):
            fp = ft % 2
            add("sync", lambda e, ft=ft, fp=fp: e.dma_start(
                out=fv_re[fp], in_=d_fv[ft, :, :]),
                reads=bsp, writes=bfv_r[fp] + (bwf if ft == 0 else []),
                kind="d", qtag=f"fv{fp}")
            add("sync", lambda e, ft=ft, fp=fp: e.dma_start(
                out=fv_im[fp], in_=d_fv[NFT + ft, :, :]),
                reads=bsp, writes=bfv_i[fp], kind="d", qtag=f"fv{fp}")

        def load_idx(u):
            ft, c = divmod(u, C)
            u3 = u % 3
            add("sync", lambda e, c=c, ft=ft, u3=u3: e.dma_start(
                out=s_idx[u3][:, :],
                in_=permspad[c:c + 1, ft * 128:(ft + 1) * 128].rearrange(
                    "one p -> p one")),
                writes=[bidx[u3]], kind="d", qtag=f"idx{u3}")

        def issue_k(u, first=False):
            u3 = u % 3
            wr = bgk_r[u3] + ([bA] if first else [])
            wi_ = bgk_i[u3] + ([bA] if first else [])
            add("gpsimd", lambda e, u3=u3: e.indirect_dma_start(
                out=gk_re[u3], out_offset=None,
                in_=d_fqk[0:128, :, :].rearrange("a b c2 -> a (b c2)"),
                in_offset=IndirectOffsetOnAxis(s_idx[u3][:, :], 0),
                element_offset=2 * T),
                reads=bsp + [bidx[u3]], writes=wr,
                kind="d", qtag=f"gk{u3}")
            add("gpsimd", lambda e, u3=u3: e.indirect_dma_start(
                out=gk_im[u3], out_offset=None,
                in_=d_fqk[0:128, :, :].rearrange("a b c2 -> a (b c2)"),
                in_offset=IndirectOffsetOnAxis(s_idx[u3][:, :], 0),
                element_offset=3 * T),
                reads=bsp + [bidx[u3]], writes=wi_,
                kind="d", qtag=f"gk{u3}")

        def issue_q(u, first=False):
            up = u % 2
            u3 = u % 3
            wq = [bgq[up]] + ([bG] if first else [])
            add("gpsimd", lambda e, up=up, u3=u3: e.indirect_dma_start(
                out=gq[up], out_offset=None,
                in_=d_fqk[0:128, :, :].rearrange("a b c2 -> a (b c2)"),
                in_offset=IndirectOffsetOnAxis(s_idx[u3][:, :], 0),
                element_offset=0),
                reads=bsp + [bidx[u3]], writes=wq, kind="d",
                qtag=f"gq{up}")

        def z_stage(u):
            ft, c = divmod(u, C)
            up = u % 2
            u3 = u % 3
            uk = u % 3
            add("sync", lambda e, up=up, u3=u3: e.dma_start(
                out=s_slot[up][:].rearrange("p (j c2) -> p j c2", j=4),
                in_=cc_so[u3][:, :, :].rearrange("j p c2 -> p j c2")),
                reads=[bcso[u3]], writes=[bslot[up]], kind="d", qtag=f"dl{up}")
            add("vector", lambda e, up=up: e.tensor_mul(
                out=s_mpr[:], in0=s_slot[up][:], in1=s_mask[:]),
                reads=[bslot[up], bf_(s_mask)], writes=[bmpr])
            vspace()
            add("vector", lambda e: e.tensor_reduce(
                out=s_off[:, 0:1], in_=s_mpr[:, 0:8:2], axis=AX.X, op=ALU.add),
                reads=[bmpr], writes=[boff])
            add("vector", lambda e: e.tensor_reduce(
                out=s_off[:, 1:2], in_=s_mpr[:, 1:8:2], axis=AX.X, op=ALU.add),
                reads=[bmpr], writes=[boff])
            vspace()
            vspace()
            if debug is True:
                add("sync", lambda e, u=u: e.dma_start(
                    out=d_off[u, :, :], in_=s_off[:]),
                    reads=[boff], writes=[bf_(d_off)], kind="d")
                add("sync", lambda e, u=u, up=up: e.dma_start(
                    out=d_slot[u, :, :], in_=s_slot[up][:]),
                    reads=[bslot[up]], writes=[bf_(d_slot)], kind="d")
                add("sync", lambda e, u=u: e.dma_start(
                    out=d_mpr[u, :, :], in_=s_mpr[:]),
                    reads=[bmpr], writes=[bf_(d_mpr)], kind="d")
            add("vector", lambda e, up=up: e.scalar_tensor_tensor(
                out=p1, in0=kv_re[uk], scalar=s_off[:, 0:1],
                in1=gq[up][:, 0:T], op0=ALU.add, op1=ALU.mult),
                reads=[bkv[uk], boff, bgq[up]], writes=[bp12])
            add("vector", lambda e, up=up: e.scalar_tensor_tensor(
                out=p2, in0=kv_im[uk], scalar=s_off[:, 1:2],
                in1=gq[up][:, T:2 * T], op0=ALU.add, op1=ALU.mult),
                reads=[bkv[uk], boff, bgq[up]], writes=[bp12])
            if c == 0:
                add("vector", lambda e: e.tensor_add(out=r_re, in0=p1, in1=p2),
                    reads=[bp12], writes=[brr])
            else:
                add("vector", lambda e: e.tensor_add(out=r_re, in0=r_re, in1=p1),
                    reads=[bp12, brr], writes=[brr])
                add("vector", lambda e: e.tensor_add(out=r_re, in0=r_re, in1=p2),
                    reads=[bp12, brr], writes=[brr])
            add("vector", lambda e, up=up: e.scalar_tensor_tensor(
                out=p1, in0=kv_im[uk], scalar=s_off[:, 1:2],
                in1=gq[up][:, 0:T], op0=ALU.add, op1=ALU.mult),
                reads=[bkv[uk], boff, bgq[up]], writes=[bp12])
            add("vector", lambda e, up=up: e.scalar_tensor_tensor(
                out=p2, in0=kv_re[uk], scalar=s_off[:, 0:1],
                in1=gq[up][:, T:2 * T], op0=ALU.add, op1=ALU.mult),
                reads=[bkv[uk], boff, bgq[up]], writes=[bp12])
            if c == 0:
                add("vector", lambda e: e.tensor_sub(out=r_im, in0=p1, in1=p2),
                    reads=[bp12], writes=[brr])
            else:
                add("vector", lambda e: e.tensor_add(out=r_im, in0=r_im, in1=p1),
                    reads=[bp12, brr], writes=[brr])
                add("vector", lambda e: e.tensor_sub(out=r_im, in0=r_im, in1=p2),
                    reads=[bp12, brr], writes=[brr])
            if c == C - 1:
                add("sync", lambda e, ft=ft: e.dma_start(
                    out=d_r[ft, :, :], in_=r_re), reads=[brr], writes=[bdr],
                    kind="d", qtag="rsp")
                add("sync", lambda e, ft=ft: e.dma_start(
                    out=d_r[NFT + ft, :, :], in_=r_im),
                    reads=[brr], writes=[bdr], kind="d", qtag="rsp")

        load_fv(0)
        load_idx(0)
        load_idx(1)
        issue_k(0, first=True)
        issue_k(1, first=True)
        issue_q(0, first=True)
        issue_q(1, first=True)
        for u in range(NU):
            ft, c = divmod(u, C)
            up = u % 3
            fp_ = 0
            if u + 2 < NU:
                load_idx(u + 2)
            if u + 2 < NU:
                issue_k(u + 2)
            if c == 5 and ft + 1 < NFT:
                load_fv(ft + 1)
            kp = u % 3
            fp = ft % 2
            add("vector", lambda e, kp=kp, fp=fp: e.tensor_mul(
                out=p1, in0=gk_re[kp], in1=fv_re[fp]),
                reads=bgk_r[kp] + bfv_r[fp],
                writes=[bp12] + ([bA] if u == 0 else []))
            add("vector", lambda e, kp=kp, fp=fp: e.tensor_mul(
                out=p2, in0=gk_im[kp], in1=fv_im[fp]),
                reads=bgk_i[kp] + bfv_i[fp], writes=[bp12])
            add("vector", lambda e, up=up: e.tensor_tensor_scan(
                out=kv_re[up], data0=p1, data1=p2, initial=0.0,
                op0=ALU.add, op1=ALU.subtract),
                reads=[bp12], writes=[bkv[up]] + (bwf if u == 0 else []))
            add("vector", lambda e, kp=kp, fp=fp: e.tensor_mul(
                out=p1, in0=gk_re[kp], in1=fv_im[fp]),
                reads=bgk_r[kp] + bfv_i[fp], writes=[bp12])
            add("vector", lambda e, kp=kp, fp=fp: e.tensor_mul(
                out=p2, in0=gk_im[kp], in1=fv_re[fp]),
                reads=bgk_i[kp] + bfv_r[fp], writes=[bp12])
            add("vector", lambda e, up=up: e.tensor_tensor_scan(
                out=kv_im[up], data0=p1, data1=p2, initial=0.0,
                op0=ALU.add, op1=ALU.add),
                reads=[bp12], writes=[bkv[up]])
            vspace()
            add("vector", lambda e, up=up: e.tensor_copy(
                out=s_stg[up][:, 0:1], in_=kv_re[up][:, T - 1:T]),
                reads=[bkv[up]], writes=[bstg[up]])
            add("vector", lambda e, up=up: e.tensor_copy(
                out=s_stg[up][:, 1:2], in_=kv_im[up][:, T - 1:T]),
                reads=[bkv[up]], writes=[bstg[up]])
            add("sync", lambda e, up=up: e.dma_start(
                out=cc_si[up][:, :], in_=s_stg[up][:]),
                reads=[bstg[up]], writes=[bcsi[up]], kind="d", qtag=f"up{up}")
            if u > 1:
                z_stage(u - 2)
            if u >= 2:
                issue_q(u)
            add("gpsimd", lambda e, up=up: e.collective_compute(
                "AllGather", ALU.bypass, replica_groups=RG4,
                ins=[cc_si[up][:, :]], outs=[cc_so[up][:, :, :]]),
                reads=[bcsi[up]], writes=[bcso[up]], kind="cc")
        z_stage(NU - 2)
        z_stage(NU - 1)

        def rsp_h(kt):
            return Abf[:, kt * T:(kt + 1) * T]

        def rsp_l(kt):
            if kt < 6:
                return Abf[:, 20480 + kt * T:20480 + (kt + 1) * T]
            return Gbf[:, (kt - 6) * T:(kt - 6 + 1) * T]

        batt = bB
        pc_regions = [bA, bG, brr, bp12, bg01[0], bg01[1], bgq[0], bgq[1]]
        for kt in range(2 * NFT):
            add("sync", lambda e, kt=kt: e.dma_start(
                out=W2[:, 0:T], in_=d_r[kt, :, :]),
                reads=[bdr], writes=[bW2], kind="d", qtag="rr")
            split_hilo(rsp_h(kt), rsp_l(kt), W2[:, 0:T], W1[:, 0:T],
                       [bW2], pc_regions, bW1)
        for mt in range(ND):
            bset = (mt % 2) * 4
            for kt in range(2 * NFT):
                for pp in range(2):
                    wti = (kt * 2 + pp) % 4
                    wtb = bf_(s_wt[wti])
                    base = 0 if pp == 0 else 640
                    add("sync", lambda e, base=base, kt=kt, mt=mt, wti=wti:
                        e.dma_start(
                        out=s_wtf[wti][:, :],
                        in_=cc_w1ob[mt, :, base + kt * 64:base + (kt + 1) * 64]),
                        reads=[bw1ob], writes=[wtb], kind="d", qtag=f"wt{wti}")

                    def mmi(e, kt=kt, pp=pp, bset=bset, wti=wti):
                        ins = None
                        rhs_list = ((rsp_h(kt), rsp_l(kt)) if pp == 0
                                    else (rsp_h(kt),))
                        for ri, rw in enumerate(rhs_list):
                            st = (kt == 0 and pp == 0 and ri == 0)
                            sp = (kt == 2 * NFT - 1 and pp == 1 and
                                  ri == len(rhs_list) - 1)
                            for ch in range(NCH):
                                ins = e.matmul(ps[bset + ch][:, 0:TC],
                                               s_wt[wti][:],
                                               rw[:, ch * TC:(ch + 1) * TC],
                                               start=st, stop=sp)
                        return ins
                    add("tensor", mmi, reads=[wtb, bA, bG],
                        writes=[bps[bset + ch] for ch in range(NCH)])
            for ch in range(NCH):
                add("scalar", lambda e, mt=mt, bset=bset, ch=ch: e.copy(
                    out=Bb[:, mt * T + ch * TC:mt * T + (ch + 1) * TC],
                    in_=ps[bset + ch][:, 0:TC]),
                    reads=[bps[bset + ch]], writes=[batt])

        def xq(k):
            return Abf[:, k * T:(k + 1) * T]

        bxq = bA
        add("sync", lambda e: e.dma_start(out=s_gmax[:], in_=cc_m1o[:, :]),
            reads=[bf_(cc_m1o)], writes=[bf_(s_gmax)], kind="d", qtag="gm")
        add("scalar", lambda e: e.activation(
            out=s_tmp[:, 6:7], in_=s_clip[:, 0:1], func=ACT.Sigmoid),
            reads=[bf_(s_clip)], writes=[btmp])
        add("vector", lambda e: e.tensor_mul(
            out=s_kx[:], in0=s_tmp[:, 6:7], in1=s_gmax[:]),
            reads=[btmp, bf_(s_gmax)], writes=[bf_(s_kx)])
        add("vector", lambda e: e.tensor_scalar(
            s_sxq[:], s_kx[:], 1.0 / QMAX, None, ALU.mult),
            reads=[bf_(s_kx)], writes=[bf_(s_sxq)])
        add("vector", lambda e: e.tensor_copy(out=s_tmp[:, 4:5], in_=s_kx[:]),
            reads=[bf_(s_kx)], writes=[btmp])
        add("vector", lambda e: e.reciprocal(out=s_kx[:], in_=s_tmp[:, 4:5]),
            reads=[btmp], writes=[bf_(s_kx)])
        vspace()
        add("vector", lambda e: e.tensor_scalar(
            s_kx[:], s_kx[:], QMAX, None, ALU.mult),
            reads=[bf_(s_kx)], writes=[bf_(s_kx)])
        vspace()

        def quant_act(src_ap, out_ap, recip_b, src_b, out_b, width, krecip):
            add("vector", lambda e, w=width: e.tensor_scalar(
                W1[:, 0:w], src_ap, krecip, None, ALU.mult),
                reads=[src_b, recip_b], writes=[bW1])
            add("vector", lambda e, w=width: e.tensor_copy(
                out=W0i[:, 0:w], in_=W1[:, 0:w]), reads=[bW1], writes=[bW0])
            add("vector", lambda e, w=width: e.tensor_copy(
                out=EV[:, 0:w], in_=W0i[:, 0:w]), reads=[bW0], writes=[bEV])
            add("vector", lambda e, w=width: e.scalar_tensor_tensor(
                out=W0[:, 0:w], in0=W1[:, 0:w], scalar=0.5,
                in1=EV[:, 0:w], op0=ALU.subtract, op1=ALU.is_gt),
                reads=[bW1, bEV], writes=[bW0])
            add("vector", lambda e, w=width: e.tensor_add(
                out=EV[:, 0:w], in0=EV[:, 0:w], in1=W0[:, 0:w]),
                reads=[bW0, bEV], writes=[bEV])
            add("vector", lambda e, w=width: e.scalar_tensor_tensor(
                out=W0[:, 0:w], in0=W1[:, 0:w], scalar=0.5,
                in1=EV[:, 0:w], op0=ALU.add, op1=ALU.is_lt),
                reads=[bW1, bEV], writes=[bW0])
            add("vector", lambda e, w=width: e.tensor_sub(
                out=EV[:, 0:w], in0=EV[:, 0:w], in1=W0[:, 0:w]),
                reads=[bW0, bEV], writes=[bEV])
            add("vector", lambda e, w=width, o=out_ap: e.tensor_scalar(
                o, EV[:, 0:w], -(QMAX + 1.0), QMAX, ALU.max, ALU.min),
                reads=[bEV], writes=[out_b])

        for k in range(ND):
            add("sync", lambda e, k=k: e.dma_start(
                out=W2[:, 0:T], in_=d_xnt[k, :, :]),
                reads=[bdxnt], writes=[bW2], kind="d", qtag="xn")
            quant_act(W2[:, 0:T], xq(k), bf_(s_kx), bW2, bxq, T, s_kx[:, 0:1])

        for j in range(8):
            add("sync", lambda e, j=j: e.dma_start(
                out=s_scf[:, j * 7:(j + 1) * 7],
                in_=cc_w2o[j, :, 4480:4487]),
                reads=[bw2o], writes=[bf_(s_scf)], kind="d", qtag="sc")
        add("vector", lambda e: e.tensor_scalar(
            s_sxgu[:, :], s_scf[:, :], s_sxq[:, 0:1], None, ALU.mult),
            reads=[bf_(s_sxq), bf_(s_scf)], writes=[bf_(s_sxgu)])

        def gcol(it):
            return (it // 3) * 7 + (it % 3)

        def ucol(it):
            return (it // 3) * 7 + 3 + (it % 3)

        bhm = bf_(s_hmax)
        bdh0 = Buf()
        bdh1 = Buf()
        first_h = [True]
        wgu_t = [s_qb[0], s_qb[1], s_wgu[0], s_wgu[1]]
        wgu_f = [t.bitcast(F32) for t in wgu_t]
        bwgu = [bf_(t) for t in wgu_t]
        for it in range(NI):
            for wi in range(2):
                sl = (it % 2) * 2 + wi
                add("sync", lambda e, it=it, wi=wi, sl=sl: e.dma_start(
                    out=wgu_f[sl][:, :],
                    in_=cc_w2o[it // 3, :, wi * 1536 + (it % 3) * 512:
                               wi * 1536 + (it % 3) * 512 + 512]),
                    reads=[bw2o], writes=[bwgu[sl]], kind="d", qtag=f"wg{sl}")
            for wi in range(2):
                sl = (it % 2) * 2 + wi

                def mmg(e, wi=wi, sl=sl):
                    ins = None
                    for kt in range(ND):
                        for ch in range(NCH):
                            ins = e.matmul(ps[wi * 4 + ch][:, 0:TC],
                                           wgu_t[sl][:, kt * 128:(kt + 1) * 128],
                                           xq(kt)[:, ch * TC:(ch + 1) * TC],
                                           start=(kt == 0), stop=(kt == ND - 1))
                    return ins
                add("tensor", mmg, reads=[bwgu[sl], bxq],
                    writes=[bps[wi * 4 + ch] for ch in range(NCH)])
            hb = W0 if it % 2 == 0 else EV
            bhb = bW0 if it % 2 == 0 else bEV
            for ch in range(NCH):
                sg = s_sgs[:, (ch % 2) * TC:(ch % 2) * TC + TC]
                add("scalar", lambda e, it=it, ch=ch, sg=sg: e.activation(
                    out=sg, in_=ps[ch][:, 0:TC], func=ACT.Sigmoid,
                    scale=s_sxgu[:, gcol(it):gcol(it) + 1]),
                    reads=[bps[ch], bf_(s_sxgu)], writes=[bf_(s_sgs)])
                add("vector", lambda e, it=it, ch=ch, sg=sg: e.scalar_tensor_tensor(
                    out=W1[:, ch * TC:(ch + 1) * TC], in0=ps[ch][:, 0:TC],
                    scalar=s_sxgu[:, gcol(it):gcol(it) + 1],
                    in1=sg, op0=ALU.mult, op1=ALU.mult),
                    reads=[bps[ch], bf_(s_sgs), bf_(s_sxgu)], writes=[bW1])
            for ch in range(NCH):
                add("vector", lambda e, it=it, ch=ch, hb=hb: e.scalar_tensor_tensor(
                    out=hb[:, ch * TC:(ch + 1) * TC], in0=ps[4 + ch][:, 0:TC],
                    scalar=s_sxgu[:, ucol(it):ucol(it) + 1],
                    in1=W1[:, ch * TC:(ch + 1) * TC],
                    op0=ALU.mult, op1=ALU.mult),
                    reads=[bps[4 + ch], bW1, bf_(s_sxgu)], writes=[bhb])
            add("vector", lambda e, hb=hb: e.reduce_max(
                out=s_tmp[:, 7:8], in_=hb[:, 0:T], axis=AX.X,
                apply_absolute_value=True),
                reads=[bhb], writes=[btmp])
            vspace()
            if first_h[0]:
                add("vector", lambda e: e.tensor_copy(out=s_hmax[:], in_=s_tmp[:, 7:8]),
                    reads=[btmp], writes=[bhm])
                first_h[0] = False
            else:
                add("vector", lambda e: e.tensor_max(
                    out=s_hmax[:], in0=s_hmax[:], in1=s_tmp[:, 7:8]),
                    reads=[btmp, bhm], writes=[bhm])
            add("sync", lambda e, it=it, hb=hb: e.dma_start(
                out=d_h[it, :, :], in_=hb[:, 0:T]),
                reads=[bhb], writes=[bdh0 if it % 2 == 0 else bdh1],
                kind="d", qtag=f"h{it % 2}")

        add("vector", lambda e: e.tensor_scalar(
            s_bc[:], s_id[:], 0.0, s_hmax[:, 0:1], ALU.mult, ALU.add),
            reads=[bid, bhm], writes=[bf_(s_bc)])
        add("tensor", lambda e: e.transpose(ps[0][:, 0:128], s_bc[:], s_id[:]),
            reads=[bf_(s_bc), bid], writes=[bps[0]])
        add("scalar", lambda e: e.copy(out=s_bc[:], in_=ps[0][:, 0:128]),
            reads=[bps[0]], writes=[bf_(s_bc)])
        add("vector", lambda e: e.reduce_max(out=s_hmax[:], in_=s_bc[:], axis=AX.X),
            reads=[bf_(s_bc)], writes=[bhm])
        add("sync", lambda e: e.dma_start(out=cc_m2i[:, :], in_=s_hmax[:]),
            reads=[bhm], writes=[bf_(cc_m2i)], kind="d", qtag="upm")
        add("gpsimd", lambda e: e.collective_compute(
            "AllReduce", ALU.max, replica_groups=RG,
            ins=[cc_m2i[:, :]], outs=[cc_m2o[:, :]]),
            reads=[bf_(cc_m2i)], writes=[bf_(cc_m2o)], kind="cc")
        add("sync", lambda e: e.dma_start(out=s_hmax[:], in_=cc_m2o[:, :]),
            reads=[bf_(cc_m2o)], writes=[bhm], kind="d", qtag="gm")
        add("scalar", lambda e: e.activation(
            out=s_tmp[:, 6:7], in_=s_clip[:, 1:2], func=ACT.Sigmoid),
            reads=[bf_(s_clip)], writes=[btmp])
        add("vector", lambda e: e.tensor_mul(
            out=s_kh[:], in0=s_tmp[:, 6:7], in1=s_hmax[:]),
            reads=[btmp, bhm], writes=[bf_(s_kh)])
        add("vector", lambda e: e.tensor_scalar(
            s_shq[:], s_kh[:], 1.0 / QMAX, None, ALU.mult),
            reads=[bf_(s_kh)], writes=[bf_(s_shq)])
        add("vector", lambda e: e.tensor_copy(out=s_tmp[:, 4:5], in_=s_kh[:]),
            reads=[bf_(s_kh)], writes=[btmp])
        add("vector", lambda e: e.reciprocal(out=s_kh[:], in_=s_tmp[:, 4:5]),
            reads=[btmp], writes=[bf_(s_kh)])
        vspace()
        add("vector", lambda e: e.tensor_scalar(
            s_kh[:], s_kh[:], QMAX, None, ALU.mult),
            reads=[bf_(s_kh)], writes=[bf_(s_kh)])
        vspace()
        add("vector", lambda e: e.tensor_scalar(
            s_shdall[:, :], s_scf[:, :], s_shq[:, 0:1], None, ALU.mult),
            reads=[bf_(s_shq), bf_(s_scf)], writes=[bf_(s_shdall)])

        def hq(kt):
            return Gbf[:, kt * TC:(kt + 1) * TC]

        bhq = bG
        boutd = bA
        for ch in range(NCH):
            for kt in range(NI):
                add("sync", lambda e, kt=kt, ch=ch: e.dma_start(
                    out=W2[:, 0:TC], in_=d_h[kt, :, ch * TC:(ch + 1) * TC]),
                    reads=[bdh0, bdh1], writes=[bW2], kind="d", qtag="hr")
                quant_act(W2[:, 0:TC], hq(kt), bf_(s_kh), bW2, bhq, TC, s_kh[:, 0:1])
            for mt in range(ND):
                bnk = mt % 8
                wdst = W0 if mt % 2 == 0 else W1
                wdstb = W0bf if mt % 2 == 0 else W1bf
                bwd = bW0 if mt % 2 == 0 else bW1
                add("sync", lambda e, mt=mt, wdst=wdst: e.dma_start(
                    out=wdst[:, 0:1408],
                    in_=cc_w2o[mt, :, 3072:4480]),
                    reads=[bw2o], writes=[bwd], kind="d", qtag=f"wd{mt % 2}")

                def mmd(e, bnk=bnk, wdstb=wdstb):
                    ins = None
                    for kt in range(NI):
                        ins = e.matmul(ps[bnk][:, 0:TC],
                                       wdstb[:, kt * 128:(kt + 1) * 128],
                                       hq(kt), start=(kt == 0), stop=(kt == NI - 1))
                    return ins
                add("tensor", mmd, reads=[bwd, bhq], writes=[bps[bnk]])
                add("vector", lambda e, mt=mt, bnk=bnk, ch=ch: e.scalar_tensor_tensor(
                    out=A[:, mt * T + ch * TC:mt * T + (ch + 1) * TC],
                    in0=ps[bnk][:, 0:TC], scalar=s_shdall[:, mt * 7 + 6:mt * 7 + 7],
                    in1=Bb[:, mt * T + ch * TC:mt * T + (ch + 1) * TC],
                    op0=ALU.mult, op1=ALU.add),
                    reads=[bps[bnk], bf_(s_shdall), batt], writes=[boutd])

        bout = bf_(out_t)
        for tt in range(NT):
            buf = W0 if tt % 2 == 0 else W1
            bb = bW0 if tt % 2 == 0 else bW1
            add("sync", lambda e, tt=tt, buf=buf: e.dma_start(
                out=buf[:, 0:D], in_=x_in[tt * 128:(tt + 1) * 128, :]),
                writes=[bb], kind="d", qtag=f"x{tt % 2}")
            for k in range(ND):
                bnk = (tt * ND + k) % 2
                add("tensor", lambda e, tt=tt, k=k, bnk=bnk: e.transpose(
                    ps[bnk][:, 0:128], A[:, k * T + 128 * tt:k * T + 128 * (tt + 1)],
                    s_id[:]),
                    reads=[boutd, bid], writes=[bps[bnk]])
                add("vector", lambda e, k=k, bnk=bnk, buf=buf: e.tensor_add(
                    out=W2[:, 128 * k:128 * (k + 1)], in0=ps[bnk][:, 0:128],
                    in1=buf[:, 128 * k:128 * (k + 1)]),
                    reads=[bps[bnk], bb], writes=[bW2])
            add("sync", lambda e, tt=tt: e.dma_start(
                out=out_t[tt * 128:(tt + 1) * 128, :], in_=W2[:, 0:D]),
                reads=[bW2], writes=[bout], kind="d", qtag="out")
        if debug:
            add("sync", lambda e: e.dma_start(
                out=d_mxdbg[0, :, :], in_=cc_m1o[:, :]),
                reads=[bf_(cc_m1o)], writes=[bf_(d_mxdbg)], kind="d")
            add("sync", lambda e: e.dma_start(
                out=d_mxdbg[1, :, :], in_=cc_m2o[:, :]),
                reads=[bf_(cc_m2o)], writes=[bf_(d_mxdbg)], kind="d")
            add("sync", lambda e: e.dma_start(out=d_w2dbg[:, :], in_=cc_w2i[:, :]),
                reads=[bw2i] + bw2s, writes=[bf_(d_w2dbg)], kind="d")
            for j in range(8):
                add("sync", lambda e, j=j: e.dma_start(
                    out=d_w2odbg[j, :, :], in_=cc_w2o[j, :, 0:561]),
                    reads=[bw2o], writes=[bf_(d_w2odbg)], kind="d")
        add("sync", lambda e: e.nop(), reads=[bout])

        P.emit(nc, es)
    return nc


def kernel(**inputs):
    x = np.ascontiguousarray(
        np.asarray(inputs["L0_attn_x"], dtype=np.float32).reshape(8 * T, D))
    dft, ctm, idn = _host_consts()
    perms = np.asarray(inputs["L0_attn_perms"], dtype=np.int32)
    permspad = np.concatenate(
        [perms, np.full((C, FP - perms.shape[1]), 513, dtype=np.int32)], axis=1)
    clip_x = float(np.asarray(inputs["L0_ffn_clip_x"]).ravel()[0])
    clip_d = float(np.asarray(inputs["L0_ffn_clip_down"]).ravel()[0])
    clip2 = np.tile(np.array([[clip_x, clip_d]], dtype=np.float32), (128, 1))

    if "nc" not in _CACHE:
        _CACHE["nc"] = build_program()
    nc = _CACHE["nc"]

    wq = np.asarray(inputs["L0_attn_Wq"], dtype=np.float32)
    wk = np.asarray(inputs["L0_attn_Wk"], dtype=np.float32)
    wv = np.asarray(inputs["L0_attn_Wv"], dtype=np.float32)
    wo = np.asarray(inputs["L0_attn_Wo"], dtype=np.float32)
    wg = np.asarray(inputs["L0_ffn_w_gate"], dtype=np.float32)
    wu = np.asarray(inputs["L0_ffn_w_up"], dtype=np.float32)
    wd = np.asarray(inputs["L0_ffn_w_down"], dtype=np.float32)
    pad = np.ones((3 * 8 * 128 - wg.shape[0], D), dtype=np.float32)
    wgp = np.concatenate([wg, pad], axis=0)
    wup = np.concatenate([wu, pad], axis=0)

    common = dict(clip2=clip2, dft=dft, ct=ctm, idn=idn, permspad=permspad)
    in_maps = []
    for r in range(8):
        mask = np.zeros((128, 8), dtype=np.float32)
        for j in range(4):
            if j < r % 4:
                mask[:, 2 * j] = 1.0
                mask[:, 2 * j + 1] = 1.0
        m = dict(common)
        m["x_in"] = np.ascontiguousarray(x[r * T:(r + 1) * T])
        m["mask16"] = mask
        m["wq_s"] = np.ascontiguousarray(wq[:, r * 128:(r + 1) * 128])
        m["wk_s"] = np.ascontiguousarray(wk[:, r * 128:(r + 1) * 128])
        m["wv_s"] = np.ascontiguousarray(wv[:, r * 128:(r + 1) * 128])
        m["wo_s"] = np.ascontiguousarray(wo[r * 128:(r + 1) * 128, :])
        m["wg_s"] = np.ascontiguousarray(wgp[r * 384:(r + 1) * 384])
        m["wu_s"] = np.ascontiguousarray(wup[r * 384:(r + 1) * 384])
        m["wd_s"] = np.ascontiguousarray(wd[r * 128:(r + 1) * 128, :])
        in_maps.append(m)

    res = run_bass_kernel_spmd(nc, in_maps, core_ids=list(range(8)))
    _CACHE["last_results"] = res.results
    out = np.concatenate([res.results[r]["out_t"] for r in range(8)], axis=0)
    return out.reshape(2, 8192, D).astype(np.float32)


# version 32
# speedup vs baseline: 1.0095x; 1.0095x over previous
import contextlib
import numpy as np
import ml_dtypes
import concourse.bass as bass
import concourse.mybir as mybir
from concourse.bass_core_rust import IndirectOffsetOnAxis
from concourse.bass_utils import run_bass_kernel_spmd

F32 = mybir.dt.float32
I32 = mybir.dt.int32
BF16 = mybir.dt.bfloat16
AX = mybir.AxisListType
ALU = mybir.AluOpType
ACT = mybir.ActivationFunctionType

D = 1024
T = 2048
NT = T // 128
ND = D // 128
FP = 640
NFT = FP // 128
FR = 2 * FP
C = 10
NU = NFT * C
II = 2816
NI = II // 128
NCH = 4
TC = T // NCH
QMAX = 127.0
EPS = 1e-5
RG = [[0, 1, 2, 3, 4, 5, 6, 7]]
RG4 = [[0, 1, 2, 3], [4, 5, 6, 7]]
ENGS = ("sync", "vector", "scalar", "tensor", "gpsimd")

_CACHE = {}


class Op:
    __slots__ = ("eng", "fn", "deps", "kind", "sem_inc", "tok", "qtag")

    def __init__(self, eng, fn, deps, kind, qtag=""):
        self.eng, self.fn, self.deps, self.kind = eng, fn, deps, kind
        self.sem_inc = 16 if kind == "d" else 1
        self.tok = None
        self.qtag = qtag


class Buf:
    __slots__ = ("w", "r")

    def __init__(self):
        self.w = None
        self.r = []


class Prog:
    def __init__(self):
        self.ops = {e: [] for e in ENGS}

    def add(self, eng, fn, reads=(), writes=(), kind="c", qtag=""):
        deps = []
        for b in reads:
            if b.w is not None:
                deps.append(b.w)
        for b in writes:
            if b.w is not None:
                deps.append(b.w)
            deps.extend(b.r)
        op = Op(eng, fn, deps, kind, qtag)
        self.ops[eng].append(op)
        for b in reads:
            b.r.append(op)
        for b in writes:
            b.w = op
            b.r = []
        return op

    def emit(self, nc, es):
        signaled = set()
        for e in ENGS:
            for op in self.ops[e]:
                for dp in op.deps:
                    if dp.eng != op.eng or dp.kind != "c":
                        signaled.add(id(dp))
        streams = {}

        def stream(key):
            if key not in streams:
                nm = f"q_{key[0]}_{key[1]}_{key[2]}"[:28]
                streams[key] = [es.enter_context(nc.semaphore(nm)), 0]
            return streams[key]

        for e in ENGS:
            for op in self.ops[e]:
                if id(op) in signaled or op.kind != "c":
                    st = stream((op.eng, op.kind, op.qtag))
                    st[1] += op.sem_inc
                    op.tok = (st[0], st[1])

        with nc.Block() as b:
            def make(e):
                def body(eng_obj):
                    for op in self.ops[e]:
                        seen = {}
                        for dp in op.deps:
                            if dp.eng == op.eng and dp.kind == "c":
                                continue
                            if dp.tok is None:
                                continue
                            sem, val = dp.tok
                            if id(sem) not in seen or seen[id(sem)][1] < val:
                                seen[id(sem)] = (sem, val)
                        for sem, val in seen.values():
                            eng_obj.wait_ge(sem, val)
                        ins = op.fn(eng_obj)
                        if op.tok is not None:
                            assert ins is not None, f"{e} op returned None but needs signal"
                            ins.then_inc(op.tok[0], op.sem_inc)
                return body
            for e in ENGS:
                if self.ops[e]:
                    getattr(b, e)(make(e))


def _host_consts():
    dd = np.arange(D)[:, None].astype(np.float64)
    ff = np.arange(FP)[None, :].astype(np.float64)
    ang = 2.0 * np.pi * (dd * ff) / D
    valid = (ff < 513).astype(np.float64)
    cosm = np.cos(ang) * valid
    sinm = np.sin(ang) * valid
    dft = np.concatenate([cosm, -sinm], axis=1).astype(np.float32)
    w = np.where((ff == 0) | (ff == 512), 1.0, 2.0) * valid
    ctm = np.concatenate([(w * cosm) / (D * C), -(w * sinm) / (D * C)],
                         axis=1).astype(np.float32)
    idn = np.eye(128, dtype=np.float32)
    return dft, ctm, idn


def build_program(debug=False):
    ik = "ExternalOutput" if debug else "Internal"
    nc = bass.Bass("TRN2", target_bir_lowering=False, debug=False)

    x_in = nc.dram_tensor("x_in", [T, D], F32, kind="ExternalInput").ap()
    w_s = [nc.dram_tensor(n, [D, 128], F32, kind="ExternalInput").ap()
           for n in ("wq_s", "wk_s", "wv_s")]
    wo_s = nc.dram_tensor("wo_s", [128, D], F32, kind="ExternalInput").ap()
    wgu_s = [nc.dram_tensor(n, [384, D], F32, kind="ExternalInput").ap()
             for n in ("wg_s", "wu_s")]
    wd_s = nc.dram_tensor("wd_s", [128, II], F32, kind="ExternalInput").ap()
    clip2 = nc.dram_tensor("clip2", [128, 2], F32, kind="ExternalInput").ap()
    dft_in = nc.dram_tensor("dft", [D, FR], F32, kind="ExternalInput").ap()
    ct_in = nc.dram_tensor("ct", [D, FR], F32, kind="ExternalInput").ap()
    idn_in = nc.dram_tensor("idn", [128, 128], F32, kind="ExternalInput").ap()
    permspad = nc.dram_tensor("permspad", [C, FP], I32, kind="ExternalInput").ap()
    mask16_in = nc.dram_tensor("mask16", [128, 8], F32, kind="ExternalInput").ap()
    out_t = nc.dram_tensor("out_t", [T, D], F32, kind="ExternalOutput").ap()

    d_fqk = nc.dram_tensor("d_fqk", [FP, 4, T], F32, kind=ik).ap()
    d_fv = nc.dram_tensor("d_fv", [2 * NFT, 128, T], F32, kind=ik).ap()
    d_xnt = nc.dram_tensor("d_xnt", [ND, 128, T], F32, kind=ik).ap()
    d_r = nc.dram_tensor("d_r", [2 * NFT, 128, T], F32, kind=ik).ap()
    d_h = nc.dram_tensor("d_h", [NI, 128, T], F32, kind=ik).ap()
    cc_w1ai = nc.dram_tensor("cc_w1ai", [128, 3840], F32, kind="Internal").ap()
    cc_w1oa = nc.dram_tensor("cc_w1oa", [8, 128, 3840], F32, kind="Internal").ap()
    cc_w1bi = nc.dram_tensor("cc_w1bi", [128, 1280], F32, kind="Internal").ap()
    cc_w1ob = nc.dram_tensor("cc_w1ob", [8, 128, 1280], F32, kind="Internal").ap()
    cc_w2i = nc.dram_tensor("cc_w2i", [128, 4488], F32, kind="Internal").ap()
    cc_w2o = nc.dram_tensor("cc_w2o", [8, 128, 4488], F32, kind="Internal").ap()
    cc_m1i = nc.dram_tensor("cc_m1i", [128, 1], F32, kind="Internal").ap()
    cc_m1o = nc.dram_tensor("cc_m1o", [128, 1], F32, kind="Internal").ap()
    cc_m2i = nc.dram_tensor("cc_m2i", [128, 1], F32, kind="Internal").ap()
    cc_m2o = nc.dram_tensor("cc_m2o", [128, 1], F32, kind="Internal").ap()
    cc_si = [nc.dram_tensor(f"cc_si{p}", [128, 2], F32, kind="Internal").ap()
             for p in range(3)]
    cc_so = [nc.dram_tensor(f"cc_so{p}", [4, 128, 2], F32, kind="Internal").ap()
             for p in range(3)]
    d_off = nc.dram_tensor("d_off", [NU, 128, 2], F32, kind=ik).ap()
    d_slot = nc.dram_tensor("d_slot", [NU, 128, 8], F32, kind=ik).ap()
    d_mpr = nc.dram_tensor("d_mpr", [NU, 128, 8], F32, kind=ik).ap()
    d_w2dbg = nc.dram_tensor("d_w2dbg", [128, 4488], F32, kind=ik).ap()
    d_mxdbg = nc.dram_tensor("d_mxdbg", [2, 128, 1], F32, kind=ik).ap()
    d_w2odbg = nc.dram_tensor("d_w2odbg", [8, 128, 561], F32, kind=ik).ap()

    es = contextlib.ExitStack()
    with es:
        def sbt(name, shape, dt=F32):
            return es.enter_context(nc.sbuf_tensor(name, shape, dt))

        A = sbt("A", [128, 16384])
        Bb = sbt("Bb", [128, 16384])
        G = sbt("G", [128, 8192])
        W0 = sbt("W0", [128, 2048])
        W1 = sbt("W1", [128, 2048])
        W2 = sbt("W2", [128, 2048])
        EV = sbt("EV", [128, 2048])
        s_id = sbt("s_id", [128, 128])
        s_clip = sbt("s_clip", [128, 2])
        s_mask = sbt("s_mask", [128, 8])
        s_idx = [sbt(f"s_idx{p}", [128, 1], I32) for p in range(3)]
        s_slot = [sbt(f"s_slot{p}", [128, 8]) for p in range(2)]
        s_mpr = sbt("s_mpr", [128, 8])
        s_stg = [sbt(f"s_stg{p}", [128, 2]) for p in range(3)]
        s_off = sbt("s_off", [128, 2])
        s_tmp = sbt("s_tmp", [128, 8])
        s_rmax = sbt("s_rmax", [128, 1])
        s_gmax = sbt("s_gmax", [128, 1])
        s_hmax = sbt("s_hmax", [128, 1])
        s_kx = sbt("s_kx", [128, 1])
        s_kh = sbt("s_kh", [128, 1])
        s_sxq = sbt("s_sxq", [128, 1])
        s_shq = sbt("s_shq", [128, 1])
        s_mysc = sbt("s_mysc", [128, 8])
        s_scf = sbt("s_scf", [128, 56])
        s_sxgu = sbt("s_sxgu", [128, 56])
        s_shdall = sbt("s_shdall", [128, 56])
        s_bc = sbt("s_bc", [128, 128])
        s_sgs = sbt("s_sgs", [128, 2 * TC])
        s_wt = [sbt(f"s_wt{p}", [128, 128], BF16) for p in range(4)]
        s_qb = [sbt(f"s_qb{p}", [128, 1024], BF16) for p in range(2)]
        s_wgu = [sbt(f"s_wgu{p}", [128, 1024], BF16) for p in range(2)]

        ps = [es.enter_context(nc.psum_tensor(f"ps{i}", [128, TC], F32))
              for i in range(8)]

        Abf = A.bitcast(BF16)
        Gbf = G.bitcast(BF16)
        Bbf = Bb.bitcast(BF16)
        Bi32 = Bb.bitcast(I32)
        W0i = W0.bitcast(I32)
        W0bf = W0.bitcast(BF16)
        W1bf = W1.bitcast(BF16)

        P = Prog()
        BUFS = {}

        def bf_(t):
            if id(t) not in BUFS:
                BUFS[id(t)] = Buf()
            return BUFS[id(t)]

        bA, bB, bG, bW0, bW1, bW2, bEV = (bf_(x) for x in (A, Bb, G, W0, W1, W2, EV))
        bps = [bf_(p_) for p_ in ps]
        bid = bf_(s_id)
        btmp = bf_(s_tmp)

        s_jnk = sbt("s_jnk", [128, 16])

        def add(eng, fn, reads=(), writes=(), kind="c", qtag=""):
            return P.add(eng, fn, reads, writes, kind, qtag)

        def vspace():
            P.add("vector", lambda e: e.tensor_copy(out=s_jnk[:, 0:1],
                                                    in_=s_id[:, 0:1]), (), (), "c")

        def split_hilo(dst_h, dst_l, src, tmp, rbs, wbs, tmpb):
            add("vector", lambda e, a=dst_h, b=src: e.tensor_copy(out=a, in_=b),
                reads=rbs, writes=wbs)
            add("vector", lambda e, a=tmp, b=dst_h: e.tensor_copy(out=a, in_=b),
                reads=wbs, writes=[tmpb])
            add("vector", lambda e, a=tmp, b=src, c_=tmp: e.tensor_sub(out=a, in0=b, in1=c_),
                reads=rbs + [tmpb], writes=[tmpb])
            add("vector", lambda e, a=dst_l, b=tmp: e.tensor_copy(out=a, in_=b),
                reads=[tmpb], writes=wbs)

        add("sync", lambda e: e.dma_start(out=s_id[:], in_=idn_in[:, :]),
            writes=[bid], kind="d", qtag="i0")
        add("sync", lambda e: e.dma_start(out=s_clip[:], in_=clip2[:, :]),
            writes=[bf_(s_clip)], kind="d", qtag="i1")
        add("sync", lambda e: e.dma_start(out=s_mask[:], in_=mask16_in[:, :]),
            writes=[bf_(s_mask)], kind="d", qtag="i2")

        for tt in range(NT):
            buf = W0 if tt % 2 == 0 else W1
            bb = bW0 if tt % 2 == 0 else bW1
            add("sync", lambda e, buf=buf, tt=tt: e.dma_start(
                out=buf[:, 0:D], in_=x_in[tt * 128:(tt + 1) * 128, :]),
                writes=[bb], kind="d", qtag=f"x{tt % 2}")
            add("vector", lambda e, buf=buf: e.tensor_reduce(
                out=s_tmp[:, 0:1], in_=buf[:, 0:D], axis=AX.X, op=ALU.add),
                reads=[bb], writes=[btmp])
            vspace()
            add("vector", lambda e: e.tensor_scalar(
                s_tmp[:, 0:1], s_tmp[:, 0:1], 1.0 / D, None, ALU.mult),
                reads=[btmp], writes=[btmp])
            vspace()
            add("vector", lambda e, buf=buf: e.tensor_scalar(
                buf[:, 0:D], buf[:, 0:D], s_tmp[:, 0:1], None, ALU.subtract),
                reads=[bb, btmp], writes=[bb])
            add("vector", lambda e, buf=buf: e.tensor_mul(
                out=buf[:, D:2 * D], in0=buf[:, 0:D], in1=buf[:, 0:D]),
                reads=[bb], writes=[bb])
            add("vector", lambda e, buf=buf: e.tensor_reduce(
                out=s_tmp[:, 1:2], in_=buf[:, D:2 * D], axis=AX.X, op=ALU.add),
                reads=[bb], writes=[btmp])
            vspace()
            add("vector", lambda e: e.tensor_scalar(
                s_tmp[:, 1:2], s_tmp[:, 1:2], 1.0 / D, EPS, ALU.mult, ALU.add),
                reads=[btmp], writes=[btmp])
            add("scalar", lambda e: e.activation(
                out=s_tmp[:, 4:5], in_=s_tmp[:, 1:2], func=ACT.Sqrt),
                reads=[btmp], writes=[btmp])
            add("vector", lambda e: e.reciprocal(
                out=s_tmp[:, 2:3], in_=s_tmp[:, 4:5]),
                reads=[btmp], writes=[btmp])
            vspace()
            add("scalar", lambda e, buf=buf: e.activation(
                out=buf[:, D:2 * D], in_=buf[:, 0:D], func=ACT.Copy,
                scale=s_tmp[:, 2:3]),
                reads=[bb, btmp], writes=[bb])
            if tt == 0:
                add("vector", lambda e, buf=buf: e.reduce_max(
                    out=s_rmax[:], in_=buf[:, D:2 * D], axis=AX.X,
                    apply_absolute_value=True),
                    reads=[bb], writes=[bf_(s_rmax)])
                vspace()
            else:
                add("vector", lambda e, buf=buf: e.reduce_max(
                    out=s_tmp[:, 3:4], in_=buf[:, D:2 * D], axis=AX.X,
                    apply_absolute_value=True),
                    reads=[bb], writes=[btmp])
                vspace()
                add("vector", lambda e: e.tensor_max(
                    out=s_rmax[:], in0=s_rmax[:], in1=s_tmp[:, 3:4]),
                    reads=[btmp, bf_(s_rmax)], writes=[bf_(s_rmax)])
            for k in range(ND):
                bnk = (tt * ND + k) % 2
                add("tensor", lambda e, buf=buf, k=k, bnk=bnk: e.transpose(
                    ps[bnk][:, 0:128], buf[:, D + 128 * k:D + 128 * (k + 1)], s_id[:]),
                    reads=[bb, bid], writes=[bps[bnk]])
                add("scalar", lambda e, tt=tt, k=k, bnk=bnk: e.copy(
                    out=A[:, k * T + 128 * tt:k * T + 128 * (tt + 1)],
                    in_=ps[bnk][:, 0:128]),
                    reads=[bps[bnk]], writes=[bA])

        bdxnt = bf_(d_xnt)
        for k in range(ND):
            add("sync", lambda e, k=k: e.dma_start(
                out=d_xnt[k, :, :], in_=A[:, k * T:(k + 1) * T]),
                reads=[bA], writes=[bdxnt], kind="d", qtag="xn")

        add("vector", lambda e: e.tensor_scalar(
            s_bc[:], s_id[:], 0.0, s_rmax[:, 0:1], ALU.mult, ALU.add),
            reads=[bid, bf_(s_rmax)], writes=[bf_(s_bc)])
        add("tensor", lambda e: e.transpose(ps[0][:, 0:128], s_bc[:], s_id[:]),
            reads=[bf_(s_bc), bid], writes=[bps[0]])
        add("scalar", lambda e: e.copy(out=s_bc[:], in_=ps[0][:, 0:128]),
            reads=[bps[0]], writes=[bf_(s_bc)])
        add("vector", lambda e: e.reduce_max(out=s_rmax[:], in_=s_bc[:], axis=AX.X),
            reads=[bf_(s_bc)], writes=[bf_(s_rmax)])
        add("sync", lambda e: e.dma_start(out=cc_m1i[:, :], in_=s_rmax[:]),
            reads=[bf_(s_rmax)], writes=[bf_(cc_m1i)], kind="d", qtag="upm")
        add("gpsimd", lambda e: e.collective_compute(
            "AllReduce", ALU.max, replica_groups=RG,
            ins=[cc_m1i[:, :]], outs=[cc_m1o[:, :]]),
            reads=[bf_(cc_m1i)], writes=[bf_(cc_m1o)], kind="cc")

        def dfth(k):
            return Bbf[:, k * FR:(k + 1) * FR]

        def dftl(k):
            return Bbf[:, 10240 + k * FR:10240 + (k + 1) * FR]

        def wsh(k):
            return Bbf[:, 20480 + k * 128:20480 + (k + 1) * 128]

        def wsl(k):
            return Bbf[:, 21504 + k * 128:21504 + (k + 1) * 128]

        bw1as = [Buf(), Buf()]
        bw1oa = bf_(cc_w1oa)
        bw1bs = [Buf(), Buf()]
        bw1ob = bf_(cc_w1ob)
        bw2i = bf_(cc_w2i)
        bw2s = [Buf() for _ in range(5)]
        bw2o = bf_(cc_w2o)
        s_qbf = [q.bitcast(F32) for q in s_qb]
        s_wtf = [w.bitcast(F32) for w in s_wt]

        for k in range(ND):
            stgb = W2 if k % 2 == 0 else EV
            bstgb = bW2 if k % 2 == 0 else bEV
            add("sync", lambda e, k=k, stgb=stgb: e.dma_start(
                out=stgb[:, 0:FR], in_=dft_in[k * 128:(k + 1) * 128, :]),
                writes=[bstgb], kind="d", qtag=f"pw{k % 2}")
            split_hilo(dfth(k), dftl(k), stgb[:, 0:FR], W1[:, 0:FR],
                       [bstgb], [bB], bW1)

        NCS = [(0, 512), (512, 512), (1024, 256)]

        for ti in range(3):
            add("sync", lambda e, ti=ti: e.dma_start(
                out=W2[:, 0:1024].rearrange("p (k c) -> p k c", k=8),
                in_=w_s[ti].rearrange("(k p) c -> p k c", k=8)),
                writes=[bW2], kind="d")
            for k in range(ND):
                split_hilo(wsh(k), wsl(k), W2[:, k * 128:(k + 1) * 128],
                           W1[:, 0:128], [bW2], [bB], bW1)
            for j, (noff, nsz) in enumerate(NCS):
                bnk = j % 2

                def mmq(e, noff=noff, nsz=nsz, bnk=bnk):
                    ins = None
                    for k in range(ND):
                        lh = wsh(k)
                        ll = wsl(k)
                        rh = dfth(k)[:, noff:noff + nsz]
                        rl = dftl(k)[:, noff:noff + nsz]
                        ins = e.matmul(ps[bnk][:, 0:nsz], lh, rh,
                                       start=(k == 0), stop=False)
                        ins = e.matmul(ps[bnk][:, 0:nsz], lh, rl,
                                       start=False, stop=False)
                        ins = e.matmul(ps[bnk][:, 0:nsz], ll, rh,
                                       start=False, stop=(k == ND - 1))
                    return ins
                add("tensor", mmq, reads=[bB], writes=[bps[bnk]])
                add("scalar", lambda e, bnk=bnk, nsz=nsz: e.copy(
                    out=EV[:, bnk * 1024:bnk * 1024 + nsz],
                    in_=ps[bnk][:, 0:nsz]),
                    reads=[bps[bnk]], writes=[bEV])
                qb = s_qb[bnk]
                qbf = s_qbf[bnk]
                bqb = bf_(qb)
                split_hilo(qb[:, 0:nsz], qb[:, 512:512 + nsz],
                           EV[:, bnk * 1024:bnk * 1024 + nsz],
                           W1[:, 0:nsz], [bEV], [bqb], bW1)
                add("sync", lambda e, ti=ti, noff=noff, nsz=nsz, qbf=qbf:
                    e.dma_start(out=cc_w1ai[:, ti * 1280 + noff // 2:
                                            ti * 1280 + (noff + nsz) // 2],
                                in_=qbf[:, 0:nsz // 2]),
                    reads=[bqb], writes=[bw1as[bnk]], kind="d", qtag=f"w1{bnk}")
                add("sync", lambda e, ti=ti, noff=noff, nsz=nsz, qbf=qbf:
                    e.dma_start(out=cc_w1ai[:, ti * 1280 + 640 + noff // 2:
                                            ti * 1280 + 640 + (noff + nsz) // 2],
                                in_=qbf[:, 256:256 + nsz // 2]),
                    reads=[bqb], writes=[bw1as[bnk]], kind="d", qtag=f"w1{bnk}")

        add("gpsimd", lambda e: e.collective_compute(
            "AllGather", ALU.bypass, replica_groups=RG,
            ins=[cc_w1ai[:, :]], outs=[cc_w1oa[:, :, :]]),
            reads=bw1as, writes=[bw1oa], kind="cc")

        def xh(k):
            return Abf[:, k * T:(k + 1) * T]

        def xl(k):
            return Abf[:, 16384 + k * T:16384 + (k + 1) * T]

        for k in range(ND):
            stgb = W2 if k % 2 == 0 else EV
            bstgb = bW2 if k % 2 == 0 else bEV
            add("sync", lambda e, k=k, stgb=stgb: e.dma_start(
                out=stgb[:, 0:T], in_=d_xnt[k, :, :]),
                reads=[bdxnt], writes=[bstgb], kind="d", qtag=f"xr{k % 2}")
            split_hilo(xh(k), xl(k), stgb[:, 0:T], W1[:, 0:T], [bstgb], [bA], bW1)


        add("sync", lambda e: e.dma_start(out=W2[:, 0:D], in_=wo_s[:, :]),
            writes=[bW2], kind="d")
        for k in range(ND):
            bnk = k % 2
            add("tensor", lambda e, k=k, bnk=bnk: e.transpose(
                ps[bnk][:, 0:128], W2[:, 128 * k:128 * (k + 1)], s_id[:]),
                reads=[bW2, bid], writes=[bps[bnk]])
            add("scalar", lambda e, k=k, bnk=bnk: e.copy(
                out=G[:, k * 128:(k + 1) * 128], in_=ps[bnk][:, 0:128]),
                reads=[bps[bnk]], writes=[bG])

        def woth(k):
            return Gbf[:, 2048 + k * 128:2048 + (k + 1) * 128]

        def wotl(k):
            return Gbf[:, 3072 + k * 128:3072 + (k + 1) * 128]

        for k in range(ND):
            split_hilo(woth(k), wotl(k), G[:, k * 128:(k + 1) * 128],
                       W1[:, 0:128], [bG], [bG], bW1)

        for k in range(ND):
            stgb = W2 if k % 2 == 0 else EV
            bstgb = bW2 if k % 2 == 0 else bEV
            add("sync", lambda e, k=k, stgb=stgb: e.dma_start(
                out=stgb[:, 0:FR], in_=ct_in[k * 128:(k + 1) * 128, :]),
                writes=[bstgb], kind="d", qtag=f"pw{k % 2}")
            split_hilo(dfth(k), dftl(k), stgb[:, 0:FR], W1[:, 0:FR],
                       [bstgb], [bB], bW1)

        for mt in range(2 * NFT):
            bnk = mt % 2

            def mmo(e, mt=mt, bnk=bnk):
                ins = None
                for k in range(ND):
                    lh = dfth(k)[:, mt * 128:(mt + 1) * 128]
                    ll = dftl(k)[:, mt * 128:(mt + 1) * 128]
                    rh = woth(k)
                    rl = wotl(k)
                    ins = e.matmul(ps[bnk][:, 0:128], lh, rh,
                                   start=(k == 0), stop=False)
                    ins = e.matmul(ps[bnk][:, 0:128], lh, rl,
                                   start=False, stop=False)
                    ins = e.matmul(ps[bnk][:, 0:128], ll, rh,
                                   start=False, stop=(k == ND - 1))
                return ins
            add("tensor", mmo, reads=[bB, bG], writes=[bps[bnk]])
            add("scalar", lambda e, bnk=bnk: e.copy(
                out=EV[:, bnk * 1024:bnk * 1024 + 128],
                in_=ps[bnk][:, 0:128]),
                reads=[bps[bnk]], writes=[bEV])
            qb = s_qb[bnk]
            qbf = s_qbf[bnk]
            bqb = bf_(qb)
            split_hilo(qb[:, 0:128], qb[:, 512:640],
                       EV[:, bnk * 1024:bnk * 1024 + 128],
                       W1[:, 0:128], [bEV], [bqb], bW1)
            add("sync", lambda e, mt=mt, qbf=qbf: e.dma_start(
                out=cc_w1bi[:, mt * 64:(mt + 1) * 64],
                in_=qbf[:, 0:64]),
                reads=[bqb], writes=[bw1bs[bnk]], kind="d", qtag=f"w1b{bnk}")
            add("sync", lambda e, mt=mt, qbf=qbf: e.dma_start(
                out=cc_w1bi[:, 640 + mt * 64:640 + (mt + 1) * 64],
                in_=qbf[:, 256:256 + 64]),
                reads=[bqb], writes=[bw1bs[bnk]], kind="d", qtag=f"w1b{bnk}")

        add("gpsimd", lambda e: e.collective_compute(
            "AllGather", ALU.bypass, replica_groups=RG,
            ins=[cc_w1bi[:, :]], outs=[cc_w1ob[:, :, :]]),
            reads=bw1bs, writes=[bw1ob], kind="cc")

        W2i = W2.bitcast(I32)
        bmys = bf_(s_mysc)

        def qsrc(i):
            return G[:, (i % 2) * 2816:(i % 2) * 2816 + 2816]

        qcbuf = [W0, W1]
        bqcb = [bW0, bW1]
        TILES = [(wi, il) for wi in range(2) for il in range(3)] + [(2, 0)]

        def load_qsrc(i):
            if i < 6:
                wi, il = TILES[i]
                add("scalar", lambda e, i=i, wi=wi, il=il: e.dma_start(
                    out=qsrc(i)[:, 0:D],
                    in_=wgu_s[wi][il * 128:(il + 1) * 128, :]),
                    writes=[bG], kind="d", qtag=f"qs{i % 2}")
            else:
                add("scalar", lambda e, i=i: e.dma_start(
                    out=qsrc(i)[:, 0:II], in_=wd_s[:, :]),
                    writes=[bG], kind="d", qtag=f"qs{i % 2}")

        def quant_chunk(i, jc, co, cw):
            qc = qcbuf[jc % 2][:, 0:cw]
            bqc = bqcb[jc % 2]
            add("vector", lambda e, i=i, co=co, cw=cw: e.tensor_scalar(
                EV[:, 0:cw], qsrc(i)[:, co:co + cw], s_tmp[:, 5:6], None,
                ALU.mult),
                reads=[bG, btmp], writes=[bEV])
            add("vector", lambda e, cw=cw: e.tensor_copy(
                out=W2i[:, 0:cw], in_=EV[:, 0:cw]), reads=[bEV], writes=[bW2])
            add("vector", lambda e, cw=cw, qc=qc: e.tensor_copy(
                out=qc, in_=W2i[:, 0:cw]), reads=[bW2], writes=[bqc])
            add("vector", lambda e, cw=cw, qc=qc: e.scalar_tensor_tensor(
                out=W2[:, 0:cw], in0=EV[:, 0:cw], scalar=0.5,
                in1=qc, op0=ALU.subtract, op1=ALU.is_gt),
                reads=[bEV, bqc], writes=[bW2])
            add("vector", lambda e, cw=cw, qc=qc: e.tensor_add(
                out=qc, in0=qc, in1=W2[:, 0:cw]),
                reads=[bW2, bqc], writes=[bqc])
            add("vector", lambda e, cw=cw, qc=qc: e.scalar_tensor_tensor(
                out=W2[:, 0:cw], in0=EV[:, 0:cw], scalar=0.5,
                in1=qc, op0=ALU.add, op1=ALU.is_lt),
                reads=[bEV, bqc], writes=[bW2])
            add("vector", lambda e, cw=cw, qc=qc: e.tensor_sub(
                out=qc, in0=qc, in1=W2[:, 0:cw]),
                reads=[bW2, bqc], writes=[bqc])
            add("vector", lambda e, cw=cw, qc=qc: e.tensor_scalar(
                qc, qc, -(QMAX + 1.0), QMAX, ALU.max, ALU.min),
                reads=[bqc], writes=[bqc])

        def tq_chunk(jc, cw, colbase):
            for kt in range(cw // 128):
                bnk = kt % 2
                add("tensor", lambda e, jc=jc, kt=kt, bnk=bnk: e.transpose(
                    ps[bnk][:, 0:128],
                    qcbuf[jc % 2][:, 128 * kt:128 * (kt + 1)], s_id[:]),
                    reads=[bqcb[jc % 2], bid], writes=[bps[bnk]])
                wtb = bf_(s_wt[kt % 4])
                add("scalar", lambda e, kt=kt, bnk=bnk: e.copy(
                    out=s_wt[kt % 4][:], in_=ps[bnk][:, 0:128]),
                    reads=[bps[bnk]], writes=[wtb])
                add("gpsimd", lambda e, kt=kt, colbase=colbase: e.dma_start(
                    out=cc_w2i[:, colbase + kt * 64:colbase + (kt + 1) * 64],
                    in_=s_wtf[kt % 4][:, :]),
                    reads=[wtb], writes=[bw2s[kt % 4]], kind="d",
                    qtag=f"w2{kt % 4}")

        load_qsrc(0)
        jc = 0
        for i in range(7):
            if i + 1 < 7:
                load_qsrc(i + 1)
            wdt = D if i < 6 else II
            add("vector", lambda e, i=i, wdt=wdt: e.reduce_max(
                out=s_tmp[:, 4:5], in_=qsrc(i)[:, 0:wdt], axis=AX.X,
                apply_absolute_value=True),
                reads=[bG], writes=[btmp])
            vspace()
            sc = s_mysc[:, i:i + 1]
            add("vector", lambda e, sc=sc: e.tensor_scalar(
                sc, s_tmp[:, 4:5], 1.0 / QMAX, None, ALU.mult),
                reads=[btmp], writes=[bmys])
            add("vector", lambda e: e.reciprocal(out=s_tmp[:, 5:6], in_=s_tmp[:, 4:5]),
                reads=[btmp], writes=[btmp])
            vspace()
            add("vector", lambda e: e.tensor_scalar(
                s_tmp[:, 5:6], s_tmp[:, 5:6], QMAX, None, ALU.mult),
                reads=[btmp], writes=[btmp])
            vspace()
            if i < 6:
                wi, il = TILES[i]
                quant_chunk(i, jc, 0, D)
                tq_chunk(jc, D, wi * 1536 + il * 512)
                jc += 1
            else:
                for half in range(2):
                    quant_chunk(i, jc, half * 1408, 1408)
                    tq_chunk(jc, 1408, 3072 + half * 11 * 64)
                    jc += 1
        add("gpsimd", lambda e: e.dma_start(
            out=cc_w2i[:, 4480:4487], in_=s_mysc[:, 0:7]),
            reads=[bmys], writes=[bw2s[4]], kind="d", qtag="w2s")
        add("gpsimd", lambda e: e.collective_compute(
            "AllGather", ALU.bypass, replica_groups=RG,
            ins=[cc_w2i[:, :]], outs=[cc_w2o[:, :, :]]),
            reads=bw2s, writes=[bw2o], kind="cc")

        def wfh(k):
            return Bbf[:, k * FR:(k + 1) * FR]

        def wfl(k):
            return Bbf[:, 10240 + k * FR:10240 + (k + 1) * FR]

        stg_f32 = [Bb[:, 11264:13312], Bb[:, 13312:15360]]
        bsp = [Buf(), Buf()]
        bwf = [Buf() for _ in range(ND)]

        for ti in range(3):
            for k in range(ND):
                add("sync", lambda e, ti=ti, k=k: e.dma_start(
                    out=Bb[:, k * 640:(k + 1) * 640],
                    in_=cc_w1oa[k, :, ti * 1280:ti * 1280 + 640]),
                    reads=[bw1oa], writes=[bwf[k]], kind="d", qtag=f"wf{k}")
                add("sync", lambda e, ti=ti, k=k: e.dma_start(
                    out=Bb[:, 5120 + k * 640:5120 + (k + 1) * 640],
                    in_=cc_w1oa[k, :, ti * 1280 + 640:ti * 1280 + 1280]),
                    reads=[bw1oa], writes=[bwf[k]], kind="d", qtag=f"wf{k}")
            for mt in range(2 * NFT):
                bset = (mt % 2) * 4
                for k in range(ND):
                    def mmx(e, mt=mt, bset=bset, k=k):
                        ins = None
                        lh = wfh(k)[:, mt * 128:(mt + 1) * 128]
                        ll = wfl(k)[:, mt * 128:(mt + 1) * 128]
                        for pp, (lw, rsel) in enumerate(((lh, xh), (lh, xl),
                                                         (ll, xh))):
                            for ch in range(NCH):
                                ins = e.matmul(
                                    ps[bset + ch][:, 0:TC], lw,
                                    rsel(k)[:, ch * TC:(ch + 1) * TC],
                                    start=(k == 0 and pp == 0),
                                    stop=(k == ND - 1 and pp == 2))
                        return ins
                    add("tensor", mmx, reads=[bA, bwf[k]],
                        writes=[bps[bset + ch] for ch in range(NCH)])
                stg = stg_f32[mt % 2]
                for ch in range(NCH):
                    add("scalar", lambda e, stg=stg, bset=bset, ch=ch: e.copy(
                        out=stg[:, ch * TC:(ch + 1) * TC], in_=ps[bset + ch][:, 0:TC]),
                        reads=[bps[bset + ch]], writes=[bB])
                if ti < 2:
                    comp = (0 if ti == 0 else 2) + (0 if mt < NFT else 1)
                    fmt = mt % NFT
                    add("sync", lambda e, comp=comp, fmt=fmt, stg=stg: e.dma_start(
                        out=d_fqk.rearrange("a b c2 -> a (b c2)")[
                            fmt * 128:(fmt + 1) * 128, comp * T:(comp + 1) * T],
                        in_=stg[:, 0:T]),
                        reads=[bB], writes=[bsp[mt % 2]], kind="d",
                        qtag=f"sp{mt % 2}")
                else:
                    add("sync", lambda e, mt=mt, stg=stg: e.dma_start(
                        out=d_fv[mt, :, :], in_=stg[:, 0:T]),
                        reads=[bB], writes=[bsp[mt % 2]], kind="d",
                        qtag=f"sp{mt % 2}")

        fv_re = [Bb[:, 0:T], W2[:, 0:T]]
        fv_im = [Bb[:, T:2 * T], EV[:, 0:T]]
        kv_re = [Bb[:, 2 * T:3 * T], Bb[:, 3 * T:4 * T], Bb[:, 4 * T:5 * T]]
        kv_im = [Bb[:, 5 * T:6 * T], Bb[:, 6 * T:7 * T], Bb[:, 7 * T:8 * T]]
        r_re = A[:, 0:T]
        r_im = A[:, T:2 * T]
        p1 = A[:, 2 * T:3 * T]
        p2 = A[:, 3 * T:4 * T]
        gk_re = [A[:, 4 * T:5 * T], A[:, 6 * T:7 * T], W0[:, 0:T]]
        gk_im = [A[:, 5 * T:6 * T], A[:, 7 * T:8 * T], W1[:, 0:T]]
        gq = [G[:, 0:2 * T], G[:, 2 * T:4 * T]]

        bfv_r = [[bB], [bW2]]
        bfv_i = [[bB], [bEV]]
        bkv = [bB, bB, bB]
        brr = Buf()
        bp12 = Buf()
        bg01 = [Buf(), Buf()]
        bgq = [Buf(), Buf()]
        bgk_r = [[bg01[0]], [bg01[1]], [bW0]]
        bgk_i = [[bg01[0]], [bg01[1]], [bW1]]
        bslot = [bf_(s_slot[p]) for p in range(2)]
        bstg = [bf_(s_stg[p]) for p in range(3)]
        boff = bf_(s_off)
        bmpr = bf_(s_mpr)
        bidx = [bf_(s_idx[p]) for p in range(3)]
        bcsi = [bf_(cc_si[p]) for p in range(3)]
        bcso = [bf_(cc_so[p]) for p in range(3)]
        bdr = bf_(d_r)

        def load_fv(ft):
            fp = ft % 2
            add("sync", lambda e, ft=ft, fp=fp: e.dma_start(
                out=fv_re[fp], in_=d_fv[ft, :, :]),
                reads=bsp, writes=bfv_r[fp] + (bwf if ft == 0 else []),
                kind="d", qtag=f"fv{fp}")
            add("sync", lambda e, ft=ft, fp=fp: e.dma_start(
                out=fv_im[fp], in_=d_fv[NFT + ft, :, :]),
                reads=bsp, writes=bfv_i[fp], kind="d", qtag=f"fv{fp}")

        def load_idx(u):
            ft, c = divmod(u, C)
            u3 = u % 3
            add("sync", lambda e, c=c, ft=ft, u3=u3: e.dma_start(
                out=s_idx[u3][:, :],
                in_=permspad[c:c + 1, ft * 128:(ft + 1) * 128].rearrange(
                    "one p -> p one")),
                writes=[bidx[u3]], kind="d", qtag=f"idx{u3}")

        def issue_k(u, first=False):
            u3 = u % 3
            wr = bgk_r[u3] + ([bA] if first else [])
            wi_ = bgk_i[u3] + ([bA] if first else [])
            add("gpsimd", lambda e, u3=u3: e.indirect_dma_start(
                out=gk_re[u3], out_offset=None,
                in_=d_fqk[0:128, :, :].rearrange("a b c2 -> a (b c2)"),
                in_offset=IndirectOffsetOnAxis(s_idx[u3][:, :], 0),
                element_offset=2 * T),
                reads=bsp + [bidx[u3]], writes=wr,
                kind="d", qtag=f"gk{u3}")
            add("gpsimd", lambda e, u3=u3: e.indirect_dma_start(
                out=gk_im[u3], out_offset=None,
                in_=d_fqk[0:128, :, :].rearrange("a b c2 -> a (b c2)"),
                in_offset=IndirectOffsetOnAxis(s_idx[u3][:, :], 0),
                element_offset=3 * T),
                reads=bsp + [bidx[u3]], writes=wi_,
                kind="d", qtag=f"gk{u3}")

        def issue_q(u, first=False):
            up = u % 2
            u3 = u % 3
            wq = [bgq[up]] + ([bG] if first else [])
            add("gpsimd", lambda e, up=up, u3=u3: e.indirect_dma_start(
                out=gq[up], out_offset=None,
                in_=d_fqk[0:128, :, :].rearrange("a b c2 -> a (b c2)"),
                in_offset=IndirectOffsetOnAxis(s_idx[u3][:, :], 0),
                element_offset=0),
                reads=bsp + [bidx[u3]], writes=wq, kind="d",
                qtag=f"gq{up}")

        def z_stage(u):
            ft, c = divmod(u, C)
            up = u % 2
            u3 = u % 3
            uk = u % 3
            add("sync", lambda e, up=up, u3=u3: e.dma_start(
                out=s_slot[up][:].rearrange("p (j c2) -> p j c2", j=4),
                in_=cc_so[u3][:, :, :].rearrange("j p c2 -> p j c2")),
                reads=[bcso[u3]], writes=[bslot[up]], kind="d", qtag=f"dl{up}")
            add("vector", lambda e, up=up: e.tensor_mul(
                out=s_mpr[:], in0=s_slot[up][:], in1=s_mask[:]),
                reads=[bslot[up], bf_(s_mask)], writes=[bmpr])
            vspace()
            add("vector", lambda e: e.tensor_reduce(
                out=s_off[:, 0:1], in_=s_mpr[:, 0:8:2], axis=AX.X, op=ALU.add),
                reads=[bmpr], writes=[boff])
            add("vector", lambda e: e.tensor_reduce(
                out=s_off[:, 1:2], in_=s_mpr[:, 1:8:2], axis=AX.X, op=ALU.add),
                reads=[bmpr], writes=[boff])
            vspace()
            vspace()
            if debug is True:
                add("sync", lambda e, u=u: e.dma_start(
                    out=d_off[u, :, :], in_=s_off[:]),
                    reads=[boff], writes=[bf_(d_off)], kind="d")
                add("sync", lambda e, u=u, up=up: e.dma_start(
                    out=d_slot[u, :, :], in_=s_slot[up][:]),
                    reads=[bslot[up]], writes=[bf_(d_slot)], kind="d")
                add("sync", lambda e, u=u: e.dma_start(
                    out=d_mpr[u, :, :], in_=s_mpr[:]),
                    reads=[bmpr], writes=[bf_(d_mpr)], kind="d")
            add("vector", lambda e, up=up: e.scalar_tensor_tensor(
                out=p1, in0=kv_re[uk], scalar=s_off[:, 0:1],
                in1=gq[up][:, 0:T], op0=ALU.add, op1=ALU.mult),
                reads=[bkv[uk], boff, bgq[up]], writes=[bp12])
            add("vector", lambda e, up=up: e.scalar_tensor_tensor(
                out=p2, in0=kv_im[uk], scalar=s_off[:, 1:2],
                in1=gq[up][:, T:2 * T], op0=ALU.add, op1=ALU.mult),
                reads=[bkv[uk], boff, bgq[up]], writes=[bp12])
            if c == 0:
                add("vector", lambda e: e.tensor_add(out=r_re, in0=p1, in1=p2),
                    reads=[bp12], writes=[brr])
            else:
                add("vector", lambda e: e.tensor_add(out=r_re, in0=r_re, in1=p1),
                    reads=[bp12, brr], writes=[brr])
                add("vector", lambda e: e.tensor_add(out=r_re, in0=r_re, in1=p2),
                    reads=[bp12, brr], writes=[brr])
            add("vector", lambda e, up=up: e.scalar_tensor_tensor(
                out=p1, in0=kv_im[uk], scalar=s_off[:, 1:2],
                in1=gq[up][:, 0:T], op0=ALU.add, op1=ALU.mult),
                reads=[bkv[uk], boff, bgq[up]], writes=[bp12])
            add("vector", lambda e, up=up: e.scalar_tensor_tensor(
                out=p2, in0=kv_re[uk], scalar=s_off[:, 0:1],
                in1=gq[up][:, T:2 * T], op0=ALU.add, op1=ALU.mult),
                reads=[bkv[uk], boff, bgq[up]], writes=[bp12])
            if c == 0:
                add("vector", lambda e: e.tensor_sub(out=r_im, in0=p1, in1=p2),
                    reads=[bp12], writes=[brr])
            else:
                add("vector", lambda e: e.tensor_add(out=r_im, in0=r_im, in1=p1),
                    reads=[bp12, brr], writes=[brr])
                add("vector", lambda e: e.tensor_sub(out=r_im, in0=r_im, in1=p2),
                    reads=[bp12, brr], writes=[brr])
            if c == C - 1:
                add("sync", lambda e, ft=ft: e.dma_start(
                    out=d_r[ft, :, :], in_=r_re), reads=[brr], writes=[bdr],
                    kind="d", qtag="rsp")
                add("sync", lambda e, ft=ft: e.dma_start(
                    out=d_r[NFT + ft, :, :], in_=r_im),
                    reads=[brr], writes=[bdr], kind="d", qtag="rsp")

        load_fv(0)
        load_idx(0)
        load_idx(1)
        issue_k(0, first=True)
        issue_k(1, first=True)
        issue_q(0, first=True)
        issue_q(1, first=True)
        for u in range(NU):
            ft, c = divmod(u, C)
            up = u % 3
            fp_ = 0
            if u + 2 < NU:
                load_idx(u + 2)
            if u + 2 < NU:
                issue_k(u + 2)
            if c == 5 and ft + 1 < NFT:
                load_fv(ft + 1)
            kp = u % 3
            fp = ft % 2
            add("vector", lambda e, kp=kp, fp=fp: e.tensor_mul(
                out=p1, in0=gk_re[kp], in1=fv_re[fp]),
                reads=bgk_r[kp] + bfv_r[fp],
                writes=[bp12] + ([bA] if u == 0 else []))
            add("vector", lambda e, kp=kp, fp=fp: e.tensor_mul(
                out=p2, in0=gk_im[kp], in1=fv_im[fp]),
                reads=bgk_i[kp] + bfv_i[fp], writes=[bp12])
            add("vector", lambda e, up=up: e.tensor_tensor_scan(
                out=kv_re[up], data0=p1, data1=p2, initial=0.0,
                op0=ALU.add, op1=ALU.subtract),
                reads=[bp12], writes=[bkv[up]] + (bwf if u == 0 else []))
            add("vector", lambda e, kp=kp, fp=fp: e.tensor_mul(
                out=p1, in0=gk_re[kp], in1=fv_im[fp]),
                reads=bgk_r[kp] + bfv_i[fp], writes=[bp12])
            add("vector", lambda e, kp=kp, fp=fp: e.tensor_mul(
                out=p2, in0=gk_im[kp], in1=fv_re[fp]),
                reads=bgk_i[kp] + bfv_r[fp], writes=[bp12])
            add("vector", lambda e, up=up: e.tensor_tensor_scan(
                out=kv_im[up], data0=p1, data1=p2, initial=0.0,
                op0=ALU.add, op1=ALU.add),
                reads=[bp12], writes=[bkv[up]])
            vspace()
            add("vector", lambda e, up=up: e.tensor_copy(
                out=s_stg[up][:, 0:1], in_=kv_re[up][:, T - 1:T]),
                reads=[bkv[up]], writes=[bstg[up]])
            add("vector", lambda e, up=up: e.tensor_copy(
                out=s_stg[up][:, 1:2], in_=kv_im[up][:, T - 1:T]),
                reads=[bkv[up]], writes=[bstg[up]])
            add("sync", lambda e, up=up: e.dma_start(
                out=cc_si[up][:, :], in_=s_stg[up][:]),
                reads=[bstg[up]], writes=[bcsi[up]], kind="d", qtag=f"up{up}")
            if u > 1:
                z_stage(u - 2)
            if u >= 2:
                issue_q(u)
            add("gpsimd", lambda e, up=up: e.collective_compute(
                "AllGather", ALU.bypass, replica_groups=RG4,
                ins=[cc_si[up][:, :]], outs=[cc_so[up][:, :, :]]),
                reads=[bcsi[up]], writes=[bcso[up]], kind="cc")
        z_stage(NU - 2)
        z_stage(NU - 1)

        def rsp_h(kt):
            return Abf[:, kt * T:(kt + 1) * T]

        def rsp_l(kt):
            if kt < 6:
                return Abf[:, 20480 + kt * T:20480 + (kt + 1) * T]
            return Gbf[:, (kt - 6) * T:(kt - 6 + 1) * T]

        batt = bB
        pc_regions = [bA, bG, brr, bp12, bg01[0], bg01[1], bgq[0], bgq[1]]
        for kt in range(2 * NFT):
            add("sync", lambda e, kt=kt: e.dma_start(
                out=W2[:, 0:T], in_=d_r[kt, :, :]),
                reads=[bdr], writes=[bW2], kind="d", qtag="rr")
            split_hilo(rsp_h(kt), rsp_l(kt), W2[:, 0:T], W1[:, 0:T],
                       [bW2], pc_regions, bW1)
        for mt in range(ND):
            bset = (mt % 2) * 4
            for kt in range(2 * NFT):
                for pp in range(2):
                    wti = (kt * 2 + pp) % 4
                    wtb = bf_(s_wt[wti])
                    base = 0 if pp == 0 else 640
                    add("sync", lambda e, base=base, kt=kt, mt=mt, wti=wti:
                        e.dma_start(
                        out=s_wtf[wti][:, :],
                        in_=cc_w1ob[mt, :, base + kt * 64:base + (kt + 1) * 64]),
                        reads=[bw1ob], writes=[wtb], kind="d", qtag=f"wt{wti}")

                    def mmi(e, kt=kt, pp=pp, bset=bset, wti=wti):
                        ins = None
                        rhs_list = ((rsp_h(kt), rsp_l(kt)) if pp == 0
                                    else (rsp_h(kt),))
                        for ri, rw in enumerate(rhs_list):
                            st = (kt == 0 and pp == 0 and ri == 0)
                            sp = (kt == 2 * NFT - 1 and pp == 1 and
                                  ri == len(rhs_list) - 1)
                            for ch in range(NCH):
                                ins = e.matmul(ps[bset + ch][:, 0:TC],
                                               s_wt[wti][:],
                                               rw[:, ch * TC:(ch + 1) * TC],
                                               start=st, stop=sp)
                        return ins
                    add("tensor", mmi, reads=[wtb, bA, bG],
                        writes=[bps[bset + ch] for ch in range(NCH)])
            for ch in range(NCH):
                add("scalar", lambda e, mt=mt, bset=bset, ch=ch: e.copy(
                    out=Bb[:, mt * T + ch * TC:mt * T + (ch + 1) * TC],
                    in_=ps[bset + ch][:, 0:TC]),
                    reads=[bps[bset + ch]], writes=[batt])

        def xq(k):
            return Abf[:, k * T:(k + 1) * T]

        bxq = bA
        add("sync", lambda e: e.dma_start(out=s_gmax[:], in_=cc_m1o[:, :]),
            reads=[bf_(cc_m1o)], writes=[bf_(s_gmax)], kind="d", qtag="gm")
        add("scalar", lambda e: e.activation(
            out=s_tmp[:, 6:7], in_=s_clip[:, 0:1], func=ACT.Sigmoid),
            reads=[bf_(s_clip)], writes=[btmp])
        add("vector", lambda e: e.tensor_mul(
            out=s_kx[:], in0=s_tmp[:, 6:7], in1=s_gmax[:]),
            reads=[btmp, bf_(s_gmax)], writes=[bf_(s_kx)])
        add("vector", lambda e: e.tensor_scalar(
            s_sxq[:], s_kx[:], 1.0 / QMAX, None, ALU.mult),
            reads=[bf_(s_kx)], writes=[bf_(s_sxq)])
        add("vector", lambda e: e.tensor_copy(out=s_tmp[:, 4:5], in_=s_kx[:]),
            reads=[bf_(s_kx)], writes=[btmp])
        add("vector", lambda e: e.reciprocal(out=s_kx[:], in_=s_tmp[:, 4:5]),
            reads=[btmp], writes=[bf_(s_kx)])
        vspace()
        add("vector", lambda e: e.tensor_scalar(
            s_kx[:], s_kx[:], QMAX, None, ALU.mult),
            reads=[bf_(s_kx)], writes=[bf_(s_kx)])
        vspace()

        def quant_act(src_ap, out_ap, recip_b, src_b, out_b, width, krecip):
            add("vector", lambda e, w=width: e.tensor_scalar(
                W1[:, 0:w], src_ap, krecip, None, ALU.mult),
                reads=[src_b, recip_b], writes=[bW1])
            add("vector", lambda e, w=width: e.tensor_copy(
                out=W0i[:, 0:w], in_=W1[:, 0:w]), reads=[bW1], writes=[bW0])
            add("vector", lambda e, w=width: e.tensor_copy(
                out=EV[:, 0:w], in_=W0i[:, 0:w]), reads=[bW0], writes=[bEV])
            add("vector", lambda e, w=width: e.scalar_tensor_tensor(
                out=W0[:, 0:w], in0=W1[:, 0:w], scalar=0.5,
                in1=EV[:, 0:w], op0=ALU.subtract, op1=ALU.is_gt),
                reads=[bW1, bEV], writes=[bW0])
            add("vector", lambda e, w=width: e.tensor_add(
                out=EV[:, 0:w], in0=EV[:, 0:w], in1=W0[:, 0:w]),
                reads=[bW0, bEV], writes=[bEV])
            add("vector", lambda e, w=width: e.scalar_tensor_tensor(
                out=W0[:, 0:w], in0=W1[:, 0:w], scalar=0.5,
                in1=EV[:, 0:w], op0=ALU.add, op1=ALU.is_lt),
                reads=[bW1, bEV], writes=[bW0])
            add("vector", lambda e, w=width: e.tensor_sub(
                out=EV[:, 0:w], in0=EV[:, 0:w], in1=W0[:, 0:w]),
                reads=[bW0, bEV], writes=[bEV])
            add("vector", lambda e, w=width, o=out_ap: e.tensor_scalar(
                o, EV[:, 0:w], -(QMAX + 1.0), QMAX, ALU.max, ALU.min),
                reads=[bEV], writes=[out_b])

        for k in range(ND):
            add("sync", lambda e, k=k: e.dma_start(
                out=W2[:, 0:T], in_=d_xnt[k, :, :]),
                reads=[bdxnt], writes=[bW2], kind="d", qtag="xn")
            quant_act(W2[:, 0:T], xq(k), bf_(s_kx), bW2, bxq, T, s_kx[:, 0:1])

        for j in range(8):
            add("sync", lambda e, j=j: e.dma_start(
                out=s_scf[:, j * 7:(j + 1) * 7],
                in_=cc_w2o[j, :, 4480:4487]),
                reads=[bw2o], writes=[bf_(s_scf)], kind="d", qtag="sc")
        add("vector", lambda e: e.tensor_scalar(
            s_sxgu[:, :], s_scf[:, :], s_sxq[:, 0:1], None, ALU.mult),
            reads=[bf_(s_sxq), bf_(s_scf)], writes=[bf_(s_sxgu)])

        def gcol(it):
            return (it // 3) * 7 + (it % 3)

        def ucol(it):
            return (it // 3) * 7 + 3 + (it % 3)

        bhm = bf_(s_hmax)
        bdh0 = Buf()
        bdh1 = Buf()
        first_h = [True]
        wgu_t = [s_qb[0], s_qb[1], s_wgu[0], s_wgu[1]]
        wgu_f = [t.bitcast(F32) for t in wgu_t]
        bwgu = [bf_(t) for t in wgu_t]
        for it in range(NI):
            for wi in range(2):
                sl = (it % 2) * 2 + wi
                add("sync", lambda e, it=it, wi=wi, sl=sl: e.dma_start(
                    out=wgu_f[sl][:, :],
                    in_=cc_w2o[it // 3, :, wi * 1536 + (it % 3) * 512:
                               wi * 1536 + (it % 3) * 512 + 512]),
                    reads=[bw2o], writes=[bwgu[sl]], kind="d", qtag=f"wg{sl}")
            for wi in range(2):
                sl = (it % 2) * 2 + wi

                def mmg(e, wi=wi, sl=sl):
                    ins = None
                    for kt in range(ND):
                        for ch in range(NCH):
                            ins = e.matmul(ps[wi * 4 + ch][:, 0:TC],
                                           wgu_t[sl][:, kt * 128:(kt + 1) * 128],
                                           xq(kt)[:, ch * TC:(ch + 1) * TC],
                                           start=(kt == 0), stop=(kt == ND - 1))
                    return ins
                add("tensor", mmg, reads=[bwgu[sl], bxq],
                    writes=[bps[wi * 4 + ch] for ch in range(NCH)])
            hb = W0 if it % 2 == 0 else EV
            bhb = bW0 if it % 2 == 0 else bEV
            for ch in range(NCH):
                sg = s_sgs[:, (ch % 2) * TC:(ch % 2) * TC + TC]
                add("scalar", lambda e, it=it, ch=ch, sg=sg: e.activation(
                    out=sg, in_=ps[ch][:, 0:TC], func=ACT.Sigmoid,
                    scale=s_sxgu[:, gcol(it):gcol(it) + 1]),
                    reads=[bps[ch], bf_(s_sxgu)], writes=[bf_(s_sgs)])
                add("vector", lambda e, it=it, ch=ch, sg=sg: e.scalar_tensor_tensor(
                    out=W1[:, ch * TC:(ch + 1) * TC], in0=ps[ch][:, 0:TC],
                    scalar=s_sxgu[:, gcol(it):gcol(it) + 1],
                    in1=sg, op0=ALU.mult, op1=ALU.mult),
                    reads=[bps[ch], bf_(s_sgs), bf_(s_sxgu)], writes=[bW1])
            for ch in range(NCH):
                add("vector", lambda e, it=it, ch=ch, hb=hb: e.scalar_tensor_tensor(
                    out=hb[:, ch * TC:(ch + 1) * TC], in0=ps[4 + ch][:, 0:TC],
                    scalar=s_sxgu[:, ucol(it):ucol(it) + 1],
                    in1=W1[:, ch * TC:(ch + 1) * TC],
                    op0=ALU.mult, op1=ALU.mult),
                    reads=[bps[4 + ch], bW1, bf_(s_sxgu)], writes=[bhb])
            add("vector", lambda e, hb=hb: e.reduce_max(
                out=s_tmp[:, 7:8], in_=hb[:, 0:T], axis=AX.X,
                apply_absolute_value=True),
                reads=[bhb], writes=[btmp])
            vspace()
            if first_h[0]:
                add("vector", lambda e: e.tensor_copy(out=s_hmax[:], in_=s_tmp[:, 7:8]),
                    reads=[btmp], writes=[bhm])
                first_h[0] = False
            else:
                add("vector", lambda e: e.tensor_max(
                    out=s_hmax[:], in0=s_hmax[:], in1=s_tmp[:, 7:8]),
                    reads=[btmp, bhm], writes=[bhm])
            add("sync", lambda e, it=it, hb=hb: e.dma_start(
                out=d_h[it, :, :], in_=hb[:, 0:T]),
                reads=[bhb], writes=[bdh0 if it % 2 == 0 else bdh1],
                kind="d", qtag=f"h{it % 2}")

        add("vector", lambda e: e.tensor_scalar(
            s_bc[:], s_id[:], 0.0, s_hmax[:, 0:1], ALU.mult, ALU.add),
            reads=[bid, bhm], writes=[bf_(s_bc)])
        add("tensor", lambda e: e.transpose(ps[0][:, 0:128], s_bc[:], s_id[:]),
            reads=[bf_(s_bc), bid], writes=[bps[0]])
        add("scalar", lambda e: e.copy(out=s_bc[:], in_=ps[0][:, 0:128]),
            reads=[bps[0]], writes=[bf_(s_bc)])
        add("vector", lambda e: e.reduce_max(out=s_hmax[:], in_=s_bc[:], axis=AX.X),
            reads=[bf_(s_bc)], writes=[bhm])
        add("sync", lambda e: e.dma_start(out=cc_m2i[:, :], in_=s_hmax[:]),
            reads=[bhm], writes=[bf_(cc_m2i)], kind="d", qtag="upm")
        add("gpsimd", lambda e: e.collective_compute(
            "AllReduce", ALU.max, replica_groups=RG,
            ins=[cc_m2i[:, :]], outs=[cc_m2o[:, :]]),
            reads=[bf_(cc_m2i)], writes=[bf_(cc_m2o)], kind="cc")
        add("sync", lambda e: e.dma_start(out=s_hmax[:], in_=cc_m2o[:, :]),
            reads=[bf_(cc_m2o)], writes=[bhm], kind="d", qtag="gm")
        add("scalar", lambda e: e.activation(
            out=s_tmp[:, 6:7], in_=s_clip[:, 1:2], func=ACT.Sigmoid),
            reads=[bf_(s_clip)], writes=[btmp])
        add("vector", lambda e: e.tensor_mul(
            out=s_kh[:], in0=s_tmp[:, 6:7], in1=s_hmax[:]),
            reads=[btmp, bhm], writes=[bf_(s_kh)])
        add("vector", lambda e: e.tensor_scalar(
            s_shq[:], s_kh[:], 1.0 / QMAX, None, ALU.mult),
            reads=[bf_(s_kh)], writes=[bf_(s_shq)])
        add("vector", lambda e: e.tensor_copy(out=s_tmp[:, 4:5], in_=s_kh[:]),
            reads=[bf_(s_kh)], writes=[btmp])
        add("vector", lambda e: e.reciprocal(out=s_kh[:], in_=s_tmp[:, 4:5]),
            reads=[btmp], writes=[bf_(s_kh)])
        vspace()
        add("vector", lambda e: e.tensor_scalar(
            s_kh[:], s_kh[:], QMAX, None, ALU.mult),
            reads=[bf_(s_kh)], writes=[bf_(s_kh)])
        vspace()
        add("vector", lambda e: e.tensor_scalar(
            s_shdall[:, :], s_scf[:, :], s_shq[:, 0:1], None, ALU.mult),
            reads=[bf_(s_shq), bf_(s_scf)], writes=[bf_(s_shdall)])

        def hq(kt):
            return Gbf[:, kt * TC:(kt + 1) * TC]

        bhq = bG
        boutd = bA
        for ch in range(NCH):
            for kt in range(NI):
                add("sync", lambda e, kt=kt, ch=ch: e.dma_start(
                    out=W2[:, 0:TC], in_=d_h[kt, :, ch * TC:(ch + 1) * TC]),
                    reads=[bdh0, bdh1], writes=[bW2], kind="d", qtag="hr")
                quant_act(W2[:, 0:TC], hq(kt), bf_(s_kh), bW2, bhq, TC, s_kh[:, 0:1])
            for mt in range(ND):
                bnk = mt % 8
                wdst = W0 if mt % 2 == 0 else W1
                wdstb = W0bf if mt % 2 == 0 else W1bf
                bwd = bW0 if mt % 2 == 0 else bW1
                add("sync", lambda e, mt=mt, wdst=wdst: e.dma_start(
                    out=wdst[:, 0:1408],
                    in_=cc_w2o[mt, :, 3072:4480]),
                    reads=[bw2o], writes=[bwd], kind="d", qtag=f"wd{mt % 2}")

                def mmd(e, bnk=bnk, wdstb=wdstb):
                    ins = None
                    for kt in range(NI):
                        ins = e.matmul(ps[bnk][:, 0:TC],
                                       wdstb[:, kt * 128:(kt + 1) * 128],
                                       hq(kt), start=(kt == 0), stop=(kt == NI - 1))
                    return ins
                add("tensor", mmd, reads=[bwd, bhq], writes=[bps[bnk]])
                add("vector", lambda e, mt=mt, bnk=bnk, ch=ch: e.scalar_tensor_tensor(
                    out=A[:, mt * T + ch * TC:mt * T + (ch + 1) * TC],
                    in0=ps[bnk][:, 0:TC], scalar=s_shdall[:, mt * 7 + 6:mt * 7 + 7],
                    in1=Bb[:, mt * T + ch * TC:mt * T + (ch + 1) * TC],
                    op0=ALU.mult, op1=ALU.add),
                    reads=[bps[bnk], bf_(s_shdall), batt], writes=[boutd])

        bout = bf_(out_t)
        for tt in range(NT):
            buf = W0 if tt % 2 == 0 else W1
            bb = bW0 if tt % 2 == 0 else bW1
            add("sync", lambda e, tt=tt, buf=buf: e.dma_start(
                out=buf[:, 0:D], in_=x_in[tt * 128:(tt + 1) * 128, :]),
                writes=[bb], kind="d", qtag=f"x{tt % 2}")
            for k in range(ND):
                bnk = (tt * ND + k) % 2
                add("tensor", lambda e, tt=tt, k=k, bnk=bnk: e.transpose(
                    ps[bnk][:, 0:128], A[:, k * T + 128 * tt:k * T + 128 * (tt + 1)],
                    s_id[:]),
                    reads=[boutd, bid], writes=[bps[bnk]])
                add("vector", lambda e, k=k, bnk=bnk, buf=buf: e.tensor_add(
                    out=W2[:, 128 * k:128 * (k + 1)], in0=ps[bnk][:, 0:128],
                    in1=buf[:, 128 * k:128 * (k + 1)]),
                    reads=[bps[bnk], bb], writes=[bW2])
            add("sync", lambda e, tt=tt: e.dma_start(
                out=out_t[tt * 128:(tt + 1) * 128, :], in_=W2[:, 0:D]),
                reads=[bW2], writes=[bout], kind="d", qtag="out")
        if debug:
            add("sync", lambda e: e.dma_start(
                out=d_mxdbg[0, :, :], in_=cc_m1o[:, :]),
                reads=[bf_(cc_m1o)], writes=[bf_(d_mxdbg)], kind="d")
            add("sync", lambda e: e.dma_start(
                out=d_mxdbg[1, :, :], in_=cc_m2o[:, :]),
                reads=[bf_(cc_m2o)], writes=[bf_(d_mxdbg)], kind="d")
            add("sync", lambda e: e.dma_start(out=d_w2dbg[:, :], in_=cc_w2i[:, :]),
                reads=[bw2i] + bw2s, writes=[bf_(d_w2dbg)], kind="d")
            for j in range(8):
                add("sync", lambda e, j=j: e.dma_start(
                    out=d_w2odbg[j, :, :], in_=cc_w2o[j, :, 0:561]),
                    reads=[bw2o], writes=[bf_(d_w2odbg)], kind="d")
        add("sync", lambda e: e.nop(), reads=[bout])

        P.emit(nc, es)
    return nc


def kernel(**inputs):
    x = np.ascontiguousarray(
        np.asarray(inputs["L0_attn_x"], dtype=np.float32).reshape(8 * T, D))
    dft, ctm, idn = _host_consts()
    perms = np.asarray(inputs["L0_attn_perms"], dtype=np.int32)
    permspad = np.concatenate(
        [perms, np.full((C, FP - perms.shape[1]), 513, dtype=np.int32)], axis=1)
    clip_x = float(np.asarray(inputs["L0_ffn_clip_x"]).ravel()[0])
    clip_d = float(np.asarray(inputs["L0_ffn_clip_down"]).ravel()[0])
    clip2 = np.tile(np.array([[clip_x, clip_d]], dtype=np.float32), (128, 1))

    if "nc" not in _CACHE:
        _CACHE["nc"] = build_program()
    nc = _CACHE["nc"]

    wq = np.asarray(inputs["L0_attn_Wq"], dtype=np.float32)
    wk = np.asarray(inputs["L0_attn_Wk"], dtype=np.float32)
    wv = np.asarray(inputs["L0_attn_Wv"], dtype=np.float32)
    wo = np.asarray(inputs["L0_attn_Wo"], dtype=np.float32)
    wg = np.asarray(inputs["L0_ffn_w_gate"], dtype=np.float32)
    wu = np.asarray(inputs["L0_ffn_w_up"], dtype=np.float32)
    wd = np.asarray(inputs["L0_ffn_w_down"], dtype=np.float32)
    pad = np.ones((3 * 8 * 128 - wg.shape[0], D), dtype=np.float32)
    wgp = np.concatenate([wg, pad], axis=0)
    wup = np.concatenate([wu, pad], axis=0)

    common = dict(clip2=clip2, dft=dft, ct=ctm, idn=idn, permspad=permspad)
    in_maps = []
    for r in range(8):
        mask = np.zeros((128, 8), dtype=np.float32)
        for j in range(4):
            if j < r % 4:
                mask[:, 2 * j] = 1.0
                mask[:, 2 * j + 1] = 1.0
        m = dict(common)
        m["x_in"] = np.ascontiguousarray(x[r * T:(r + 1) * T])
        m["mask16"] = mask
        m["wq_s"] = np.ascontiguousarray(wq[:, r * 128:(r + 1) * 128])
        m["wk_s"] = np.ascontiguousarray(wk[:, r * 128:(r + 1) * 128])
        m["wv_s"] = np.ascontiguousarray(wv[:, r * 128:(r + 1) * 128])
        m["wo_s"] = np.ascontiguousarray(wo[r * 128:(r + 1) * 128, :])
        m["wg_s"] = np.ascontiguousarray(wgp[r * 384:(r + 1) * 384])
        m["wu_s"] = np.ascontiguousarray(wup[r * 384:(r + 1) * 384])
        m["wd_s"] = np.ascontiguousarray(wd[r * 128:(r + 1) * 128, :])
        in_maps.append(m)

    res = run_bass_kernel_spmd(nc, in_maps, core_ids=list(range(8)))
    _CACHE["last_results"] = res.results
    out = np.concatenate([res.results[r]["out_t"] for r in range(8)], axis=0)
    return out.reshape(2, 8192, D).astype(np.float32)


# version 33 (final)
# speedup vs baseline: 1.0212x; 1.0116x over previous
import contextlib
import numpy as np
import ml_dtypes
import concourse.bass as bass
import concourse.mybir as mybir
from concourse.bass_core_rust import IndirectOffsetOnAxis
from concourse.bass_utils import run_bass_kernel_spmd

F32 = mybir.dt.float32
I32 = mybir.dt.int32
BF16 = mybir.dt.bfloat16
AX = mybir.AxisListType
ALU = mybir.AluOpType
ACT = mybir.ActivationFunctionType

D = 1024
T = 2048
NT = T // 128
ND = D // 128
FP = 640
NFT = FP // 128
FR = 2 * FP
C = 10
NU = NFT * C
II = 2816
NI = II // 128
NCH = 4
TC = T // NCH
QMAX = 127.0
EPS = 1e-5
RG = [[0, 1, 2, 3, 4, 5, 6, 7]]
RG4 = [[0, 1, 2, 3], [4, 5, 6, 7]]
ENGS = ("sync", "vector", "scalar", "tensor", "gpsimd")

_CACHE = {}


class Op:
    __slots__ = ("eng", "fn", "deps", "kind", "sem_inc", "tok", "qtag")

    def __init__(self, eng, fn, deps, kind, qtag=""):
        self.eng, self.fn, self.deps, self.kind = eng, fn, deps, kind
        self.sem_inc = 16 if kind == "d" else 1
        self.tok = None
        self.qtag = qtag


class Buf:
    __slots__ = ("w", "r")

    def __init__(self):
        self.w = None
        self.r = []


class Prog:
    def __init__(self):
        self.ops = {e: [] for e in ENGS}

    def add(self, eng, fn, reads=(), writes=(), kind="c", qtag=""):
        deps = []
        for b in reads:
            if b.w is not None:
                deps.append(b.w)
        for b in writes:
            if b.w is not None:
                deps.append(b.w)
            deps.extend(b.r)
        op = Op(eng, fn, deps, kind, qtag)
        self.ops[eng].append(op)
        for b in reads:
            b.r.append(op)
        for b in writes:
            b.w = op
            b.r = []
        return op

    def emit(self, nc, es):
        signaled = set()
        for e in ENGS:
            for op in self.ops[e]:
                for dp in op.deps:
                    if dp.eng != op.eng or dp.kind != "c":
                        signaled.add(id(dp))
        streams = {}

        def stream(key):
            if key not in streams:
                nm = f"q_{key[0]}_{key[1]}_{key[2]}"[:28]
                streams[key] = [es.enter_context(nc.semaphore(nm)), 0]
            return streams[key]

        for e in ENGS:
            for op in self.ops[e]:
                if id(op) in signaled or op.kind != "c":
                    st = stream((op.eng, op.kind, op.qtag))
                    st[1] += op.sem_inc
                    op.tok = (st[0], st[1])

        with nc.Block() as b:
            def make(e):
                def body(eng_obj):
                    for op in self.ops[e]:
                        seen = {}
                        for dp in op.deps:
                            if dp.eng == op.eng and dp.kind == "c":
                                continue
                            if dp.tok is None:
                                continue
                            sem, val = dp.tok
                            if id(sem) not in seen or seen[id(sem)][1] < val:
                                seen[id(sem)] = (sem, val)
                        for sem, val in seen.values():
                            eng_obj.wait_ge(sem, val)
                        ins = op.fn(eng_obj)
                        if op.tok is not None:
                            assert ins is not None, f"{e} op returned None but needs signal"
                            ins.then_inc(op.tok[0], op.sem_inc)
                return body
            for e in ENGS:
                if self.ops[e]:
                    getattr(b, e)(make(e))


def _host_consts():
    dd = np.arange(D)[:, None].astype(np.float64)
    ff = np.arange(FP)[None, :].astype(np.float64)
    ang = 2.0 * np.pi * (dd * ff) / D
    valid = (ff < 513).astype(np.float64)
    cosm = np.cos(ang) * valid
    sinm = np.sin(ang) * valid
    dft = np.concatenate([cosm, -sinm], axis=1).astype(np.float32)
    w = np.where((ff == 0) | (ff == 512), 1.0, 2.0) * valid
    ctm = np.concatenate([(w * cosm) / (D * C), -(w * sinm) / (D * C)],
                         axis=1).astype(np.float32)
    idn = np.eye(128, dtype=np.float32)
    return dft, ctm, idn


def build_program(debug=False):
    ik = "ExternalOutput" if debug else "Internal"
    nc = bass.Bass("TRN2", target_bir_lowering=False, debug=False)

    x_in = nc.dram_tensor("x_in", [T, D], F32, kind="ExternalInput").ap()
    w_s = [nc.dram_tensor(n, [D, 128], F32, kind="ExternalInput").ap()
           for n in ("wq_s", "wk_s", "wv_s")]
    wo_s = nc.dram_tensor("wo_s", [128, D], F32, kind="ExternalInput").ap()
    wgu_s = [nc.dram_tensor(n, [384, D], F32, kind="ExternalInput").ap()
             for n in ("wg_s", "wu_s")]
    wd_s = nc.dram_tensor("wd_s", [128, II], F32, kind="ExternalInput").ap()
    clip2 = nc.dram_tensor("clip2", [128, 2], F32, kind="ExternalInput").ap()
    dft_in = nc.dram_tensor("dft", [D, FR], F32, kind="ExternalInput").ap()
    ct_in = nc.dram_tensor("ct", [D, FR], F32, kind="ExternalInput").ap()
    idn_in = nc.dram_tensor("idn", [128, 128], F32, kind="ExternalInput").ap()
    permspad = nc.dram_tensor("permspad", [C, FP], I32, kind="ExternalInput").ap()
    mask16_in = nc.dram_tensor("mask16", [128, 8], F32, kind="ExternalInput").ap()
    out_t = nc.dram_tensor("out_t", [T, D], F32, kind="ExternalOutput").ap()

    d_fqk = nc.dram_tensor("d_fqk", [FP, 4, T], F32, kind=ik).ap()
    d_fv = nc.dram_tensor("d_fv", [2 * NFT, 128, T], F32, kind=ik).ap()
    d_xnt = nc.dram_tensor("d_xnt", [ND, 128, T], F32, kind=ik).ap()
    d_r = nc.dram_tensor("d_r", [2 * NFT, 128, T], F32, kind=ik).ap()
    d_h = nc.dram_tensor("d_h", [NI, 128, T], F32, kind=ik).ap()
    cc_w1ai = nc.dram_tensor("cc_w1ai", [128, 3840], F32, kind="Internal").ap()
    cc_w1oa = nc.dram_tensor("cc_w1oa", [8, 128, 3840], F32, kind="Internal").ap()
    cc_w1bi = nc.dram_tensor("cc_w1bi", [128, 1280], F32, kind="Internal").ap()
    cc_w1ob = nc.dram_tensor("cc_w1ob", [8, 128, 1280], F32, kind="Internal").ap()
    cc_w2i = nc.dram_tensor("cc_w2i", [128, 4488], F32, kind="Internal").ap()
    cc_w2o = nc.dram_tensor("cc_w2o", [8, 128, 4488], F32, kind="Internal").ap()
    cc_m1i = nc.dram_tensor("cc_m1i", [128, 1], F32, kind="Internal").ap()
    cc_m1o = nc.dram_tensor("cc_m1o", [128, 1], F32, kind="Internal").ap()
    cc_m2i = nc.dram_tensor("cc_m2i", [128, 1], F32, kind="Internal").ap()
    cc_m2o = nc.dram_tensor("cc_m2o", [128, 1], F32, kind="Internal").ap()
    cc_si = [nc.dram_tensor(f"cc_si{p}", [128, 2], F32, kind="Internal").ap()
             for p in range(3)]
    cc_so = [nc.dram_tensor(f"cc_so{p}", [4, 128, 2], F32, kind="Internal").ap()
             for p in range(3)]
    d_off = nc.dram_tensor("d_off", [NU, 128, 2], F32, kind=ik).ap()
    d_slot = nc.dram_tensor("d_slot", [NU, 128, 8], F32, kind=ik).ap()
    d_mpr = nc.dram_tensor("d_mpr", [NU, 128, 8], F32, kind=ik).ap()
    d_w2dbg = nc.dram_tensor("d_w2dbg", [128, 4488], F32, kind=ik).ap()
    d_mxdbg = nc.dram_tensor("d_mxdbg", [2, 128, 1], F32, kind=ik).ap()
    d_w2odbg = nc.dram_tensor("d_w2odbg", [8, 128, 561], F32, kind=ik).ap()

    es = contextlib.ExitStack()
    with es:
        def sbt(name, shape, dt=F32):
            return es.enter_context(nc.sbuf_tensor(name, shape, dt))

        A = sbt("A", [128, 16384])
        Bb = sbt("Bb", [128, 16384])
        G = sbt("G", [128, 8192])
        W0 = sbt("W0", [128, 2048])
        W1 = sbt("W1", [128, 2048])
        W2 = sbt("W2", [128, 2048])
        EV = sbt("EV", [128, 2048])
        s_id = sbt("s_id", [128, 128])
        s_clip = sbt("s_clip", [128, 2])
        s_mask = sbt("s_mask", [128, 8])
        s_idx = [sbt(f"s_idx{p}", [128, 1], I32) for p in range(3)]
        s_slot = [sbt(f"s_slot{p}", [128, 8]) for p in range(2)]
        s_mpr = sbt("s_mpr", [128, 8])
        s_stg = [sbt(f"s_stg{p}", [128, 2]) for p in range(3)]
        s_off = sbt("s_off", [128, 2])
        s_tmp = sbt("s_tmp", [128, 8])
        s_rmax = sbt("s_rmax", [128, 1])
        s_gmax = sbt("s_gmax", [128, 1])
        s_hmax = sbt("s_hmax", [128, 1])
        s_kx = sbt("s_kx", [128, 1])
        s_kh = sbt("s_kh", [128, 1])
        s_sxq = sbt("s_sxq", [128, 1])
        s_shq = sbt("s_shq", [128, 1])
        s_mysc = sbt("s_mysc", [128, 8])
        s_scf = sbt("s_scf", [128, 56])
        s_sxgu = sbt("s_sxgu", [128, 56])
        s_shdall = sbt("s_shdall", [128, 56])
        s_bc = sbt("s_bc", [128, 128])
        s_sgs = sbt("s_sgs", [128, 2 * TC])
        s_wt = [sbt(f"s_wt{p}", [128, 128], BF16) for p in range(4)]
        s_qb = [sbt(f"s_qb{p}", [128, 1024], BF16) for p in range(2)]
        s_wgu = [sbt(f"s_wgu{p}", [128, 1024], BF16) for p in range(2)]

        ps = [es.enter_context(nc.psum_tensor(f"ps{i}", [128, TC], F32))
              for i in range(8)]

        Abf = A.bitcast(BF16)
        Gbf = G.bitcast(BF16)
        Bbf = Bb.bitcast(BF16)
        Bi32 = Bb.bitcast(I32)
        W0i = W0.bitcast(I32)
        W0bf = W0.bitcast(BF16)
        W1bf = W1.bitcast(BF16)

        P = Prog()
        BUFS = {}

        def bf_(t):
            if id(t) not in BUFS:
                BUFS[id(t)] = Buf()
            return BUFS[id(t)]

        bA, bB, bG, bW0, bW1, bW2, bEV = (bf_(x) for x in (A, Bb, G, W0, W1, W2, EV))
        bps = [bf_(p_) for p_ in ps]
        bid = bf_(s_id)
        btmp = bf_(s_tmp)

        s_jnk = sbt("s_jnk", [128, 16])

        def add(eng, fn, reads=(), writes=(), kind="c", qtag=""):
            return P.add(eng, fn, reads, writes, kind, qtag)

        def vspace():
            P.add("vector", lambda e: e.tensor_copy(out=s_jnk[:, 0:1],
                                                    in_=s_id[:, 0:1]), (), (), "c")

        def split_hilo(dst_h, dst_l, src, tmp, rbs, wbs, tmpb):
            add("vector", lambda e, a=dst_h, b=src: e.tensor_copy(out=a, in_=b),
                reads=rbs, writes=wbs)
            add("vector", lambda e, a=tmp, b=dst_h: e.tensor_copy(out=a, in_=b),
                reads=wbs, writes=[tmpb])
            add("vector", lambda e, a=tmp, b=src, c_=tmp: e.tensor_sub(out=a, in0=b, in1=c_),
                reads=rbs + [tmpb], writes=[tmpb])
            add("vector", lambda e, a=dst_l, b=tmp: e.tensor_copy(out=a, in_=b),
                reads=[tmpb], writes=wbs)

        add("sync", lambda e: e.dma_start(out=s_id[:], in_=idn_in[:, :]),
            writes=[bid], kind="d", qtag="i0")
        add("sync", lambda e: e.dma_start(out=s_clip[:], in_=clip2[:, :]),
            writes=[bf_(s_clip)], kind="d", qtag="i1")
        add("sync", lambda e: e.dma_start(out=s_mask[:], in_=mask16_in[:, :]),
            writes=[bf_(s_mask)], kind="d", qtag="i2")

        for tt in range(NT):
            buf = W0 if tt % 2 == 0 else W1
            bb = bW0 if tt % 2 == 0 else bW1
            add("sync", lambda e, buf=buf, tt=tt: e.dma_start(
                out=buf[:, 0:D], in_=x_in[tt * 128:(tt + 1) * 128, :]),
                writes=[bb], kind="d", qtag=f"x{tt % 2}")
            add("vector", lambda e, buf=buf: e.tensor_reduce(
                out=s_tmp[:, 0:1], in_=buf[:, 0:D], axis=AX.X, op=ALU.add),
                reads=[bb], writes=[btmp])
            vspace()
            add("vector", lambda e: e.tensor_scalar(
                s_tmp[:, 0:1], s_tmp[:, 0:1], 1.0 / D, None, ALU.mult),
                reads=[btmp], writes=[btmp])
            vspace()
            add("vector", lambda e, buf=buf: e.tensor_scalar(
                buf[:, 0:D], buf[:, 0:D], s_tmp[:, 0:1], None, ALU.subtract),
                reads=[bb, btmp], writes=[bb])
            add("vector", lambda e, buf=buf: e.tensor_mul(
                out=buf[:, D:2 * D], in0=buf[:, 0:D], in1=buf[:, 0:D]),
                reads=[bb], writes=[bb])
            add("vector", lambda e, buf=buf: e.tensor_reduce(
                out=s_tmp[:, 1:2], in_=buf[:, D:2 * D], axis=AX.X, op=ALU.add),
                reads=[bb], writes=[btmp])
            vspace()
            add("vector", lambda e: e.tensor_scalar(
                s_tmp[:, 1:2], s_tmp[:, 1:2], 1.0 / D, EPS, ALU.mult, ALU.add),
                reads=[btmp], writes=[btmp])
            add("scalar", lambda e: e.activation(
                out=s_tmp[:, 4:5], in_=s_tmp[:, 1:2], func=ACT.Sqrt),
                reads=[btmp], writes=[btmp])
            add("vector", lambda e: e.reciprocal(
                out=s_tmp[:, 2:3], in_=s_tmp[:, 4:5]),
                reads=[btmp], writes=[btmp])
            vspace()
            add("scalar", lambda e, buf=buf: e.activation(
                out=buf[:, D:2 * D], in_=buf[:, 0:D], func=ACT.Copy,
                scale=s_tmp[:, 2:3]),
                reads=[bb, btmp], writes=[bb])
            if tt == 0:
                add("vector", lambda e, buf=buf: e.reduce_max(
                    out=s_rmax[:], in_=buf[:, D:2 * D], axis=AX.X,
                    apply_absolute_value=True),
                    reads=[bb], writes=[bf_(s_rmax)])
                vspace()
            else:
                add("vector", lambda e, buf=buf: e.reduce_max(
                    out=s_tmp[:, 3:4], in_=buf[:, D:2 * D], axis=AX.X,
                    apply_absolute_value=True),
                    reads=[bb], writes=[btmp])
                vspace()
                add("vector", lambda e: e.tensor_max(
                    out=s_rmax[:], in0=s_rmax[:], in1=s_tmp[:, 3:4]),
                    reads=[btmp, bf_(s_rmax)], writes=[bf_(s_rmax)])
            for k in range(ND):
                bnk = (tt * ND + k) % 2
                add("tensor", lambda e, buf=buf, k=k, bnk=bnk: e.transpose(
                    ps[bnk][:, 0:128], buf[:, D + 128 * k:D + 128 * (k + 1)], s_id[:]),
                    reads=[bb, bid], writes=[bps[bnk]])
                add("scalar", lambda e, tt=tt, k=k, bnk=bnk: e.copy(
                    out=A[:, k * T + 128 * tt:k * T + 128 * (tt + 1)],
                    in_=ps[bnk][:, 0:128]),
                    reads=[bps[bnk]], writes=[bA])

        bdxnt = bf_(d_xnt)
        for k in range(ND):
            add("sync", lambda e, k=k: e.dma_start(
                out=d_xnt[k, :, :], in_=A[:, k * T:(k + 1) * T]),
                reads=[bA], writes=[bdxnt], kind="d", qtag="xn")

        add("vector", lambda e: e.tensor_scalar(
            s_bc[:], s_id[:], 0.0, s_rmax[:, 0:1], ALU.mult, ALU.add),
            reads=[bid, bf_(s_rmax)], writes=[bf_(s_bc)])
        add("tensor", lambda e: e.transpose(ps[0][:, 0:128], s_bc[:], s_id[:]),
            reads=[bf_(s_bc), bid], writes=[bps[0]])
        add("scalar", lambda e: e.copy(out=s_bc[:], in_=ps[0][:, 0:128]),
            reads=[bps[0]], writes=[bf_(s_bc)])
        add("vector", lambda e: e.reduce_max(out=s_rmax[:], in_=s_bc[:], axis=AX.X),
            reads=[bf_(s_bc)], writes=[bf_(s_rmax)])
        add("sync", lambda e: e.dma_start(out=cc_m1i[:, :], in_=s_rmax[:]),
            reads=[bf_(s_rmax)], writes=[bf_(cc_m1i)], kind="d", qtag="upm")
        add("gpsimd", lambda e: e.collective_compute(
            "AllReduce", ALU.max, replica_groups=RG,
            ins=[cc_m1i[:, :]], outs=[cc_m1o[:, :]]),
            reads=[bf_(cc_m1i)], writes=[bf_(cc_m1o)], kind="cc")

        def dfth(k):
            return Bbf[:, k * FR:(k + 1) * FR]

        def dftl(k):
            return Bbf[:, 10240 + k * FR:10240 + (k + 1) * FR]

        def wsh(k):
            return Bbf[:, 20480 + k * 128:20480 + (k + 1) * 128]

        def wsl(k):
            return Bbf[:, 21504 + k * 128:21504 + (k + 1) * 128]

        bw1as = [Buf(), Buf()]
        bw1oa = bf_(cc_w1oa)
        bw1bs = [Buf(), Buf()]
        bw1ob = bf_(cc_w1ob)
        bw2i = bf_(cc_w2i)
        bw2s = [Buf() for _ in range(5)]
        bw2o = bf_(cc_w2o)
        s_qbf = [q.bitcast(F32) for q in s_qb]
        s_wtf = [w.bitcast(F32) for w in s_wt]

        for k in range(ND):
            stgb = W2 if k % 2 == 0 else EV
            bstgb = bW2 if k % 2 == 0 else bEV
            add("sync", lambda e, k=k, stgb=stgb: e.dma_start(
                out=stgb[:, 0:FR], in_=dft_in[k * 128:(k + 1) * 128, :]),
                writes=[bstgb], kind="d", qtag=f"pw{k % 2}")
            split_hilo(dfth(k), dftl(k), stgb[:, 0:FR], W1[:, 0:FR],
                       [bstgb], [bB], bW1)

        NCS = [(0, 512), (512, 512), (1024, 256)]

        for ti in range(3):
            add("sync", lambda e, ti=ti: e.dma_start(
                out=W2[:, 0:1024].rearrange("p (k c) -> p k c", k=8),
                in_=w_s[ti].rearrange("(k p) c -> p k c", k=8)),
                writes=[bW2], kind="d")
            for k in range(ND):
                split_hilo(wsh(k), wsl(k), W2[:, k * 128:(k + 1) * 128],
                           W1[:, 0:128], [bW2], [bB], bW1)
            for j, (noff, nsz) in enumerate(NCS):
                bnk = j % 2

                def mmq(e, noff=noff, nsz=nsz, bnk=bnk):
                    ins = None
                    for k in range(ND):
                        lh = wsh(k)
                        ll = wsl(k)
                        rh = dfth(k)[:, noff:noff + nsz]
                        rl = dftl(k)[:, noff:noff + nsz]
                        ins = e.matmul(ps[bnk][:, 0:nsz], lh, rh,
                                       start=(k == 0), stop=False)
                        ins = e.matmul(ps[bnk][:, 0:nsz], lh, rl,
                                       start=False, stop=False)
                        ins = e.matmul(ps[bnk][:, 0:nsz], ll, rh,
                                       start=False, stop=(k == ND - 1))
                    return ins
                add("tensor", mmq, reads=[bB], writes=[bps[bnk]])
                add("scalar", lambda e, bnk=bnk, nsz=nsz: e.copy(
                    out=EV[:, bnk * 1024:bnk * 1024 + nsz],
                    in_=ps[bnk][:, 0:nsz]),
                    reads=[bps[bnk]], writes=[bEV])
                qb = s_qb[bnk]
                qbf = s_qbf[bnk]
                bqb = bf_(qb)
                split_hilo(qb[:, 0:nsz], qb[:, 512:512 + nsz],
                           EV[:, bnk * 1024:bnk * 1024 + nsz],
                           W1[:, 0:nsz], [bEV], [bqb], bW1)
                add("sync", lambda e, ti=ti, noff=noff, nsz=nsz, qbf=qbf:
                    e.dma_start(out=cc_w1ai[:, ti * 1280 + noff // 2:
                                            ti * 1280 + (noff + nsz) // 2],
                                in_=qbf[:, 0:nsz // 2]),
                    reads=[bqb], writes=[bw1as[bnk]], kind="d", qtag=f"w1{bnk}")
                add("sync", lambda e, ti=ti, noff=noff, nsz=nsz, qbf=qbf:
                    e.dma_start(out=cc_w1ai[:, ti * 1280 + 640 + noff // 2:
                                            ti * 1280 + 640 + (noff + nsz) // 2],
                                in_=qbf[:, 256:256 + nsz // 2]),
                    reads=[bqb], writes=[bw1as[bnk]], kind="d", qtag=f"w1{bnk}")

        add("gpsimd", lambda e: e.collective_compute(
            "AllGather", ALU.bypass, replica_groups=RG,
            ins=[cc_w1ai[:, :]], outs=[cc_w1oa[:, :, :]]),
            reads=bw1as, writes=[bw1oa], kind="cc")

        def xh(k):
            return Abf[:, k * T:(k + 1) * T]

        def xl(k):
            return Abf[:, 16384 + k * T:16384 + (k + 1) * T]

        for k in range(ND):
            stgb = W2 if k % 2 == 0 else EV
            bstgb = bW2 if k % 2 == 0 else bEV
            add("sync", lambda e, k=k, stgb=stgb: e.dma_start(
                out=stgb[:, 0:T], in_=d_xnt[k, :, :]),
                reads=[bdxnt], writes=[bstgb], kind="d", qtag=f"xr{k % 2}")
            split_hilo(xh(k), xl(k), stgb[:, 0:T], W1[:, 0:T], [bstgb], [bA], bW1)


        add("sync", lambda e: e.dma_start(out=W2[:, 0:D], in_=wo_s[:, :]),
            writes=[bW2], kind="d")
        for k in range(ND):
            bnk = k % 2
            add("tensor", lambda e, k=k, bnk=bnk: e.transpose(
                ps[bnk][:, 0:128], W2[:, 128 * k:128 * (k + 1)], s_id[:]),
                reads=[bW2, bid], writes=[bps[bnk]])
            add("scalar", lambda e, k=k, bnk=bnk: e.copy(
                out=G[:, k * 128:(k + 1) * 128], in_=ps[bnk][:, 0:128]),
                reads=[bps[bnk]], writes=[bG])

        def woth(k):
            return Gbf[:, 2048 + k * 128:2048 + (k + 1) * 128]

        def wotl(k):
            return Gbf[:, 3072 + k * 128:3072 + (k + 1) * 128]

        for k in range(ND):
            split_hilo(woth(k), wotl(k), G[:, k * 128:(k + 1) * 128],
                       W1[:, 0:128], [bG], [bG], bW1)

        for k in range(ND):
            stgb = W2 if k % 2 == 0 else EV
            bstgb = bW2 if k % 2 == 0 else bEV
            add("sync", lambda e, k=k, stgb=stgb: e.dma_start(
                out=stgb[:, 0:FR], in_=ct_in[k * 128:(k + 1) * 128, :]),
                writes=[bstgb], kind="d", qtag=f"pw{k % 2}")
            split_hilo(dfth(k), dftl(k), stgb[:, 0:FR], W1[:, 0:FR],
                       [bstgb], [bB], bW1)

        for mt in range(2 * NFT):
            bnk = mt % 2

            def mmo(e, mt=mt, bnk=bnk):
                ins = None
                for k in range(ND):
                    lh = dfth(k)[:, mt * 128:(mt + 1) * 128]
                    ll = dftl(k)[:, mt * 128:(mt + 1) * 128]
                    rh = woth(k)
                    rl = wotl(k)
                    ins = e.matmul(ps[bnk][:, 0:128], lh, rh,
                                   start=(k == 0), stop=False)
                    ins = e.matmul(ps[bnk][:, 0:128], lh, rl,
                                   start=False, stop=False)
                    ins = e.matmul(ps[bnk][:, 0:128], ll, rh,
                                   start=False, stop=(k == ND - 1))
                return ins
            add("tensor", mmo, reads=[bB, bG], writes=[bps[bnk]])
            add("scalar", lambda e, bnk=bnk: e.copy(
                out=EV[:, bnk * 1024:bnk * 1024 + 128],
                in_=ps[bnk][:, 0:128]),
                reads=[bps[bnk]], writes=[bEV])
            qb = s_qb[bnk]
            qbf = s_qbf[bnk]
            bqb = bf_(qb)
            split_hilo(qb[:, 0:128], qb[:, 512:640],
                       EV[:, bnk * 1024:bnk * 1024 + 128],
                       W1[:, 0:128], [bEV], [bqb], bW1)
            add("sync", lambda e, mt=mt, qbf=qbf: e.dma_start(
                out=cc_w1bi[:, mt * 64:(mt + 1) * 64],
                in_=qbf[:, 0:64]),
                reads=[bqb], writes=[bw1bs[bnk]], kind="d", qtag=f"w1b{bnk}")
            add("sync", lambda e, mt=mt, qbf=qbf: e.dma_start(
                out=cc_w1bi[:, 640 + mt * 64:640 + (mt + 1) * 64],
                in_=qbf[:, 256:256 + 64]),
                reads=[bqb], writes=[bw1bs[bnk]], kind="d", qtag=f"w1b{bnk}")

        add("gpsimd", lambda e: e.collective_compute(
            "AllGather", ALU.bypass, replica_groups=RG,
            ins=[cc_w1bi[:, :]], outs=[cc_w1ob[:, :, :]]),
            reads=bw1bs, writes=[bw1ob], kind="cc")

        W2i = W2.bitcast(I32)
        bmys = bf_(s_mysc)

        def qsrc(i):
            return G[:, (i % 2) * 2816:(i % 2) * 2816 + 2816]

        qcbuf = [W0, W1]
        bqcb = [bW0, bW1]
        TILES = [(wi, il) for wi in range(2) for il in range(3)] + [(2, 0)]

        def load_qsrc(i):
            if i < 6:
                wi, il = TILES[i]
                add("scalar", lambda e, i=i, wi=wi, il=il: e.dma_start(
                    out=qsrc(i)[:, 0:D],
                    in_=wgu_s[wi][il * 128:(il + 1) * 128, :]),
                    writes=[bG], kind="d", qtag=f"qs{i % 2}")
            else:
                add("scalar", lambda e, i=i: e.dma_start(
                    out=qsrc(i)[:, 0:II], in_=wd_s[:, :]),
                    writes=[bG], kind="d", qtag=f"qs{i % 2}")

        def quant_chunk(i, jc, co, cw):
            qc = qcbuf[jc % 2][:, 0:cw]
            bqc = bqcb[jc % 2]
            add("vector", lambda e, i=i, co=co, cw=cw: e.tensor_scalar(
                EV[:, 0:cw], qsrc(i)[:, co:co + cw], s_tmp[:, 5:6], None,
                ALU.mult),
                reads=[bG, btmp], writes=[bEV])
            add("vector", lambda e, cw=cw: e.tensor_copy(
                out=W2i[:, 0:cw], in_=EV[:, 0:cw]), reads=[bEV], writes=[bW2])
            add("vector", lambda e, cw=cw, qc=qc: e.tensor_copy(
                out=qc, in_=W2i[:, 0:cw]), reads=[bW2], writes=[bqc])
            add("vector", lambda e, cw=cw, qc=qc: e.scalar_tensor_tensor(
                out=W2[:, 0:cw], in0=EV[:, 0:cw], scalar=0.5,
                in1=qc, op0=ALU.subtract, op1=ALU.is_gt),
                reads=[bEV, bqc], writes=[bW2])
            add("vector", lambda e, cw=cw, qc=qc: e.tensor_add(
                out=qc, in0=qc, in1=W2[:, 0:cw]),
                reads=[bW2, bqc], writes=[bqc])
            add("vector", lambda e, cw=cw, qc=qc: e.scalar_tensor_tensor(
                out=W2[:, 0:cw], in0=EV[:, 0:cw], scalar=0.5,
                in1=qc, op0=ALU.add, op1=ALU.is_lt),
                reads=[bEV, bqc], writes=[bW2])
            add("vector", lambda e, cw=cw, qc=qc: e.tensor_sub(
                out=qc, in0=qc, in1=W2[:, 0:cw]),
                reads=[bW2, bqc], writes=[bqc])
            add("vector", lambda e, cw=cw, qc=qc: e.tensor_scalar(
                qc, qc, -(QMAX + 1.0), QMAX, ALU.max, ALU.min),
                reads=[bqc], writes=[bqc])

        def tq_chunk(jc, cw, colbase):
            for kt in range(cw // 128):
                bnk = kt % 2
                add("tensor", lambda e, jc=jc, kt=kt, bnk=bnk: e.transpose(
                    ps[bnk][:, 0:128],
                    qcbuf[jc % 2][:, 128 * kt:128 * (kt + 1)], s_id[:]),
                    reads=[bqcb[jc % 2], bid], writes=[bps[bnk]])
                wtb = bf_(s_wt[kt % 4])
                add("scalar", lambda e, kt=kt, bnk=bnk: e.copy(
                    out=s_wt[kt % 4][:], in_=ps[bnk][:, 0:128]),
                    reads=[bps[bnk]], writes=[wtb])
                add("gpsimd", lambda e, kt=kt, colbase=colbase: e.dma_start(
                    out=cc_w2i[:, colbase + kt * 64:colbase + (kt + 1) * 64],
                    in_=s_wtf[kt % 4][:, :]),
                    reads=[wtb], writes=[bw2s[kt % 4]], kind="d",
                    qtag=f"w2{kt % 4}")

        load_qsrc(0)
        jc = 0
        for i in range(7):
            if i + 1 < 7:
                load_qsrc(i + 1)
            wdt = D if i < 6 else II
            add("vector", lambda e, i=i, wdt=wdt: e.reduce_max(
                out=s_tmp[:, 4:5], in_=qsrc(i)[:, 0:wdt], axis=AX.X,
                apply_absolute_value=True),
                reads=[bG], writes=[btmp])
            vspace()
            sc = s_mysc[:, i:i + 1]
            add("vector", lambda e, sc=sc: e.tensor_scalar(
                sc, s_tmp[:, 4:5], 1.0 / QMAX, None, ALU.mult),
                reads=[btmp], writes=[bmys])
            add("vector", lambda e: e.reciprocal(out=s_tmp[:, 5:6], in_=s_tmp[:, 4:5]),
                reads=[btmp], writes=[btmp])
            vspace()
            add("vector", lambda e: e.tensor_scalar(
                s_tmp[:, 5:6], s_tmp[:, 5:6], QMAX, None, ALU.mult),
                reads=[btmp], writes=[btmp])
            vspace()
            if i < 6:
                wi, il = TILES[i]
                quant_chunk(i, jc, 0, D)
                tq_chunk(jc, D, wi * 1536 + il * 512)
                jc += 1
            else:
                for half in range(2):
                    quant_chunk(i, jc, half * 1408, 1408)
                    tq_chunk(jc, 1408, 3072 + half * 11 * 64)
                    jc += 1
        add("gpsimd", lambda e: e.dma_start(
            out=cc_w2i[:, 4480:4487], in_=s_mysc[:, 0:7]),
            reads=[bmys], writes=[bw2s[4]], kind="d", qtag="w2s")
        add("gpsimd", lambda e: e.collective_compute(
            "AllGather", ALU.bypass, replica_groups=RG,
            ins=[cc_w2i[:, :]], outs=[cc_w2o[:, :, :]]),
            reads=bw2s, writes=[bw2o], kind="cc")

        def wfh(k):
            return Bbf[:, k * FR:(k + 1) * FR]

        def wfl(k):
            return Bbf[:, 10240 + k * FR:10240 + (k + 1) * FR]

        stg_f32 = [Bb[:, 11264:13312], Bb[:, 13312:15360]]
        bsp = [Buf(), Buf()]
        bwf = [Buf() for _ in range(ND)]

        for ti in range(3):
            for k in range(ND):
                add("sync", lambda e, ti=ti, k=k: e.dma_start(
                    out=Bb[:, k * 640:(k + 1) * 640],
                    in_=cc_w1oa[k, :, ti * 1280:ti * 1280 + 640]),
                    reads=[bw1oa], writes=[bwf[k]], kind="d", qtag=f"wf{k}")
                add("sync", lambda e, ti=ti, k=k: e.dma_start(
                    out=Bb[:, 5120 + k * 640:5120 + (k + 1) * 640],
                    in_=cc_w1oa[k, :, ti * 1280 + 640:ti * 1280 + 1280]),
                    reads=[bw1oa], writes=[bwf[k]], kind="d", qtag=f"wf{k}")
            for mt in range(2 * NFT):
                bset = (mt % 2) * 4
                for k in range(ND):
                    def mmx(e, mt=mt, bset=bset, k=k):
                        ins = None
                        lh = wfh(k)[:, mt * 128:(mt + 1) * 128]
                        ll = wfl(k)[:, mt * 128:(mt + 1) * 128]
                        for pp, (lw, rsel) in enumerate(((lh, xh), (lh, xl),
                                                         (ll, xh))):
                            for ch in range(NCH):
                                ins = e.matmul(
                                    ps[bset + ch][:, 0:TC], lw,
                                    rsel(k)[:, ch * TC:(ch + 1) * TC],
                                    start=(k == 0 and pp == 0),
                                    stop=(k == ND - 1 and pp == 2))
                        return ins
                    add("tensor", mmx, reads=[bA, bwf[k]],
                        writes=[bps[bset + ch] for ch in range(NCH)])
                stg = stg_f32[mt % 2]
                for ch in range(NCH):
                    add("scalar", lambda e, stg=stg, bset=bset, ch=ch: e.copy(
                        out=stg[:, ch * TC:(ch + 1) * TC], in_=ps[bset + ch][:, 0:TC]),
                        reads=[bps[bset + ch]], writes=[bB])
                if ti < 2:
                    comp = (0 if ti == 0 else 2) + (0 if mt < NFT else 1)
                    fmt = mt % NFT
                    add("sync", lambda e, comp=comp, fmt=fmt, stg=stg: e.dma_start(
                        out=d_fqk.rearrange("a b c2 -> a (b c2)")[
                            fmt * 128:(fmt + 1) * 128, comp * T:(comp + 1) * T],
                        in_=stg[:, 0:T]),
                        reads=[bB], writes=[bsp[mt % 2]], kind="d",
                        qtag=f"sp{mt % 2}")
                else:
                    add("sync", lambda e, mt=mt, stg=stg: e.dma_start(
                        out=d_fv[mt, :, :], in_=stg[:, 0:T]),
                        reads=[bB], writes=[bsp[mt % 2]], kind="d",
                        qtag=f"sp{mt % 2}")

        fv_re = [Bb[:, 0:T], W2[:, 0:T]]
        fv_im = [Bb[:, T:2 * T], EV[:, 0:T]]
        kv_re = [Bb[:, 2 * T:3 * T], Bb[:, 3 * T:4 * T], Bb[:, 4 * T:5 * T]]
        kv_im = [Bb[:, 5 * T:6 * T], Bb[:, 6 * T:7 * T], Bb[:, 7 * T:8 * T]]
        r_re = A[:, 0:T]
        r_im = A[:, T:2 * T]
        p1 = A[:, 2 * T:3 * T]
        p2 = A[:, 3 * T:4 * T]
        gk_re = [A[:, 4 * T:5 * T], A[:, 6 * T:7 * T], W0[:, 0:T]]
        gk_im = [A[:, 5 * T:6 * T], A[:, 7 * T:8 * T], W1[:, 0:T]]
        gq = [G[:, 0:2 * T], G[:, 2 * T:4 * T]]

        bfv_r = [[bB], [bW2]]
        bfv_i = [[bB], [bEV]]
        bkv = [bB, bB, bB]
        brr = Buf()
        bp12 = Buf()
        bg01 = [Buf(), Buf()]
        bgq = [Buf(), Buf()]
        bgk_r = [[bg01[0]], [bg01[1]], [bW0]]
        bgk_i = [[bg01[0]], [bg01[1]], [bW1]]
        bslot = [bf_(s_slot[p]) for p in range(2)]
        bstg = [bf_(s_stg[p]) for p in range(3)]
        boff = bf_(s_off)
        bmpr = bf_(s_mpr)
        bidx = [bf_(s_idx[p]) for p in range(3)]
        bcsi = [bf_(cc_si[p]) for p in range(3)]
        bcso = [bf_(cc_so[p]) for p in range(3)]
        bdr = bf_(d_r)

        def load_fv(ft):
            fp = ft % 2
            add("sync", lambda e, ft=ft, fp=fp: e.dma_start(
                out=fv_re[fp], in_=d_fv[ft, :, :]),
                reads=bsp, writes=bfv_r[fp] + (bwf if ft == 0 else []),
                kind="d", qtag=f"fv{fp}")
            add("sync", lambda e, ft=ft, fp=fp: e.dma_start(
                out=fv_im[fp], in_=d_fv[NFT + ft, :, :]),
                reads=bsp, writes=bfv_i[fp], kind="d", qtag=f"fv{fp}")

        def load_idx(u):
            ft, c = divmod(u, C)
            u3 = u % 3
            add("sync", lambda e, c=c, ft=ft, u3=u3: e.dma_start(
                out=s_idx[u3][:, :],
                in_=permspad[c:c + 1, ft * 128:(ft + 1) * 128].rearrange(
                    "one p -> p one")),
                writes=[bidx[u3]], kind="d", qtag=f"idx{u3}")

        def issue_k(u, first=False):
            u3 = u % 3
            wr = bgk_r[u3] + ([bA] if first else [])
            wi_ = bgk_i[u3] + ([bA] if first else [])
            add("gpsimd", lambda e, u3=u3: e.indirect_dma_start(
                out=gk_re[u3], out_offset=None,
                in_=d_fqk[0:128, :, :].rearrange("a b c2 -> a (b c2)"),
                in_offset=IndirectOffsetOnAxis(s_idx[u3][:, :], 0),
                element_offset=2 * T),
                reads=bsp + [bidx[u3]], writes=wr,
                kind="d", qtag=f"gk{u3}")
            add("gpsimd", lambda e, u3=u3: e.indirect_dma_start(
                out=gk_im[u3], out_offset=None,
                in_=d_fqk[0:128, :, :].rearrange("a b c2 -> a (b c2)"),
                in_offset=IndirectOffsetOnAxis(s_idx[u3][:, :], 0),
                element_offset=3 * T),
                reads=bsp + [bidx[u3]], writes=wi_,
                kind="d", qtag=f"gk{u3}")

        def issue_q(u, first=False):
            up = u % 2
            u3 = u % 3
            wq = [bgq[up]] + ([bG] if first else [])
            add("gpsimd", lambda e, up=up, u3=u3: e.indirect_dma_start(
                out=gq[up], out_offset=None,
                in_=d_fqk[0:128, :, :].rearrange("a b c2 -> a (b c2)"),
                in_offset=IndirectOffsetOnAxis(s_idx[u3][:, :], 0),
                element_offset=0),
                reads=bsp + [bidx[u3]], writes=wq, kind="d",
                qtag=f"gq{up}")

        def z_stage(u):
            ft, c = divmod(u, C)
            up = u % 2
            u3 = u % 3
            uk = u % 3
            add("sync", lambda e, up=up, u3=u3: e.dma_start(
                out=s_slot[up][:].rearrange("p (j c2) -> p j c2", j=4),
                in_=cc_so[u3][:, :, :].rearrange("j p c2 -> p j c2")),
                reads=[bcso[u3]], writes=[bslot[up]], kind="d", qtag=f"dl{up}")
            add("vector", lambda e, up=up: e.tensor_mul(
                out=s_mpr[:], in0=s_slot[up][:], in1=s_mask[:]),
                reads=[bslot[up], bf_(s_mask)], writes=[bmpr])
            vspace()
            add("vector", lambda e: e.tensor_reduce(
                out=s_off[:, 0:1], in_=s_mpr[:, 0:8:2], axis=AX.X, op=ALU.add),
                reads=[bmpr], writes=[boff])
            add("vector", lambda e: e.tensor_reduce(
                out=s_off[:, 1:2], in_=s_mpr[:, 1:8:2], axis=AX.X, op=ALU.add),
                reads=[bmpr], writes=[boff])
            vspace()
            vspace()
            if debug is True:
                add("sync", lambda e, u=u: e.dma_start(
                    out=d_off[u, :, :], in_=s_off[:]),
                    reads=[boff], writes=[bf_(d_off)], kind="d")
                add("sync", lambda e, u=u, up=up: e.dma_start(
                    out=d_slot[u, :, :], in_=s_slot[up][:]),
                    reads=[bslot[up]], writes=[bf_(d_slot)], kind="d")
                add("sync", lambda e, u=u: e.dma_start(
                    out=d_mpr[u, :, :], in_=s_mpr[:]),
                    reads=[bmpr], writes=[bf_(d_mpr)], kind="d")
            add("vector", lambda e, up=up: e.scalar_tensor_tensor(
                out=p1, in0=kv_re[uk], scalar=s_off[:, 0:1],
                in1=gq[up][:, 0:T], op0=ALU.add, op1=ALU.mult),
                reads=[bkv[uk], boff, bgq[up]], writes=[bp12])
            add("vector", lambda e, up=up: e.scalar_tensor_tensor(
                out=p2, in0=kv_im[uk], scalar=s_off[:, 1:2],
                in1=gq[up][:, T:2 * T], op0=ALU.add, op1=ALU.mult),
                reads=[bkv[uk], boff, bgq[up]], writes=[bp12])
            if c == 0:
                add("vector", lambda e: e.tensor_add(out=r_re, in0=p1, in1=p2),
                    reads=[bp12], writes=[brr])
            else:
                add("vector", lambda e: e.tensor_add(out=r_re, in0=r_re, in1=p1),
                    reads=[bp12, brr], writes=[brr])
                add("vector", lambda e: e.tensor_add(out=r_re, in0=r_re, in1=p2),
                    reads=[bp12, brr], writes=[brr])
            add("vector", lambda e, up=up: e.scalar_tensor_tensor(
                out=p1, in0=kv_im[uk], scalar=s_off[:, 1:2],
                in1=gq[up][:, 0:T], op0=ALU.add, op1=ALU.mult),
                reads=[bkv[uk], boff, bgq[up]], writes=[bp12])
            add("vector", lambda e, up=up: e.scalar_tensor_tensor(
                out=p2, in0=kv_re[uk], scalar=s_off[:, 0:1],
                in1=gq[up][:, T:2 * T], op0=ALU.add, op1=ALU.mult),
                reads=[bkv[uk], boff, bgq[up]], writes=[bp12])
            if c == 0:
                add("vector", lambda e: e.tensor_sub(out=r_im, in0=p1, in1=p2),
                    reads=[bp12], writes=[brr])
            else:
                add("vector", lambda e: e.tensor_add(out=r_im, in0=r_im, in1=p1),
                    reads=[bp12, brr], writes=[brr])
                add("vector", lambda e: e.tensor_sub(out=r_im, in0=r_im, in1=p2),
                    reads=[bp12, brr], writes=[brr])
            if c == C - 1:
                add("sync", lambda e, ft=ft: e.dma_start(
                    out=d_r[ft, :, :], in_=r_re), reads=[brr], writes=[bdr],
                    kind="d", qtag="rsp")
                add("sync", lambda e, ft=ft: e.dma_start(
                    out=d_r[NFT + ft, :, :], in_=r_im),
                    reads=[brr], writes=[bdr], kind="d", qtag="rsp")

        load_fv(0)
        load_idx(0)
        load_idx(1)
        issue_k(0, first=True)
        issue_k(1, first=True)
        issue_q(0, first=True)
        issue_q(1, first=True)
        for u in range(NU):
            ft, c = divmod(u, C)
            up = u % 3
            fp_ = 0
            if u + 2 < NU:
                load_idx(u + 2)
            if u + 2 < NU:
                issue_k(u + 2)
            if c == 5 and ft + 1 < NFT:
                load_fv(ft + 1)
            kp = u % 3
            fp = ft % 2
            add("vector", lambda e, kp=kp, fp=fp: e.tensor_mul(
                out=p1, in0=gk_re[kp], in1=fv_re[fp]),
                reads=bgk_r[kp] + bfv_r[fp],
                writes=[bp12] + ([bA] if u == 0 else []))
            add("vector", lambda e, kp=kp, fp=fp: e.tensor_mul(
                out=p2, in0=gk_im[kp], in1=fv_im[fp]),
                reads=bgk_i[kp] + bfv_i[fp], writes=[bp12])
            add("vector", lambda e, up=up: e.tensor_tensor_scan(
                out=kv_re[up], data0=p1, data1=p2, initial=0.0,
                op0=ALU.add, op1=ALU.subtract),
                reads=[bp12], writes=[bkv[up]] + (bwf if u == 0 else []))
            add("vector", lambda e, kp=kp, fp=fp: e.tensor_mul(
                out=p1, in0=gk_re[kp], in1=fv_im[fp]),
                reads=bgk_r[kp] + bfv_i[fp], writes=[bp12])
            add("vector", lambda e, kp=kp, fp=fp: e.tensor_mul(
                out=p2, in0=gk_im[kp], in1=fv_re[fp]),
                reads=bgk_i[kp] + bfv_r[fp], writes=[bp12])
            add("vector", lambda e, up=up: e.tensor_tensor_scan(
                out=kv_im[up], data0=p1, data1=p2, initial=0.0,
                op0=ALU.add, op1=ALU.add),
                reads=[bp12], writes=[bkv[up]])
            vspace()
            add("vector", lambda e, up=up: e.tensor_copy(
                out=s_stg[up][:, 0:1], in_=kv_re[up][:, T - 1:T]),
                reads=[bkv[up]], writes=[bstg[up]])
            add("vector", lambda e, up=up: e.tensor_copy(
                out=s_stg[up][:, 1:2], in_=kv_im[up][:, T - 1:T]),
                reads=[bkv[up]], writes=[bstg[up]])
            add("sync", lambda e, up=up: e.dma_start(
                out=cc_si[up][:, :], in_=s_stg[up][:]),
                reads=[bstg[up]], writes=[bcsi[up]], kind="d", qtag=f"up{up}")
            if u > 1:
                z_stage(u - 2)
            if u >= 2:
                issue_q(u)
            add("gpsimd", lambda e, up=up: e.collective_compute(
                "AllGather", ALU.bypass, replica_groups=RG4,
                ins=[cc_si[up][:, :]], outs=[cc_so[up][:, :, :]]),
                reads=[bcsi[up]], writes=[bcso[up]], kind="cc")
        z_stage(NU - 2)
        z_stage(NU - 1)

        def rsp_h(kt):
            return Abf[:, kt * T:(kt + 1) * T]

        def rsp_l(kt):
            if kt < 6:
                return Abf[:, 20480 + kt * T:20480 + (kt + 1) * T]
            return Gbf[:, (kt - 6) * T:(kt - 6 + 1) * T]

        batt = bB
        pc_regions = [bA, bG, brr, bp12, bg01[0], bg01[1], bgq[0], bgq[1]]
        for kt in range(2 * NFT):
            add("sync", lambda e, kt=kt: e.dma_start(
                out=W2[:, 0:T], in_=d_r[kt, :, :]),
                reads=[bdr], writes=[bW2], kind="d", qtag="rr")
            split_hilo(rsp_h(kt), rsp_l(kt), W2[:, 0:T], W1[:, 0:T],
                       [bW2], pc_regions, bW1)
        for mt in range(ND):
            bset = (mt % 2) * 4
            for kt in range(2 * NFT):
                for pp in range(2):
                    wti = (kt * 2 + pp) % 4
                    wtb = bf_(s_wt[wti])
                    base = 0 if pp == 0 else 640
                    add("sync", lambda e, base=base, kt=kt, mt=mt, wti=wti:
                        e.dma_start(
                        out=s_wtf[wti][:, :],
                        in_=cc_w1ob[mt, :, base + kt * 64:base + (kt + 1) * 64]),
                        reads=[bw1ob], writes=[wtb], kind="d", qtag=f"wt{wti}")

                    def mmi(e, kt=kt, pp=pp, bset=bset, wti=wti):
                        ins = None
                        rhs_list = ((rsp_h(kt), rsp_l(kt)) if pp == 0
                                    else (rsp_h(kt),))
                        for ri, rw in enumerate(rhs_list):
                            st = (kt == 0 and pp == 0 and ri == 0)
                            sp = (kt == 2 * NFT - 1 and pp == 1 and
                                  ri == len(rhs_list) - 1)
                            for ch in range(NCH):
                                ins = e.matmul(ps[bset + ch][:, 0:TC],
                                               s_wt[wti][:],
                                               rw[:, ch * TC:(ch + 1) * TC],
                                               start=st, stop=sp)
                        return ins
                    add("tensor", mmi, reads=[wtb, bA, bG],
                        writes=[bps[bset + ch] for ch in range(NCH)])
            for ch in range(NCH):
                add("scalar", lambda e, mt=mt, bset=bset, ch=ch: e.copy(
                    out=Bb[:, mt * T + ch * TC:mt * T + (ch + 1) * TC],
                    in_=ps[bset + ch][:, 0:TC]),
                    reads=[bps[bset + ch]], writes=[batt])

        def xq(k):
            return Abf[:, k * T:(k + 1) * T]

        bxq = bA
        add("sync", lambda e: e.dma_start(out=s_gmax[:], in_=cc_m1o[:, :]),
            reads=[bf_(cc_m1o)], writes=[bf_(s_gmax)], kind="d", qtag="gm")
        add("scalar", lambda e: e.activation(
            out=s_tmp[:, 6:7], in_=s_clip[:, 0:1], func=ACT.Sigmoid),
            reads=[bf_(s_clip)], writes=[btmp])
        add("vector", lambda e: e.tensor_mul(
            out=s_kx[:], in0=s_tmp[:, 6:7], in1=s_gmax[:]),
            reads=[btmp, bf_(s_gmax)], writes=[bf_(s_kx)])
        add("vector", lambda e: e.tensor_scalar(
            s_sxq[:], s_kx[:], 1.0 / QMAX, None, ALU.mult),
            reads=[bf_(s_kx)], writes=[bf_(s_sxq)])
        add("vector", lambda e: e.tensor_copy(out=s_tmp[:, 4:5], in_=s_kx[:]),
            reads=[bf_(s_kx)], writes=[btmp])
        add("vector", lambda e: e.reciprocal(out=s_kx[:], in_=s_tmp[:, 4:5]),
            reads=[btmp], writes=[bf_(s_kx)])
        vspace()
        add("vector", lambda e: e.tensor_scalar(
            s_kx[:], s_kx[:], QMAX, None, ALU.mult),
            reads=[bf_(s_kx)], writes=[bf_(s_kx)])
        vspace()

        def quant_act(src_ap, out_ap, recip_b, src_b, out_b, width, krecip):
            add("vector", lambda e, w=width: e.tensor_scalar(
                W1[:, 0:w], src_ap, krecip, None, ALU.mult),
                reads=[src_b, recip_b], writes=[bW1])
            add("vector", lambda e, w=width: e.tensor_copy(
                out=W0i[:, 0:w], in_=W1[:, 0:w]), reads=[bW1], writes=[bW0])
            add("vector", lambda e, w=width: e.tensor_copy(
                out=EV[:, 0:w], in_=W0i[:, 0:w]), reads=[bW0], writes=[bEV])
            add("vector", lambda e, w=width: e.scalar_tensor_tensor(
                out=W0[:, 0:w], in0=W1[:, 0:w], scalar=0.5,
                in1=EV[:, 0:w], op0=ALU.subtract, op1=ALU.is_gt),
                reads=[bW1, bEV], writes=[bW0])
            add("vector", lambda e, w=width: e.tensor_add(
                out=EV[:, 0:w], in0=EV[:, 0:w], in1=W0[:, 0:w]),
                reads=[bW0, bEV], writes=[bEV])
            add("vector", lambda e, w=width: e.scalar_tensor_tensor(
                out=W0[:, 0:w], in0=W1[:, 0:w], scalar=0.5,
                in1=EV[:, 0:w], op0=ALU.add, op1=ALU.is_lt),
                reads=[bW1, bEV], writes=[bW0])
            add("vector", lambda e, w=width: e.tensor_sub(
                out=EV[:, 0:w], in0=EV[:, 0:w], in1=W0[:, 0:w]),
                reads=[bW0, bEV], writes=[bEV])
            add("vector", lambda e, w=width, o=out_ap: e.tensor_scalar(
                o, EV[:, 0:w], -(QMAX + 1.0), QMAX, ALU.max, ALU.min),
                reads=[bEV], writes=[out_b])

        for k in range(ND):
            add("sync", lambda e, k=k: e.dma_start(
                out=W2[:, 0:T], in_=d_xnt[k, :, :]),
                reads=[bdxnt], writes=[bW2], kind="d", qtag="xn")
            quant_act(W2[:, 0:T], xq(k), bf_(s_kx), bW2, bxq, T, s_kx[:, 0:1])

        for j in range(8):
            add("sync", lambda e, j=j: e.dma_start(
                out=s_scf[:, j * 7:(j + 1) * 7],
                in_=cc_w2o[j, :, 4480:4487]),
                reads=[bw2o], writes=[bf_(s_scf)], kind="d", qtag="sc")
        add("vector", lambda e: e.tensor_scalar(
            s_sxgu[:, :], s_scf[:, :], s_sxq[:, 0:1], None, ALU.mult),
            reads=[bf_(s_sxq), bf_(s_scf)], writes=[bf_(s_sxgu)])

        def gcol(it):
            return (it // 3) * 7 + (it % 3)

        def ucol(it):
            return (it // 3) * 7 + 3 + (it % 3)

        bhm = bf_(s_hmax)
        bdh0 = Buf()
        bdh1 = Buf()
        first_h = [True]
        wgu_t = [s_qb[0], s_qb[1], s_wgu[0], s_wgu[1]]
        wgu_f = [t.bitcast(F32) for t in wgu_t]
        bwgu = [bf_(t) for t in wgu_t]
        for it in range(NI):
            for wi in range(2):
                sl = (it % 2) * 2 + wi
                add("sync", lambda e, it=it, wi=wi, sl=sl: e.dma_start(
                    out=wgu_f[sl][:, :],
                    in_=cc_w2o[it // 3, :, wi * 1536 + (it % 3) * 512:
                               wi * 1536 + (it % 3) * 512 + 512]),
                    reads=[bw2o], writes=[bwgu[sl]], kind="d", qtag=f"wg{sl}")
            for wi in range(2):
                sl = (it % 2) * 2 + wi

                def mmg(e, wi=wi, sl=sl):
                    ins = None
                    for kt in range(ND):
                        for ch in range(NCH):
                            ins = e.matmul(ps[wi * 4 + ch][:, 0:TC],
                                           wgu_t[sl][:, kt * 128:(kt + 1) * 128],
                                           xq(kt)[:, ch * TC:(ch + 1) * TC],
                                           start=(kt == 0), stop=(kt == ND - 1))
                    return ins
                add("tensor", mmg, reads=[bwgu[sl], bxq],
                    writes=[bps[wi * 4 + ch] for ch in range(NCH)])
            hb = W0 if it % 2 == 0 else EV
            bhb = bW0 if it % 2 == 0 else bEV
            for ch in range(NCH):
                sg = s_sgs[:, (ch % 2) * TC:(ch % 2) * TC + TC]
                add("scalar", lambda e, it=it, ch=ch, sg=sg: e.activation(
                    out=sg, in_=ps[ch][:, 0:TC], func=ACT.Sigmoid,
                    scale=s_sxgu[:, gcol(it):gcol(it) + 1]),
                    reads=[bps[ch], bf_(s_sxgu)], writes=[bf_(s_sgs)])
                add("vector", lambda e, it=it, ch=ch, sg=sg: e.scalar_tensor_tensor(
                    out=W1[:, ch * TC:(ch + 1) * TC], in0=ps[ch][:, 0:TC],
                    scalar=s_sxgu[:, gcol(it):gcol(it) + 1],
                    in1=sg, op0=ALU.mult, op1=ALU.mult),
                    reads=[bps[ch], bf_(s_sgs), bf_(s_sxgu)], writes=[bW1])
            for ch in range(NCH):
                add("vector", lambda e, it=it, ch=ch, hb=hb: e.scalar_tensor_tensor(
                    out=hb[:, ch * TC:(ch + 1) * TC], in0=ps[4 + ch][:, 0:TC],
                    scalar=s_sxgu[:, ucol(it):ucol(it) + 1],
                    in1=W1[:, ch * TC:(ch + 1) * TC],
                    op0=ALU.mult, op1=ALU.mult),
                    reads=[bps[4 + ch], bW1, bf_(s_sxgu)], writes=[bhb])
            add("vector", lambda e, hb=hb: e.reduce_max(
                out=s_tmp[:, 7:8], in_=hb[:, 0:T], axis=AX.X,
                apply_absolute_value=True),
                reads=[bhb], writes=[btmp])
            vspace()
            if first_h[0]:
                add("vector", lambda e: e.tensor_copy(out=s_hmax[:], in_=s_tmp[:, 7:8]),
                    reads=[btmp], writes=[bhm])
                first_h[0] = False
            else:
                add("vector", lambda e: e.tensor_max(
                    out=s_hmax[:], in0=s_hmax[:], in1=s_tmp[:, 7:8]),
                    reads=[btmp, bhm], writes=[bhm])
            add("sync", lambda e, it=it, hb=hb: e.dma_start(
                out=d_h[it, :, :], in_=hb[:, 0:T]),
                reads=[bhb], writes=[bdh0 if it % 2 == 0 else bdh1],
                kind="d", qtag=f"h{it % 2}")

        add("vector", lambda e: e.tensor_scalar(
            s_bc[:], s_id[:], 0.0, s_hmax[:, 0:1], ALU.mult, ALU.add),
            reads=[bid, bhm], writes=[bf_(s_bc)])
        add("tensor", lambda e: e.transpose(ps[0][:, 0:128], s_bc[:], s_id[:]),
            reads=[bf_(s_bc), bid], writes=[bps[0]])
        add("scalar", lambda e: e.copy(out=s_bc[:], in_=ps[0][:, 0:128]),
            reads=[bps[0]], writes=[bf_(s_bc)])
        add("vector", lambda e: e.reduce_max(out=s_hmax[:], in_=s_bc[:], axis=AX.X),
            reads=[bf_(s_bc)], writes=[bhm])
        add("sync", lambda e: e.dma_start(out=cc_m2i[:, :], in_=s_hmax[:]),
            reads=[bhm], writes=[bf_(cc_m2i)], kind="d", qtag="upm")
        add("gpsimd", lambda e: e.collective_compute(
            "AllReduce", ALU.max, replica_groups=RG,
            ins=[cc_m2i[:, :]], outs=[cc_m2o[:, :]]),
            reads=[bf_(cc_m2i)], writes=[bf_(cc_m2o)], kind="cc")
        add("sync", lambda e: e.dma_start(out=s_hmax[:], in_=cc_m2o[:, :]),
            reads=[bf_(cc_m2o)], writes=[bhm], kind="d", qtag="gm")
        add("scalar", lambda e: e.activation(
            out=s_tmp[:, 6:7], in_=s_clip[:, 1:2], func=ACT.Sigmoid),
            reads=[bf_(s_clip)], writes=[btmp])
        add("vector", lambda e: e.tensor_mul(
            out=s_kh[:], in0=s_tmp[:, 6:7], in1=s_hmax[:]),
            reads=[btmp, bhm], writes=[bf_(s_kh)])
        add("vector", lambda e: e.tensor_scalar(
            s_shq[:], s_kh[:], 1.0 / QMAX, None, ALU.mult),
            reads=[bf_(s_kh)], writes=[bf_(s_shq)])
        add("vector", lambda e: e.tensor_copy(out=s_tmp[:, 4:5], in_=s_kh[:]),
            reads=[bf_(s_kh)], writes=[btmp])
        add("vector", lambda e: e.reciprocal(out=s_kh[:], in_=s_tmp[:, 4:5]),
            reads=[btmp], writes=[bf_(s_kh)])
        vspace()
        add("vector", lambda e: e.tensor_scalar(
            s_kh[:], s_kh[:], QMAX, None, ALU.mult),
            reads=[bf_(s_kh)], writes=[bf_(s_kh)])
        vspace()
        add("vector", lambda e: e.tensor_scalar(
            s_shdall[:, :], s_scf[:, :], s_shq[:, 0:1], None, ALU.mult),
            reads=[bf_(s_shq), bf_(s_scf)], writes=[bf_(s_shdall)])

        def hq(kt):
            return Gbf[:, kt * TC:(kt + 1) * TC]

        bhq = bG
        boutd = bA
        brl2 = [Buf(), Buf()]
        first_rl = [True]
        for ch in range(NCH):
            for kt in range(NI):
                p = kt % 2
                rl = W2[:, p * TC:(p + 1) * TC]
                w2h = [bW2] if first_rl[0] else []
                first_rl[0] = False
                add("sync", lambda e, kt=kt, ch=ch, rl=rl: e.dma_start(
                    out=rl, in_=d_h[kt, :, ch * TC:(ch + 1) * TC]),
                    reads=[bdh0, bdh1], writes=[brl2[p]] + w2h, kind="d",
                    qtag=f"hr{p}")
                quant_act(rl, hq(kt), bf_(s_kh), brl2[p], bhq, TC, s_kh[:, 0:1])
            for mt in range(ND):
                bnk = mt % 8
                wdst = W0 if mt % 2 == 0 else W1
                wdstb = W0bf if mt % 2 == 0 else W1bf
                bwd = bW0 if mt % 2 == 0 else bW1
                add("sync", lambda e, mt=mt, wdst=wdst: e.dma_start(
                    out=wdst[:, 0:1408],
                    in_=cc_w2o[mt, :, 3072:4480]),
                    reads=[bw2o], writes=[bwd], kind="d", qtag=f"wd{mt % 2}")

                def mmd(e, bnk=bnk, wdstb=wdstb):
                    ins = None
                    for kt in range(NI):
                        ins = e.matmul(ps[bnk][:, 0:TC],
                                       wdstb[:, kt * 128:(kt + 1) * 128],
                                       hq(kt), start=(kt == 0), stop=(kt == NI - 1))
                    return ins
                add("tensor", mmd, reads=[bwd, bhq], writes=[bps[bnk]])
                add("vector", lambda e, mt=mt, bnk=bnk, ch=ch: e.scalar_tensor_tensor(
                    out=A[:, mt * T + ch * TC:mt * T + (ch + 1) * TC],
                    in0=ps[bnk][:, 0:TC], scalar=s_shdall[:, mt * 7 + 6:mt * 7 + 7],
                    in1=Bb[:, mt * T + ch * TC:mt * T + (ch + 1) * TC],
                    op0=ALU.mult, op1=ALU.add),
                    reads=[bps[bnk], bf_(s_shdall), batt], writes=[boutd])

        bout = bf_(out_t)
        for tt in range(NT):
            buf = W0 if tt % 2 == 0 else W1
            bb = bW0 if tt % 2 == 0 else bW1
            add("sync", lambda e, tt=tt, buf=buf: e.dma_start(
                out=buf[:, 0:D], in_=x_in[tt * 128:(tt + 1) * 128, :]),
                writes=[bb], kind="d", qtag=f"x{tt % 2}")
            for k in range(ND):
                bnk = (tt * ND + k) % 2
                add("tensor", lambda e, tt=tt, k=k, bnk=bnk: e.transpose(
                    ps[bnk][:, 0:128], A[:, k * T + 128 * tt:k * T + 128 * (tt + 1)],
                    s_id[:]),
                    reads=[boutd, bid], writes=[bps[bnk]])
                add("vector", lambda e, k=k, bnk=bnk, buf=buf: e.tensor_add(
                    out=W2[:, 128 * k:128 * (k + 1)], in0=ps[bnk][:, 0:128],
                    in1=buf[:, 128 * k:128 * (k + 1)]),
                    reads=[bps[bnk], bb],
                    writes=[bW2] + (brl2 if tt == 0 and k == 0 else []))
            add("sync", lambda e, tt=tt: e.dma_start(
                out=out_t[tt * 128:(tt + 1) * 128, :], in_=W2[:, 0:D]),
                reads=[bW2], writes=[bout], kind="d", qtag="out")
        if debug:
            add("sync", lambda e: e.dma_start(
                out=d_mxdbg[0, :, :], in_=cc_m1o[:, :]),
                reads=[bf_(cc_m1o)], writes=[bf_(d_mxdbg)], kind="d")
            add("sync", lambda e: e.dma_start(
                out=d_mxdbg[1, :, :], in_=cc_m2o[:, :]),
                reads=[bf_(cc_m2o)], writes=[bf_(d_mxdbg)], kind="d")
            add("sync", lambda e: e.dma_start(out=d_w2dbg[:, :], in_=cc_w2i[:, :]),
                reads=[bw2i] + bw2s, writes=[bf_(d_w2dbg)], kind="d")
            for j in range(8):
                add("sync", lambda e, j=j: e.dma_start(
                    out=d_w2odbg[j, :, :], in_=cc_w2o[j, :, 0:561]),
                    reads=[bw2o], writes=[bf_(d_w2odbg)], kind="d")
        add("sync", lambda e: e.nop(), reads=[bout])

        P.emit(nc, es)
    return nc


def kernel(**inputs):
    x = np.ascontiguousarray(
        np.asarray(inputs["L0_attn_x"], dtype=np.float32).reshape(8 * T, D))
    dft, ctm, idn = _host_consts()
    perms = np.asarray(inputs["L0_attn_perms"], dtype=np.int32)
    permspad = np.concatenate(
        [perms, np.full((C, FP - perms.shape[1]), 513, dtype=np.int32)], axis=1)
    clip_x = float(np.asarray(inputs["L0_ffn_clip_x"]).ravel()[0])
    clip_d = float(np.asarray(inputs["L0_ffn_clip_down"]).ravel()[0])
    clip2 = np.tile(np.array([[clip_x, clip_d]], dtype=np.float32), (128, 1))

    if "nc" not in _CACHE:
        _CACHE["nc"] = build_program()
    nc = _CACHE["nc"]

    wq = np.asarray(inputs["L0_attn_Wq"], dtype=np.float32)
    wk = np.asarray(inputs["L0_attn_Wk"], dtype=np.float32)
    wv = np.asarray(inputs["L0_attn_Wv"], dtype=np.float32)
    wo = np.asarray(inputs["L0_attn_Wo"], dtype=np.float32)
    wg = np.asarray(inputs["L0_ffn_w_gate"], dtype=np.float32)
    wu = np.asarray(inputs["L0_ffn_w_up"], dtype=np.float32)
    wd = np.asarray(inputs["L0_ffn_w_down"], dtype=np.float32)
    pad = np.ones((3 * 8 * 128 - wg.shape[0], D), dtype=np.float32)
    wgp = np.concatenate([wg, pad], axis=0)
    wup = np.concatenate([wu, pad], axis=0)

    common = dict(clip2=clip2, dft=dft, ct=ctm, idn=idn, permspad=permspad)
    in_maps = []
    for r in range(8):
        mask = np.zeros((128, 8), dtype=np.float32)
        for j in range(4):
            if j < r % 4:
                mask[:, 2 * j] = 1.0
                mask[:, 2 * j + 1] = 1.0
        m = dict(common)
        m["x_in"] = np.ascontiguousarray(x[r * T:(r + 1) * T])
        m["mask16"] = mask
        m["wq_s"] = np.ascontiguousarray(wq[:, r * 128:(r + 1) * 128])
        m["wk_s"] = np.ascontiguousarray(wk[:, r * 128:(r + 1) * 128])
        m["wv_s"] = np.ascontiguousarray(wv[:, r * 128:(r + 1) * 128])
        m["wo_s"] = np.ascontiguousarray(wo[r * 128:(r + 1) * 128, :])
        m["wg_s"] = np.ascontiguousarray(wgp[r * 384:(r + 1) * 384])
        m["wu_s"] = np.ascontiguousarray(wup[r * 384:(r + 1) * 384])
        m["wd_s"] = np.ascontiguousarray(wd[r * 128:(r + 1) * 128, :])
        in_maps.append(m)

    res = run_bass_kernel_spmd(nc, in_maps, core_ids=list(range(8)))
    _CACHE["last_results"] = res.results
    out = np.concatenate([res.results[r]["out_t"] for r in range(8)], axis=0)
    return out.reshape(2, 8192, D).astype(np.float32)
